# version 18
# baseline (speedup 1.0000x reference)
import math
import numpy as np
import ml_dtypes
import concourse.bass as bass
import concourse.mybir as mybir
from concourse.bass_utils import run_bass_kernel_spmd
from contextlib import ExitStack

F32 = mybir.dt.float32
BF16 = mybir.dt.bfloat16
AX = mybir.AxisListType.X
OP = mybir.AluOpType
AF = mybir.ActivationFunctionType

B, S, D, H, HD, E, II = 8, 2048, 1024, 16, 64, 16, 512
DC, IC = D // 128, II // 128
TT = 256
GT = 512
NT = S // GT
GTO = [1, 2, 3, 0]
LNT = 512
NLT = S // LNT
EPS = 1e-5
LN8 = math.log(0.125)
TH_C = 16777216.0 * 0.01 / 0.9
NEG_BIG = -1.0e30
TILE_ORDER = [2, 3, 4, 5, 6, 7, 0, 1]
LN_EARLY = [1, 2, 3]
NEXP = 17
DEBUG = False


def build_nc():
    nc = bass.Bass("TRN2", target_bir_lowering=False, detect_race_conditions=False)

    def hin(name, shape):
        return nc.dram_tensor(name, list(shape), F32, kind="ExternalInput").ap()

    xT_h = hin("xT", (D, S))
    selT_h = hin("selT", (128, 16))
    x16T_h = hin("x16T", (D, 128))
    qwT_h = hin("qwT", (D, D)); kwT_h = hin("kwT", (D, D))
    vwT_h = hin("vwT", (D, D)); owT_h = hin("owT", (D, D))
    lqbd_h = hin("lqbd", (128, 128)); lkbd_h = hin("lkbd", (128, 128)); lvbd_h = hin("lvbd", (128, 128))
    qb8_h = hin("qb8", (128, DC)); vb8_h = hin("vb8", (128, DC)); ob8_h = hin("ob8", (128, DC))
    lna8_h = hin("lna8", (128, DC)); lnb8_h = hin("lnb8", (128, DC))
    lqb1_h = hin("lqb1", (128, 1)); lkb1_h = hin("lkb1", (128, 1)); lvb1_h = hin("lvb1", (128, 1))
    lnc1_h = hin("lnc1", (128, 1)); lnd1_h = hin("lnd1", (128, 1))
    gw128_h = hin("gw128", (128, 128))
    gb16_h = hin("gb16", (16, 1))
    idf_h = hin("idf", (128, 128))
    mask_h = hin("mask", (128, 128))
    oh16_h = hin("oh16", (16, 16 * 128))
    inv1024r_h = hin("inv1024r", (1, 128))
    ones1r_h = hin("ones1r", (1, 128))
    onescol_h = hin("onescol", (128, 1))
    hm2s_h = hin("hm2s", (128, 2))
    hm2T_h = hin("hm2T", (2, 128))
    swg_h = hin("swg", (D, II)); swu_h = hin("swu", (D, II)); swd_h = hin("swd", (II, D))
    ewg_h = hin("ewg", (E, D, II)); ewu_h = hin("ewu", (E, D, II)); ewd_h = hin("ewd", (E, II, D))
    outT_h = nc.dram_tensor("outT", [D, S], F32, kind="ExternalOutput").ap()
    outT_v = outT_h.rearrange("(c p) t -> p c t", p=128)

    P = {"t": [], "v": [], "s": [], "y": [], "g": []}

    class Sem:
        def __init__(self, h, step):
            self.h, self.step, self.n = h, step, 0

    es = ExitStack()
    S_T = Sem(es.enter_context(nc.semaphore("s_t")), 1)
    S_V = Sem(es.enter_context(nc.semaphore("s_v")), 1)
    S_S = Sem(es.enter_context(nc.semaphore("s_s")), 1)
    S_Y = Sem(es.enter_context(nc.semaphore("s_y")), 16)
    S_G = Sem(es.enter_context(nc.semaphore("s_g")), 16)
    S_C = Sem(es.enter_context(nc.semaphore("s_c")), 16)
    QSEM = {"t": S_T, "v": S_V, "s": S_S, "y": S_Y, "g": S_G}

    def emit(q, fn, wait=(), sem="auto"):
        s = QSEM[q] if sem == "auto" else sem
        w = tuple((sm.h, v) for (sm, v) in wait)
        def run(eng, fn=fn, w=w, sh=None if s is None else s.h,
                amt=0 if s is None else s.step):
            for h, v in w:
                eng.wait_ge(h, v)
            r = fn(eng)
            if sh is not None:
                r.then_inc(sh, amt)
        P[q].append(run)
        if s is not None:
            s.n += s.step

    MK = {}

    def vfill():
        emit("v", lambda v: v.tensor_copy(fillb[:, 1:2], fillb[:, 0:1]))

    sb = lambda name, shape, dt=F32: es.enter_context(nc.sbuf_tensor(name, list(shape), dt))
    psb = lambda name: es.enter_context(nc.psum_tensor(name, [128, 512], F32))

    strm = sb("strm", (128, DC, S))
    wg_sb = [sb(f"wgsb{i}", (128, DC, II), BF16) for i in range(2)]
    wu_sb = [sb(f"wusb{i}", (128, DC, II), BF16) for i in range(2)]
    wd_sb = [sb(f"wdsb{i}", (128, IC, D), BF16) for i in range(2)]
    h_sb = [sb(f"hsb{i}", (128, IC, GT), BF16) for i in range(2)]
    sx_sb = sb("sxsb", (128, IC, GT))
    wbc = sb("wbc", (128, S), BF16)
    w_hi = sb("w_hi", (16, S), BF16)
    w_lo = sb("w_lo", (16, S), BF16)
    o16 = sb("o16", (128, DC, 16))
    idf_s = sb("idf_s", (128, 128))
    idb_s = sb("idb_s", (128, 128), BF16)
    mask_s = sb("mask_s", (128, 128))
    oh16_b = sb("oh16_b", (16, 16 * 128), BF16)
    inv1024_b = sb("inv1024_b", (1, 128), BF16)
    ones1_b = sb("ones1_b", (1, 128), BF16)
    onescol_b = sb("onescol_b", (128, 1), BF16)
    hm2s_b = sb("hm2s_b", (128, 2), BF16)
    hm2T_b = sb("hm2T_b", (2, 128), BF16)
    selT_b = sb("selT_b", (128, 16), BF16)
    qb8_s = sb("qb8_s", (128, DC)); vb8_s = sb("vb8_s", (128, DC)); ob8_s = sb("ob8_s", (128, DC))
    lna8_s = sb("lna8_s", (128, DC)); lnb8_s = sb("lnb8_s", (128, DC))
    lqb1_s = sb("lqb1_s", (128, 1)); lkb1_s = sb("lkb1_s", (128, 1)); lvb1_s = sb("lvb1_s", (128, 1))
    lnc1_s = sb("lnc1_s", (128, 1)); lnd1_s = sb("lnd1_s", (128, 1))
    gwhi_s = sb("gwhi_s", (128, 128), BF16)
    gwlo_s = sb("gwlo_s", (128, 128), BF16)
    gb_s = sb("gb_s", (16, 1))
    cpool = sb("cpool", (128, 4))
    fillb = sb("fillb", (128, 2))
    nc.const_aps.register(F32, 0.0, cpool[:, 0:1])
    nc.const_aps.register(F32, EPS, cpool[:, 1:2])
    nc.const_aps.register(F32, LN8, cpool[:, 2:3])

    pg0, pg1, pu0, pu1 = psb("pg0"), psb("pg1"), psb("pu0"), psb("pu1")
    pd0, pd1, pd2, pd3 = psb("pd0"), psb("pd1"), psb("pd2"), psb("pd3")
    pg = [pg0, pg1]; pu = [pu0, pu1]; pd = [pd0, pd1, pd2, pd3]

    SA = ExitStack()
    sa = lambda name, shape, dt=F32: SA.enter_context(nc.sbuf_tensor(name, list(shape), dt))
    awT = sa("awT", (128, DC, D), BF16)
    x16b = sa("x16b", (128, DC, 128), BF16)
    sq16 = sa("sq16", (128, DC, 128), BF16)
    xab = sa("xab", (128, DC, 128), BF16)
    xbb = sa("xbb", (128, DC, 128), BF16)
    qf = sa("qf", (128, DC, 128))
    qcb = sa("qcb", (128, DC, 128), BF16)
    kvb = sa("kvb", (128, DC, 128), BF16)
    ascr = sa("ascr", (128, DC, 128))
    asb = sa("asb", (128, DC, 128), BF16)
    knb = sa("knb", (128, DC, 128), BF16)
    vit = sa("vit", (128, DC, 128), BF16)
    it0 = sa("it0", (128, DC, 128), BF16)
    it1 = sa("it1", (128, DC, 128), BF16)
    lbd_b = sa("lbd_b", (128, 3, 128), BF16)
    gwf_s = sa("gwf_s", (128, 128))
    pf = [sa(f"pf{i}", (128, 128)) for i in range(4)]
    pf2 = [sa(f"pf2{i}", (128, 128)) for i in range(4)]
    ptb = [sa(f"ptb{i}", (128, 128), BF16) for i in range(4)]
    ptb2 = [sa(f"ptb2{i}", (128, 128), BF16) for i in range(4)]
    lmaxn = [sa(f"lmaxn{i}", (128, 1)) for i in range(4)]
    rsum = [sa(f"rsum{i}", (128, 1)) for i in range(4)]
    rrec = [sa(f"rrec{i}", (128, 1)) for i in range(4)]
    ptob = [sa(f"ptob{i}", (128, 16), BF16) for i in range(2)]
    aob = sa("aob", (128, DC, 16), BF16)
    rowsA = sa("rowsA", (2, 1024))
    rowsB = sa("rowsB", (2, 1024), BF16)
    colscr = sa("colscr", (128, DC))
    dcol = sa("dcol", (128, 1))
    dcolb = sa("dcolb", (128, 1), BF16)
    m01s = sa("m01s", (1, 4))
    m01b = sa("m01b", (1, 1), BF16)
    m01c = sa("m01c", (128, 1))

    for dst, src in [(idb_s[:, :], idf_h[:, :]), (oh16_b[:, :], oh16_h[:, :]),
                     (inv1024_b[:, :], inv1024r_h[:, :]), (ones1_b[:, :], ones1r_h[:, :]),
                     (onescol_b[:, :], onescol_h[:, :]), (hm2s_b[:, :], hm2s_h[:, :]),
                     (hm2T_b[:, :], hm2T_h[:, :]), (selT_b[:, :], selT_h[:, :]),
                     (gwhi_s[:, :], gw128_h[:, :]),
                     (lbd_b[:, 0, :], lqbd_h[:, :]), (lbd_b[:, 1, :], lkbd_h[:, :]),
                     (lbd_b[:, 2, :], lvbd_h[:, :]),
                     (x16b[:, :, :], x16T_h.rearrange("(c p) t -> p c t", p=128))]:
        emit("g", lambda g, d=dst, s=src: g.dma_start(d, s))
    MK["g_consts"] = S_G.n
    emit("g", lambda g: g.dma_start(awT[:, :, 0:512], qwT_h[:, 0:512].rearrange("(c p) i -> p c i", p=128)))
    MK["g_awqh"] = S_G.n
    emit("g", lambda g: g.dma_start(awT[:, :, 512:1024], qwT_h[:, 512:1024].rearrange("(c p) i -> p c i", p=128)))
    MK["g_awq"] = S_G.n

    def emit_wload(run, e):
        par = run % 2
        if e == 0:
            srcs = [(wg_sb[par], swg_h), (wu_sb[par], swu_h), (wd_sb[par], swd_h)]
        else:
            srcs = [(wg_sb[par], ewg_h[e - 1]), (wu_sb[par], ewu_h[e - 1]),
                    (wd_sb[par], ewd_h[e - 1])]
        wt = [(S_T, MK[("emm", run - 2)])] if run >= 2 else []
        for k, (dst, src) in enumerate(srcs):
            emit("g", lambda g, d=dst, s=src: g.dma_start(
                d[:, :, :], s.rearrange("(c p) i -> p c i", p=128)),
                 wait=wt if k == 0 else ())
        MK[("wload", run)] = S_G.n

    for dst, src in [(idf_s[:, :], idf_h[:, :]), (mask_s[:, :], mask_h[:, :]),
                     (qb8_s[:, :], qb8_h[:, :]), (vb8_s[:, :], vb8_h[:, :]),
                     (ob8_s[:, :], ob8_h[:, :]), (lna8_s[:, :], lna8_h[:, :]),
                     (lnb8_s[:, :], lnb8_h[:, :]), (lqb1_s[:, :], lqb1_h[:, :]),
                     (lkb1_s[:, :], lkb1_h[:, :]), (lvb1_s[:, :], lvb1_h[:, :]),
                     (lnc1_s[:, :], lnc1_h[:, :]), (lnd1_s[:, :], lnd1_h[:, :]),
                     (gwf_s[:, :], gw128_h[:, :]), (gb_s[:, :], gb16_h[:, :])]:
        emit("y", lambda y, d=dst, s=src: y.dma_start(d, s))
    MK["y_consts"] = S_Y.n
    emit("y", lambda y: y.dma_start(strm[:, :, :], xT_h.rearrange("(c p) t -> p c t", p=128)))
    MK["y_xT"] = S_Y.n

    for j, val in [(0, 0.0), (1, EPS), (2, LN8), (3, 0.0)]:
        emit("v", lambda v, j=j, val=val: v.memset(cpool[:, j:j + 1], val))
    emit("v", lambda v: v.memset(fillb[:, 0:1], 0.0))
    emit("v", lambda v: v.tensor_tensor(gwlo_s[:, :], gwf_s[:, :], gwhi_s[:, :], OP.subtract),
         wait=[(S_Y, MK["y_consts"]), (S_G, MK["g_consts"])])
    MK["v_init"] = S_V.n

    def head_ln(src_f, dst_bf, wcol, use_ln8, tag):
        scap = [(S_S, S_S.n)]
        for c in range(DC):
            emit("v", lambda v, c=c: v.tensor_copy(asb[:, c, :], src_f[:, c, :]),
                 wait=scap if c == 0 else ())
        MK[(tag, "bf")] = S_V.n
        for c in range(DC):
            emit("s", lambda s, c=c: s.activation(sq16[:, c, :], src_f[:, c, :], AF.Square),
                 wait=[(S_V, MK[(tag, "bf")])] if c == 0 else ())
        MK[(tag, "sq")] = S_S.n
        for c in range(DC):
            dstp = pg[c // 4]
            emit("t", lambda t, c=c, dstp=dstp: t.matmul(
                dstp[0:2, (c % 4) * 128:(c % 4) * 128 + 128], hm2s_b[:, :], asb[:, c, :],
                start=True, stop=True),
                 wait=[(S_V, MK[(tag, "bf")])] if c == 0 else ())
        for c in range(DC):
            dstp = pd[2 + c // 4]
            emit("t", lambda t, c=c, dstp=dstp: t.matmul(
                dstp[0:2, (c % 4) * 128:(c % 4) * 128 + 128], hm2s_b[:, :], sq16[:, c, :],
                start=True, stop=True),
                 wait=[(S_S, MK[(tag, "sq")])] if c == 0 else ())
        MK[(tag, "stats")] = S_T.n
        for half in range(2):
            pm, pq = pg[half], pd[2 + half]
            wv = [(S_T, MK[(tag, "stats")])] if half == 0 else [(S_S, MK[(tag, "rstd", 0)])]
            emit("v", lambda v, pm=pm: v.tensor_copy(rowsA[0:2, 0:512], pm[0:2, 0:512]), wait=wv)
            emit("v", lambda v: v.tensor_tensor(
                rowsA[0:2, 512:1024], rowsA[0:2, 0:512], rowsA[0:2, 0:512], OP.mult))
            emit("v", lambda v, pq=pq: v.tensor_tensor(
                rowsA[0:2, 0:512], pq[0:2, 0:512], rowsA[0:2, 512:1024], OP.subtract))
            wm = [(S_T, MK[(tag, "bc", 0)])] if half == 1 else []
            emit("v", lambda v, pm=pm: v.tensor_copy(rowsB[0:2, 512:1024], pm[0:2, 0:512]),
                 wait=wm)
            MK[(tag, "var", half)] = S_V.n
            emit("s", lambda s: s.activation(rowsA[0:2, 512:1024], rowsA[0:2, 0:512],
                                             AF.Ln, bias=EPS, scale=1.0),
                 wait=[(S_V, MK[(tag, "var", half)])])
            bias = LN8 if use_ln8 else 0.0
            emit("s", lambda s, bias=bias: s.activation(
                rowsB[0:2, 0:512], rowsA[0:2, 512:1024], AF.Exp, bias=bias, scale=-0.5))
            MK[(tag, "rstd", half)] = S_S.n
            bm, br = (pu0, pu1) if half == 0 else (pd0, pd1)
            wb = [(S_S, MK[(tag, "rstd", half)]), (S_V, S_V.n)]
            for cc in range(4):
                emit("t", lambda t, cc=cc, bm=bm: t.matmul(
                    bm[:, cc * 128:cc * 128 + 128], hm2T_b[:, :],
                    rowsB[0:2, 512 + cc * 128:512 + cc * 128 + 128],
                    start=True, stop=True),
                     wait=wb if cc == 0 else ())
                emit("t", lambda t, cc=cc, br=br: t.matmul(
                    br[:, cc * 128:cc * 128 + 128], hm2T_b[:, :],
                    rowsB[0:2, cc * 128:cc * 128 + 128],
                    start=True, stop=True))
            MK[(tag, "bc", half)] = S_T.n
            for cc in range(4):
                c = half * 4 + cc
                emit("v", lambda v, c=c, cc=cc, bm=bm: v.tensor_tensor(
                    src_f[:, c, :], src_f[:, c, :], bm[:, cc * 128:cc * 128 + 128], OP.subtract),
                     wait=[(S_T, MK[(tag, "bc", half)])] if cc == 0 else ())
                emit("v", lambda v, c=c, cc=cc, br=br: v.scalar_tensor_tensor(
                    dst_bf[:, c, :], src_f[:, c, :], wcol[:, 0:1],
                    br[:, cc * 128:cc * 128 + 128], OP.mult, OP.mult))
            MK[(tag, "apply", half)] = S_V.n
        MK[(tag, "done")] = S_V.n

    for c in range(DC):
        emit("s", lambda s, c=c: s.activation(sq16[:, c, :], x16b[:, c, :], AF.Square),
             wait=[(S_G, MK["g_consts"])] if c == 0 else ())
    MK["x16sq"] = S_S.n
    for c in range(DC):
        emit("t", lambda t, c=c: t.matmul(pg0[0:1, 0:128], onescol_b[:, :], x16b[:, c, :],
                                          start=(c == 0), stop=(c == DC - 1)),
             wait=[(S_G, MK["g_consts"])] if c == 0 else ())
    for c in range(DC):
        emit("t", lambda t, c=c: t.matmul(pg0[32:33, 0:128], onescol_b[:, :], sq16[:, c, :],
                                          start=(c == 0), stop=(c == DC - 1)),
             wait=[(S_S, MK["x16sq"])] if c == 0 else ())
    MK["x16stats"] = S_T.n
    emit("v", lambda v: v.tensor_copy(rowsB[0:1, 0:128], pg0[0:1, 0:128]),
         wait=[(S_T, MK["x16stats"]), (S_V, MK["v_init"])])
    emit("v", lambda v: v.tensor_scalar(rowsA[0:1, 0:128], pg0[0:1, 0:128], 1.0 / D, None, OP.mult))
    emit("v", lambda v: v.tensor_tensor(rowsA[0:1, 128:256], rowsA[0:1, 0:128], rowsA[0:1, 0:128], OP.mult))
    emit("v", lambda v: v.tensor_scalar(rowsA[0:1, 256:384], pg0[32:33, 0:128], 1.0 / D, None, OP.mult))
    emit("v", lambda v: v.tensor_tensor(rowsA[0:1, 256:384], rowsA[0:1, 256:384], rowsA[0:1, 128:256], OP.subtract))
    MK["x16var"] = S_V.n
    emit("s", lambda s: s.activation(rowsA[0:1, 384:512], rowsA[0:1, 256:384], AF.Ln, bias=EPS, scale=1.0),
         wait=[(S_V, MK["x16var"])])
    emit("s", lambda s: s.activation(rowsB[0:1, 128:256], rowsA[0:1, 384:512], AF.Exp, bias=0.0, scale=-0.5))
    MK["x16rstd"] = S_S.n
    emit("t", lambda t: t.matmul(pg1[:, 0:128], inv1024_b[:, :], rowsB[0:1, 0:128], start=True, stop=True),
         wait=[(S_S, MK["x16rstd"])])
    emit("t", lambda t: t.matmul(pg1[:, 128:256], ones1_b[:, :], rowsB[0:1, 128:256], start=True, stop=True))
    MK["x16bc"] = S_T.n
    for c in range(DC):
        emit("v", lambda v, c=c: v.tensor_tensor(ascr[:, c, :], x16b[:, c, :], pg1[:, 0:128], OP.subtract),
             wait=[(S_T, MK["x16bc"])] if c == 0 else ())
        emit("v", lambda v, c=c: v.scalar_tensor_tensor(
            xab[:, c, :], ascr[:, c, :], lna8_s[:, c:c + 1], pg1[:, 128:256], OP.mult, OP.mult))
        emit("v", lambda v, c=c: v.scalar_tensor_tensor(
            xbb[:, c, :], ascr[:, c, :], lnb8_s[:, c:c + 1], pg1[:, 128:256], OP.mult, OP.mult))
    MK["xa"] = S_V.n
    MK["xb"] = S_V.n

    def proj1024(dstbank, tag, wt_first, wt_half=(), rhs=0):
        xr = xab if rhs == 0 else xbb
        for i in range(DC):
            for c in range(DC):
                bank, off = dstbank(i)
                wt = ()
                if i == 0 and c == 0:
                    wt = wt_first
                elif i == 4 and c == 0:
                    wt = wt_half
                emit("t", lambda t, i=i, c=c, bank=bank, off=off, xr=xr: t.matmul(
                    bank[:, off:off + 128], awT[:, c, i * 128:i * 128 + 128], xr[:, c, :],
                    start=(c == 0), stop=(c == DC - 1)),
                     wait=wt)
            MK[(tag, i)] = S_T.n
        MK[tag] = S_T.n

    proj1024(lambda i: (pg[i // 4], (i % 4) * 128), "qmm",
             [(S_G, MK["g_awqh"]), (S_V, MK["xa"])], [(S_G, MK["g_awq"])])
    emit("g", lambda g: g.dma_start(awT[:, :, 0:512], kwT_h[:, 0:512].rearrange("(c p) i -> p c i", p=128)),
         wait=[(S_T, MK["qmm"])])
    MK["g_awkh"] = S_G.n
    emit("g", lambda g: g.dma_start(awT[:, :, 512:1024], kwT_h[:, 512:1024].rearrange("(c p) i -> p c i", p=128)))
    MK["g_awk"] = S_G.n
    for i in range(DC):
        emit("s", lambda s, i=i: s.activation(
            qf[:, i, :], pg[i // 4][:, (i % 4) * 128:(i % 4) * 128 + 128], AF.Identity,
            bias=qb8_s[:, i:i + 1], scale=1.0),
             wait=[(S_T, MK[("qmm", (i // 4) * 4 + 3)])] if i % 4 == 0 else ())
        emit("s", lambda s, i=i: s.activation(qcb[:, i, :], qf[:, i, :], AF.Identity))
    MK["qevac"] = S_S.n
    proj1024(lambda i: (pu[i // 4], (i % 4) * 128), "kmm",
             [(S_G, MK["g_awkh"]), (S_S, MK["qevac"])], [(S_G, MK["g_awk"])], rhs=1)
    emit("g", lambda g: g.dma_start(awT[:, :, 0:512], vwT_h[:, 0:512].rearrange("(c p) i -> p c i", p=128)),
         wait=[(S_T, MK["kmm"])])
    MK["g_awvh"] = S_G.n
    emit("g", lambda g: g.dma_start(awT[:, :, 512:1024], vwT_h[:, 512:1024].rearrange("(c p) i -> p c i", p=128)))
    MK["g_awv"] = S_G.n
    for i in range(DC):
        emit("s", lambda s, i=i: s.activation(
            kvb[:, i, :], pu[i // 4][:, (i % 4) * 128:(i % 4) * 128 + 128], AF.Identity),
             wait=[(S_T, MK[("kmm", (i // 4) * 4 + 3)])] if i % 4 == 0 else ())
    MK["kevac"] = S_S.n
    for i in range(DC):
        emit("t", lambda t, i=i: t.matmul(
            pg[i // 4][:, (i % 4) * 128:(i % 4) * 128 + 128], lbd_b[:, 1, :], kvb[:, i, :],
            start=True, stop=True),
             wait=[(S_S, MK["kevac"])] if i == 0 else ())
    MK["kpre"] = S_T.n
    for i in range(DC):
        emit("s", lambda s, i=i: s.activation(
            ascr[:, i, :], pg[i // 4][:, (i % 4) * 128:(i % 4) * 128 + 128], AF.Identity,
            bias=lkb1_s[:, 0:1], scale=1.0),
             wait=[(S_T, MK["kpre"]), (S_V, S_V.n)] if i == 0 else ())
    MK["kpre_e"] = S_S.n
    head_ln(ascr, knb, lnd1_s, False, "klnh")

    proj1024(lambda i: (pd[2 + i // 4], (i % 4) * 128), "vmm",
             [(S_G, MK["g_awvh"]), (S_T, MK["kpre"]), (S_V, S_V.n)], [(S_G, MK["g_awv"])], rhs=1)
    emit("g", lambda g: g.dma_start(awT[:, :, 0:512], owT_h[:, 0:512].rearrange("(c p) i -> p c i", p=128)),
         wait=[(S_T, MK["vmm"])])
    MK["g_awoh"] = S_G.n
    emit("g", lambda g: g.dma_start(awT[:, :, 512:1024], owT_h[:, 512:1024].rearrange("(c p) i -> p c i", p=128)))
    MK["g_awo"] = S_G.n
    emit_wload(0, 0)
    emit_wload(1, 1)
    for i in range(DC):
        emit("s", lambda s, i=i: s.activation(
            kvb[:, i, :], pd[2 + i // 4][:, (i % 4) * 128:(i % 4) * 128 + 128], AF.Identity,
            bias=vb8_s[:, i:i + 1], scale=1.0),
             wait=[(S_T, MK[("vmm", (i // 4) * 4 + 3)])] if i % 4 == 0 else ())
    MK["vevac"] = S_S.n
    for i in range(DC):
        emit("t", lambda t, i=i: t.matmul(
            pg[i // 4][:, (i % 4) * 128:(i % 4) * 128 + 128], lbd_b[:, 2, :], kvb[:, i, :],
            start=True, stop=True),
             wait=[(S_S, MK["vevac"])] if i == 0 else ())
    MK["vpre"] = S_T.n
    for i in range(DC):
        emit("s", lambda s, i=i: s.activation(
            ascr[:, i, :], pg[i // 4][:, (i % 4) * 128:(i % 4) * 128 + 128], AF.Identity,
            bias=lvb1_s[:, 0:1], scale=1.0),
             wait=[(S_T, MK["vpre"]), (S_V, S_V.n)] if i == 0 else ())
    MK["vi_f"] = S_S.n
    for i in range(DC):
        emit("t", lambda t, i=i: t.transpose(
            pd[2 + i // 4][:, (i % 4) * 128:(i % 4) * 128 + 128], ascr[:, i, :], idf_s[:, :]),
             wait=[(S_S, MK["vi_f"]), (S_Y, MK["y_consts"])] if i == 0 else ())
    MK["vi_t"] = S_T.n
    for i in range(DC):
        emit("v", lambda v, i=i: v.tensor_copy(
            vit[:, i, :], pd[2 + i // 4][:, (i % 4) * 128:(i % 4) * 128 + 128]),
             wait=[(S_T, MK["vi_t"])] if i == 0 else ())
    MK["vit"] = S_V.n

    def attn_step(step):
        tg = ("st", step)
        for i in range(DC):
            wq = ([(S_S, MK[("qin", 0)])] if step == 0 else [(S_V, MK[("qin", step)])])
            emit("t", lambda t, i=i: t.matmul(
                pg[i // 4][:, (i % 4) * 128:(i % 4) * 128 + 128], lbd_b[:, 0, :], qcb[:, i, :],
                start=True, stop=True),
                 wait=wq if i == 0 else ())
        MK[(tg, "qpre")] = S_T.n
        for i in range(DC):
            emit("s", lambda s, i=i: s.activation(
                ascr[:, i, :], pg[i // 4][:, (i % 4) * 128:(i % 4) * 128 + 128], AF.Identity,
                bias=lqb1_s[:, 0:1], scale=1.0),
                 wait=[(S_T, MK[(tg, "qpre")]), (S_V, S_V.n)] if i == 0 else ())
        MK[(tg, "qpre_e")] = S_S.n
        head_ln(ascr, asb, lnc1_s, True, (tg, "qln"))
        for h in range(H):
            ic, half = h // 2, h % 2
            hp = h % 4
            sl = slice(half * 64, half * 64 + 64)
            lg = pd2[:, (h % 4) * 128:(h % 4) * 128 + 128]
            wt_l = [(S_V, MK[((tg, "qln"), "done")])] if h == 0 else []
            if h >= 4:
                if step < 2:
                    wt_l = wt_l + [(S_V, MK[(tg, "pevac", h - 4)])]
                else:
                    wt_l = wt_l + [(S_V, MK[(tg, "pn", h - 4)])]
            emit("t", lambda t, ic=ic, sl=sl, lg=lg: t.matmul(
                lg, asb[sl, ic, :], knb[sl, ic, :], start=True, stop=True), wait=wt_l)
            MK[(tg, "lg", h)] = S_T.n
            wt_m = [(S_T, MK[(tg, "lg", h)])]
            if h >= 4:
                wt_m.append((S_S, MK[(tg, "exp", h - 4)]))
            emit("v", lambda v, lg=lg, hp=hp: v.reduce_max(lmaxn[hp][:, 0:1], lg, axis=AX, negate=True),
                 wait=wt_m)
            emit("v", lambda v, lg=lg, hp=hp: v.tensor_tensor(pf[hp][:, :], lg, mask_s[:, :], OP.add))
            MK[(tg, "pf", h)] = S_V.n
            wt_e = [(S_V, MK[(tg, "pf", h)])]
            if h >= 4:
                wt_e.append((S_V, MK[(tg, "pn", h - 4)]))
            emit("s", lambda s, hp=hp: s.activation(pf2[hp][:, :], pf[hp][:, :], AF.Exp,
                                                    bias=lmaxn[hp][:, 0:1],
                                                    scale=1.0, accum_out=rsum[hp][:, 0:1]),
                 wait=wt_e)
            MK[(tg, "exp", h)] = S_S.n
            emit("v", lambda v, hp=hp: v.reciprocal(rrec[hp][:, 0:1], rsum[hp][:, 0:1]),
                 wait=[(S_S, MK[(tg, "exp", h)])])
            vfill()
            emit("v", lambda v, hp=hp: v.tensor_scalar(pf2[hp][:, :], pf2[hp][:, :], rrec[hp][:, 0:1], None, OP.mult))
            wt_c = [(S_T, MK[(tg, "pt" if step < 2 else "pto", h - 4)])] if h >= 4 else []
            emit("v", lambda v, hp=hp: v.tensor_copy(ptb[hp][:, :], pf2[hp][:, :]), wait=wt_c)
            MK[(tg, "pn", h)] = S_V.n
            if step < 2:
                ptp = pd3[:, (h % 4) * 128:(h % 4) * 128 + 128]
                emit("t", lambda t, ptp=ptp, hp=hp: t.matmul(ptp, ptb[hp][:, :], idb_s[:, :], start=True, stop=True),
                     wait=[(S_V, MK[(tg, "pn", h)])])
                MK[(tg, "pt", h)] = S_T.n
                wt_v = [(S_T, MK[(tg, "pt", h)])]
                if h >= 4:
                    wt_v.append((S_T, MK[(tg, "at", h - 4)]))
                emit("v", lambda v, ptp=ptp, hp=hp: v.tensor_copy(ptb2[hp][:, :], ptp),
                     wait=wt_v)
                MK[(tg, "pevac", h)] = S_V.n
                emit("t", lambda t, sl=sl, ic=ic, hp=hp: t.matmul(
                    pg[ic // 4][sl, (ic % 4) * 128:(ic % 4) * 128 + 128],
                    vit[:, ic, sl], ptb2[hp][:, :], start=True, stop=True),
                     wait=[(S_V, MK[(tg, "pevac", h)])])
                MK[(tg, "at", h)] = S_T.n
            else:
                emit("t", lambda t, h=h, hp=hp: t.matmul(pd3[:, h * 16:h * 16 + 16], ptb[hp][:, :],
                                                  selT_b[:, :], start=True, stop=True),
                     wait=[(S_V, MK[(tg, "pn", h)])])
                MK[(tg, "pto", h)] = S_T.n
                pob = ptob[h % 2]
                wt_p = [(S_T, MK[(tg, "pto", h)])]
                if h >= 2:
                    wt_p.append((S_T, MK[(tg, "aow", h - 2)]))
                emit("v", lambda v, pob=pob, h=h: v.tensor_copy(pob[:, :], pd3[:, h * 16:h * 16 + 16]),
                     wait=wt_p)
                MK[(tg, "ptob", h)] = S_V.n
                emit("t", lambda t, sl=sl, ic=ic, pob=pob: t.matmul(
                    pg0[sl, ic * 16:ic * 16 + 16], vit[:, ic, sl], pob[:, :], start=True, stop=True),
                     wait=[(S_V, MK[(tg, "ptob", h)])])
                MK[(tg, "aow", h)] = S_T.n
                if half == 1:
                    emit("v", lambda v, ic=ic: v.tensor_copy(aob[:, ic, :], pg0[:, ic * 16:ic * 16 + 16]),
                         wait=[(S_T, MK[(tg, "aow", h)])])
                    MK[(tg, "aoev", ic)] = S_V.n
        if step < 2:
            dst = it0 if step == 0 else it1
            for i in range(DC):
                emit("v", lambda v, i=i, dst=dst: v.tensor_copy(
                    dst[:, i, :], pg[i // 4][:, (i % 4) * 128:(i % 4) * 128 + 128]),
                     wait=[(S_T, MK[(tg, "at", 2 * i + 1)])])
            MK[(tg, "itev")] = S_V.n

    MK[("qin", 0)] = MK["qevac"]
    attn_step(0)
    for i in range(DC):
        emit("v", lambda v, i=i: v.tensor_tensor(qf[:, i, :], qf[:, i, :], it0[:, i, :], OP.add))
        emit("v", lambda v, i=i: v.tensor_copy(qcb[:, i, :], qf[:, i, :]))
    MK[("qin", 1)] = S_V.n
    attn_step(1)
    for i in range(DC):
        emit("v", lambda v, i=i: v.tensor_tensor(ascr[:, i, :], it1[:, i, :], it0[:, i, :], OP.subtract))
        emit("v", lambda v, i=i: v.reduce_sum(colscr[:, i:i + 1], ascr[:, i, :], axis=AX,
                                              apply_absolute_value=True))
    vfill()
    emit("v", lambda v: v.reduce_sum(dcol[:, 0:1], colscr[:, :], axis=AX))
    vfill()
    emit("v", lambda v: v.tensor_copy(dcolb[:, 0:1], dcol[:, 0:1]))
    MK["dcol"] = S_V.n
    emit("t", lambda t: t.matmul(pd3[0:1, 0:1], onescol_b[:, :], dcolb[:, 0:1], start=True, stop=True),
         wait=[(S_V, MK["dcol"])])
    MK["dsum"] = S_T.n
    emit("v", lambda v: v.tensor_scalar(m01s[0:1, 0:1], pd3[0:1, 0:1], TH_C, None, OP.is_ge),
         wait=[(S_T, MK["dsum"])])
    vfill()
    emit("v", lambda v: v.tensor_copy(m01b[0:1, 0:1], m01s[0:1, 0:1]))
    MK["m01b"] = S_V.n
    emit("t", lambda t: t.matmul(pd3[:, 4:5], ones1_b[:, :], m01b[0:1, 0:1], start=True, stop=True),
         wait=[(S_V, MK["m01b"])])
    MK["m01bc"] = S_T.n
    emit("v", lambda v: v.tensor_copy(m01c[:, 0:1], pd3[:, 4:5]),
         wait=[(S_T, MK["m01bc"])])
    vfill()
    for i in range(DC):
        emit("v", lambda v, i=i: v.scalar_tensor_tensor(
            it1[:, i, :], it1[:, i, :], m01c[:, 0:1], qf[:, i, :], OP.mult, OP.add))
        emit("v", lambda v, i=i: v.tensor_copy(qcb[:, i, :], it1[:, i, :]))
    MK[("qin", 2)] = S_V.n
    attn_step(2)

    for i in range(DC):
        for c in range(DC):
            wt = []
            if i == 0 and c == 0:
                wt = [(S_G, MK["g_awoh"]), (S_V, MK[(("st", 2), "aoev", DC - 1)])]
            elif i == 4 and c == 0:
                wt = [(S_G, MK["g_awo"])]
            emit("t", lambda t, i=i, c=c: t.matmul(
                pd2[:, i * 16:i * 16 + 16], awT[:, c, i * 128:i * 128 + 128], aob[:, c, :],
                start=(c == 0), stop=(c == DC - 1)), wait=wt)
    MK["oproj"] = S_T.n
    for i in range(DC):
        emit("v", lambda v, i=i: v.tensor_copy(o16[:, i, :], pd2[:, i * 16:i * 16 + 16]),
             wait=[(S_T, MK["oproj"])] if i == 0 else ())
    MK["o16ev"] = S_V.n

    SA.close()
    sp = lambda name, shape, dt=F32: es.enter_context(nc.sbuf_tensor(name, list(shape), dt))
    xp_bf = sp("xp_bf", (128, DC, S), BF16)
    sq_bf = sp("sq_bf", (128, DC, LNT), BF16)
    rl_s2 = [sp(f"rl_s{i}", (16, LNT)) for i in range(2)]
    rowsL = sp("rowsL", (1, 1536))
    rowsLb = sp("rowsLb", (1, 1024), BF16)
    rlt_s = sp("rlt_s", (128, 16))
    rls_t = sp("rls_t", (128, 16))
    xlt = sp("xlt", (128, LNT))
    eqs = sp("eqs", (128, 16))
    mx1 = sp("mx1", (128, 1))
    mx2 = sp("mx2", (128, 1))
    rc_ = sp("rc_", (128, 1))

    def stream_ln(tile, with_xp, tag):
        t0 = tile * LNT
        tcap = [(S_T, S_T.n), (S_S, MK["obadd"])]
        for c in range(DC):
            emit("v", lambda v, c=c, t0=t0: v.tensor_copy(xp_bf[:, c, t0:t0 + LNT], strm[:, c, t0:t0 + LNT]),
                 wait=tcap if c == 0 else ())
        MK[(tag, "cast")] = S_V.n
        for c in range(DC):
            emit("s", lambda s, c=c, t0=t0: s.activation(sq_bf[:, c, :], strm[:, c, t0:t0 + LNT], AF.Square),
                 wait=(tcap + [(S_V, MK[(tag, "cast")])]) if c == 0 else ())
        MK[(tag, "sq")] = S_S.n
        for c in range(DC):
            emit("t", lambda t, c=c, t0=t0: t.matmul(pd0[0:1, 0:512], onescol_b[:, :], xp_bf[:, c, t0:t0 + LNT],
                                              start=(c == 0), stop=(c == DC - 1)),
                 wait=[(S_V, MK[(tag, "cast")])] if c == 0 else ())
        for c in range(DC):
            emit("t", lambda t, c=c: t.matmul(pd0[32:33, 0:512], onescol_b[:, :], sq_bf[:, c, :],
                                              start=(c == 0), stop=(c == DC - 1)),
                 wait=[(S_S, MK[(tag, "sq")])] if c == 0 else ())
        MK[(tag, "stats")] = S_T.n
        emit("v", lambda v: v.tensor_copy(rowsLb[0:1, 0:512], pd0[0:1, 0:512]),
             wait=[(S_T, MK[(tag, "stats")])])
        emit("v", lambda v: v.tensor_scalar(rowsL[0:1, 0:512], pd0[0:1, 0:512], 1.0 / D, None, OP.mult))
        emit("v", lambda v: v.tensor_tensor(rowsL[0:1, 512:1024], rowsL[0:1, 0:512], rowsL[0:1, 0:512], OP.mult))
        emit("v", lambda v: v.tensor_scalar(rowsL[0:1, 1024:1536], pd0[32:33, 0:512], 1.0 / D, None, OP.mult))
        emit("v", lambda v: v.tensor_tensor(rowsL[0:1, 1024:1536], rowsL[0:1, 1024:1536],
                                            rowsL[0:1, 512:1024], OP.subtract))
        MK[(tag, "var")] = S_V.n
        emit("s", lambda s: s.activation(rowsL[0:1, 512:1024], rowsL[0:1, 1024:1536], AF.Ln, bias=EPS, scale=1.0),
             wait=[(S_V, MK[(tag, "var")])])
        emit("s", lambda s: s.activation(rowsLb[0:1, 512:1024], rowsL[0:1, 512:1024], AF.Exp, bias=0.0, scale=-0.5))
        MK[(tag, "rstd")] = S_S.n
        emit("t", lambda t: t.matmul(pd1[:, 0:512], inv1024_b[:, :], rowsLb[0:1, 0:512], start=True, stop=True),
             wait=[(S_S, MK[(tag, "rstd")])])
        MK[(tag, "bcm")] = S_T.n
        for c in range(DC):
            emit("v", lambda v, c=c, t0=t0: v.tensor_tensor(
                strm[:, c, t0:t0 + LNT], strm[:, c, t0:t0 + LNT], pd1[:, 0:512], OP.subtract),
                 wait=[(S_T, MK[(tag, "bcm")])] if c == 0 else ())
        MK[(tag, "sub")] = S_V.n
        emit("t", lambda t: t.matmul(pd1[:, 0:512], ones1_b[:, :], rowsLb[0:1, 512:1024], start=True, stop=True),
             wait=[(S_V, MK[(tag, "sub")])])
        MK[(tag, "bcr")] = S_T.n
        if with_xp:
            for c in range(DC):
                emit("v", lambda v, c=c, t0=t0: v.tensor_tensor(
                    xp_bf[:, c, t0:t0 + LNT], strm[:, c, t0:t0 + LNT], pd1[:, 0:512], OP.mult),
                     wait=[(S_T, MK[(tag, "bcr")])] if c == 0 else ())
            MK[(tag, "xpw")] = S_V.n
            emit("v", lambda v: v.tensor_scalar(rowsLb[0:1, 0:512], rowsLb[0:1, 512:1024],
                                                2.0, None, OP.mult),
                 wait=[(S_T, MK[(tag, "bcm")])])
            MK[(tag, "r2")] = S_V.n
            emit("t", lambda t: t.matmul(pd1[:, 0:512], ones1_b[:, :], rowsLb[0:1, 0:512],
                                         start=True, stop=True),
                 wait=[(S_V, MK[(tag, "r2")]), (S_V, MK[(tag, "xpw")])]),
            MK.__setitem__((tag, "bcr2"), S_T.n)
            for c in range(DC):
                emit("v", lambda v, c=c, t0=t0: v.tensor_tensor(
                    strm[:, c, t0:t0 + LNT], strm[:, c, t0:t0 + LNT], pd1[:, 0:512], OP.mult),
                     wait=[(S_T, MK[(tag, "bcr2")])] if c == 0 else ())
        else:
            for c in range(DC):
                emit("v", lambda v, c=c, t0=t0: v.tensor_tensor(
                    strm[:, c, t0:t0 + LNT], strm[:, c, t0:t0 + LNT], pd1[:, 0:512], OP.mult),
                     wait=[(S_T, MK[(tag, "bcr")])] if c == 0 else ())
        MK[(tag, "done")] = S_V.n

    def emit_xlo(lt, wt_extra=()):
        t0 = lt * LNT
        wt = [(S_V, MK[(("xln", lt), "done")]), (S_T, S_T.n)] + list(wt_extra)
        for c in range(DC):
            emit("v", lambda v, c=c, t0=t0: v.tensor_scalar(
                xlt[:, :], strm[:, c, t0:t0 + LNT], 0.5, None, OP.mult),
                 wait=wt if c == 0 else ())
            emit("v", lambda v, c=c, t0=t0: v.tensor_tensor(
                sq_bf[:, c, :], xlt[:, :], xp_bf[:, c, t0:t0 + LNT], OP.subtract))
        MK[("xlo", lt)] = S_V.n

    def emit_rlog(lt, plg, wt_extra=()):
        t0 = lt * LNT
        for c in range(DC):
            wt = ([(S_V, MK[("xlo", lt)]), (S_V, S_V.n)] + list(wt_extra)) if c == 0 else []
            emit("t", lambda t, c=c, t0=t0, plg=plg: t.matmul(
                plg[0:16, 0:512], gwhi_s[:, c * 16:c * 16 + 16], xp_bf[:, c, t0:t0 + LNT],
                start=(c == 0), stop=False), wait=wt)
            emit("t", lambda t, c=c, t0=t0, plg=plg: t.matmul(
                plg[0:16, 0:512], gwlo_s[:, c * 16:c * 16 + 16], xp_bf[:, c, t0:t0 + LNT],
                start=False, stop=False))
            emit("t", lambda t, c=c, t0=t0, plg=plg: t.matmul(
                plg[0:16, 0:512], gwhi_s[:, c * 16:c * 16 + 16], sq_bf[:, c, :],
                start=False, stop=(c == DC - 1)))
        MK[("rlog", lt)] = S_T.n
        rls = rl_s2[lt % 2]
        emit("v", lambda v, plg=plg, rls=rls: v.tensor_scalar(
            rls[0:16, 0:512], plg[0:16, 0:512], gb_s[0:16, 0:1], None, OP.add),
             wait=[(S_T, MK[("rlog", lt)]), (S_T, S_T.n)])
        MK[("sig", lt)] = S_V.n

    top2_prev = {}

    def emit_top2(lt, tp_bank, tr_bank, wt_extra=()):
        for kk in range(4):
            k = lt * 4 + kk
            wt = [(S_V, MK[("sig", lt)])] + (list(wt_extra) if kk == 0 else [])
            if kk in top2_prev:
                wt.append((S_V, MK[("wlo", top2_prev[kk])]))
            top2_prev[kk] = k
            emit("t", lambda t, kk=kk, tp_bank=tp_bank, rls=rl_s2[lt % 2]: t.transpose(
                tp_bank[:, kk * 16:kk * 16 + 16], rls[0:16, kk * 128:kk * 128 + 128],
                idf_s[0:16, 0:16]), wait=wt)
            MK[("rlt", k)] = S_T.n
            emit("s", lambda s, kk=kk, tp_bank=tp_bank: s.activation(
                rls_t[:, :], tp_bank[:, kk * 16:kk * 16 + 16], AF.Sigmoid),
                 wait=[(S_T, MK[("rlt", k)])])
            MK[("sigt", k)] = S_S.n
            emit("v", lambda v, kk=kk, tp_bank=tp_bank: v.tensor_copy(
                rlt_s[:, :], tp_bank[:, kk * 16:kk * 16 + 16]),
                 wait=[(S_T, MK[("rlt", k)]), (S_S, MK[("sigt", k)])])
            vfill()
            emit("v", lambda v: v.reduce_max(mx1[:, 0:1], rlt_s[:, :], axis=AX))
            vfill()
            emit("v", lambda v: v.tensor_scalar(eqs[:, :], rlt_s[:, :], mx1[:, 0:1], None, OP.is_ge))
            vfill()
            emit("v", lambda v: v.tensor_scalar(eqs[:, :], eqs[:, :], NEG_BIG, None, OP.mult))
            vfill()
            emit("v", lambda v: v.tensor_tensor(eqs[:, :], eqs[:, :], rlt_s[:, :], OP.add))
            vfill()
            emit("v", lambda v: v.reduce_max(mx2[:, 0:1], eqs[:, :], axis=AX))
            vfill()
            emit("v", lambda v: v.tensor_scalar(eqs[:, :], rlt_s[:, :], mx2[:, 0:1], None, OP.is_ge))
            vfill()
            emit("v", lambda v: v.tensor_tensor(rlt_s[:, :], rls_t[:, :], eqs[:, :], OP.mult))
            vfill()
            emit("v", lambda v: v.reduce_sum(mx1[:, 0:1], rlt_s[:, :], axis=AX))
            vfill()
            emit("v", lambda v: v.reciprocal(rc_[:, 0:1], mx1[:, 0:1]))
            vfill()
            emit("v", lambda v: v.tensor_scalar(rlt_s[:, :], rlt_s[:, :], rc_[:, 0:1], None, OP.mult))
            MK[("wtok", k)] = S_V.n
            emit("t", lambda t, kk=kk, tr_bank=tr_bank: t.transpose(
                tr_bank[0:16, kk * 128:kk * 128 + 128], rlt_s[:, :], idf_s[:, :]),
                 wait=[(S_V, MK[("wtok", k)])])
            MK[("wtr", k)] = S_T.n
            emit("v", lambda v, k=k, kk=kk, tr_bank=tr_bank: v.tensor_copy(
                w_hi[0:16, k * 128:k * 128 + 128], tr_bank[0:16, kk * 128:kk * 128 + 128]),
                 wait=[(S_T, MK[("wtr", k)])])
            emit("v", lambda v, k=k, kk=kk, tr_bank=tr_bank: v.tensor_tensor(
                w_lo[0:16, k * 128:k * 128 + 128], tr_bank[0:16, kk * 128:kk * 128 + 128],
                w_hi[0:16, k * 128:k * 128 + 128], OP.subtract))
            MK[("wlo", k)] = S_V.n

    for c in range(DC):
        emit("s", lambda s, c=c: s.activation(
            strm[:, c, :], strm[:, c, :], AF.Identity, bias=ob8_s[:, c:c + 1], scale=1.0),
             wait=[(S_Y, MK["y_xT"]), (S_Y, MK["y_consts"])] if c == 0 else ())
    MK["obadd"] = S_S.n
    stream_ln(1, True, ("xln", 1))
    emit_xlo(1)
    emit_rlog(1, pd1)
    stream_ln(2, True, ("xln", 2))
    emit_xlo(2)
    emit_rlog(2, pd1)

    ABANK = [pg0, pg1, pu0, pu1]
    BBANK = [pd0, pd1, pd2, pd3]

    RUNS = [(e, [1, 2, 3]) for e in range(NEXP)] + [(e, [0]) for e in range(NEXP)]
    SCHED = []
    RUN_FIRST = {}
    for _r, (_e, _ts) in enumerate(RUNS):
        RUN_FIRST[_r] = len(SCHED)
        for _t in _ts:
            SCHED.append((_r, _e, _t))
    NSCH = len(SCHED)

    def emit_gu(k):
        run, e, lt = SCHED[k]
        par = run % 2
        t0 = lt * GT
        for i in range(IC):
            hgb = (ABANK[i] if i < 2 else BBANK[i - 2])
            hub = (ABANK[2 + i] if i < 2 else BBANK[2 + (i - 2)])
            wt = []
            if i == 0:
                if k == RUN_FIRST[run]:
                    wt.append((S_G, MK[("wload", run)]))
                if run == 0 or run == NEXP:
                    wt.append((S_V, MK[(("xln", lt), "done")]))
                if k >= 1:
                    wt.append((S_S, MK[("silu", k - 1, 1)]))
                    wt.append((S_V, MK[("h1", k - 1, 1)]))
                if k >= 2:
                    wt.append((S_V, MK[("drain", k - 2, 0)]))
            if i == 2:
                if k >= 1:
                    wt.append((S_S, MK[("silu", k - 1, 3)]))
                    wt.append((S_V, MK[("h1", k - 1, 3)]))
                if k >= 2:
                    wt.append((S_V, MK[("drain", k - 2, 1)]))
                if pending_wbc:
                    wt.append((S_V, MK[("wbcev", pending_wbc.pop())]))
                if extra_wd_wait:
                    wt.extend(extra_wd_wait)
                    del extra_wd_wait[:]
            for c in range(DC):
                emit("t", lambda t, i=i, c=c, par=par, t0=t0, hgb=hgb: t.matmul(
                    hgb[:, 0:GT], wg_sb[par][:, c, i * 128:i * 128 + 128], xp_bf[:, c, t0:t0 + GT],
                    start=(c == 0), stop=(c == DC - 1)),
                     wait=wt if c == 0 else ())
            for c in range(DC):
                emit("t", lambda t, i=i, c=c, par=par, t0=t0, hub=hub: t.matmul(
                    hub[:, 0:GT], wu_sb[par][:, c, i * 128:i * 128 + 128], xp_bf[:, c, t0:t0 + GT],
                    start=(c == 0), stop=(c == DC - 1)))
            MK[("gu", k, i)] = S_T.n

    extra_wd_wait = []
    pending_wbc = []

    def emit_wd(k, sub, cur, extra=()):
        run, e, lt = SCHED[k]
        par = run % 2
        hp = k % 2
        bs = ABANK if sub == 0 else BBANK
        wt0 = [(S_V, MK[("hb", k, IC - 1)])] + list(extra)
        wt0.append((S_S, MK[("silu", cur, 1 if sub == 0 else 3)]))
        wt0.append((S_V, MK[("h1", cur, 1 if sub == 0 else 3)]))
        for c in range(DC):
            dstp = bs[c // 2][:, (c % 2) * TT:(c % 2) * TT + TT]
            for i in range(IC):
                emit("t", lambda t, i=i, c=c, par=par, hp=hp, sub=sub, dstp=dstp: t.matmul(
                    dstp, wd_sb[par][:, i, c * 128:c * 128 + 128],
                    h_sb[hp][:, i, sub * TT:sub * TT + TT],
                    start=(i == 0), stop=(i == IC - 1)),
                     wait=wt0 if (i == 0 and c == 0) else ())
            MK[("wdmm", k, sub, c)] = S_T.n
        MK[("wdmm", k, sub)] = S_T.n

    def emit_evac_sh(k):
        for i in range(IC):
            hgb = (ABANK[i] if i < 2 else BBANK[i - 2])
            hub = (ABANK[2 + i] if i < 2 else BBANK[2 + (i - 2)])
            wt = [(S_T, MK[("gu", k, i)])]
            if k >= 1:
                wt.append((S_V, MK[("hb", k - 1, i)]))
            emit("s", lambda s, i=i, hgb=hgb: s.activation(
                sx_sb[:, i, :], hgb[:, 0:GT], AF.Silu), wait=wt)
            MK[("silu", k, i)] = S_S.n
            emit("v", lambda v, i=i, hub=hub: v.tensor_tensor(
                sx_sb[:, i, :], sx_sb[:, i, :], hub[:, 0:GT], OP.mult),
                 wait=[(S_S, MK[("silu", k, i)])])
            MK[("h1", k, i)] = S_V.n

    def emit_evac_hb(k):
        run, e, lt = SCHED[k]
        hp = k % 2
        t0 = lt * GT
        for i in range(IC):
            wh = []
            if k >= 2:
                wh.append((S_T, MK[("wdmm", k - 2, 1)]))
            if e == 0:
                emit("v", lambda v, i=i, hp=hp: v.tensor_copy(h_sb[hp][:, i, :], sx_sb[:, i, :]),
                     wait=wh)
            else:
                emit("v", lambda v, i=i, hp=hp, t0=t0: v.tensor_tensor(
                    h_sb[hp][:, i, :], sx_sb[:, i, :], wbc[:, t0:t0 + GT], OP.mult),
                     wait=wh)
            MK[("hb", k, i)] = S_V.n

    def emit_drain(k, sub):
        run, e, lt = SCHED[k]
        t0 = lt * GT + sub * TT
        bs = ABANK if sub == 0 else BBANK
        for c in range(DC):
            srcp = bs[c // 2][:, (c % 2) * TT:(c % 2) * TT + TT]
            emit("v", lambda v, c=c, t0=t0, srcp=srcp: v.tensor_tensor(
                strm[:, c, t0:t0 + TT], strm[:, c, t0:t0 + TT], srcp, OP.add),
                 wait=[(S_T, MK[("wdmm", k, sub, c | 1)])])
        MK[("drain", k, sub)] = S_V.n

    def emit_wbc(run, prev_k):
        e = RUNS[run][0]
        ei = e - 1
        lts = RUNS[run][1]
        wt = [(S_V, MK[("drain", prev_k, 1)])]
        for lt in lts:
            wt.append((S_V, MK[("wlo", lt * 4 + 3)]))
        for ridx, lt in enumerate(lts):
            t0 = lt * LNT
            emit("t", lambda t, ei=ei, t0=t0, ridx=ridx: t.matmul(
                pd[ridx][:, 0:512], oh16_b[0:16, ei * 128:ei * 128 + 128], w_hi[0:16, t0:t0 + LNT],
                start=True, stop=False), wait=wt if ridx == 0 else ())
            emit("t", lambda t, ei=ei, t0=t0, ridx=ridx: t.matmul(
                pd[ridx][:, 0:512], oh16_b[0:16, ei * 128:ei * 128 + 128], w_lo[0:16, t0:t0 + LNT],
                start=False, stop=True))
            MK[("wbcmm", run, ridx)] = S_T.n
        for ridx, lt in enumerate(lts):
            t0 = lt * LNT
            emit("v", lambda v, t0=t0, ridx=ridx: v.tensor_copy(wbc[:, t0:t0 + LNT], pd[ridx][:, 0:512]),
                 wait=[(S_T, MK[("wbcmm", run, ridx)])])
        MK[("wbcev", run)] = S_V.n

    def hook_a():
        emit_top2(1, pd0, pd1, wt_extra=[(S_V, S_V.n)])
        stream_ln(3, True, ("xln", 3))
        emit_xlo(3)
        emit_rlog(3, pd1)
        extra_wd_wait.append((S_V, MK[("wlo", 7)]))
        extra_wd_wait.append((S_V, MK[(("xln", 3), "done")]))
        extra_wd_wait.append((S_V, MK[("xlo", 3)]))

    def hook_b():
        emit_top2(2, pd0, pd1)
        extra_wd_wait.append((S_V, MK[("wlo", 11)]))

    def hook_c():
        emit_top2(3, pd0, pd1)
        extra_wd_wait.append((S_V, MK[("wlo", 15)]))

    def hook_xln0():
        for c in range(DC):
            emit("v", lambda v, c=c: v.tensor_tensor(
                strm[:, c, 0:16], strm[:, c, 0:16], o16[:, c, :], OP.add))
        MK["ownadd"] = S_V.n
        stream_ln(0, True, ("xln", 0))
        emit_xlo(0)
        extra_wd_wait.append((S_V, MK[(("xln", 0), "done")]))
        extra_wd_wait.append((S_V, MK[("xlo", 0)]))

    def hook_router0():
        emit_rlog(0, pd0, wt_extra=[(S_V, MK[(("xln", 0), "done")])])
        emit_top2(0, pd0, pd1, wt_extra=[(S_V, MK[("sig", 0)])])
        MK["router_all"] = S_V.n
        extra_wd_wait.append((S_V, MK["router_all"]))

    ndma = [0]

    def emit_fln(lt):
        stream_ln(lt, False, ("fln", lt))
        t0 = lt * LNT
        for c in range(DC):
            wt = [(S_V, MK[(("fln", lt), "done")])] if c < 2 else []
            if ndma[0] % 2 == 0:
                emit("y", lambda y, c=c, t0=t0: y.dma_start(
                    outT_h[c * 128:(c + 1) * 128, t0:t0 + LNT], strm[:, c, t0:t0 + LNT]), wait=wt)
            else:
                emit("s", lambda s, c=c, t0=t0: s.dma_start(
                    outT_h[c * 128:(c + 1) * 128, t0:t0 + LNT], strm[:, c, t0:t0 + LNT]), wait=wt, sem=S_C)
            ndma[0] += 1

    def hook_fln(lt):
        def h():
            emit_fln(lt)
            extra_wd_wait.append((S_V, MK[(("fln", lt), "done")]))
        return h

    NP1 = NEXP * 3
    hooks = {0: hook_a, 1: hook_b, 2: hook_c,
             NP1 - 8: hook_xln0, NP1 - 7: hook_router0,
             NP1 - 3: hook_fln(1), NP1 - 2: hook_fln(2), NP1 - 1: hook_fln(3)}

    prev_k = None
    for k in range(NSCH):
        run, e, lt = SCHED[k]
        emit_gu(k)
        emit_evac_sh(k)
        if prev_k is not None:
            emit_wd(prev_k, 0, cur=k)
            emit_drain(prev_k, 0)
            emit_wd(prev_k, 1, cur=k)
            emit_drain(prev_k, 1)
            if prev_k in hooks:
                hooks[prev_k]()
            prun = SCHED[prev_k][0]
            if run != prun:
                MK[("emm", prun)] = S_T.n
                if RUNS[run][0] >= 1:
                    emit_wbc(run, prev_k)
                    pending_wbc.append(run)
                if run + 1 < len(RUNS):
                    emit_wload(run + 1, RUNS[run + 1][0])
        emit_evac_hb(k)
        prev_k = k
    fln_gate = list(extra_wd_wait)
    del extra_wd_wait[:]
    emit_wd(NSCH - 1, 0, cur=NSCH - 1, extra=fln_gate)
    emit_drain(NSCH - 1, 0)
    emit_wd(NSCH - 1, 1, cur=NSCH - 1)
    emit_drain(NSCH - 1, 1)
    MK[("emm", len(RUNS) - 1)] = S_T.n
    emit_fln(0)

    if DEBUG:
        dbg_y_h = nc.dram_tensor("dbg_y", [128, DC, S], F32, kind="ExternalOutput").ap()
        dbg_xp_h = nc.dram_tensor("dbg_xp", [128, DC, S], BF16, kind="ExternalOutput").ap()
        emit("y", lambda y: y.dma_start(dbg_y_h, strm[:, :, :]),
             wait=[(S_V, MK[("drain", NEXP - 1, NT - 1, 1)])])
        emit("y", lambda y: y.dma_start(dbg_xp_h, xp_bf[:, :, :]))
        MK["dbgdma"] = S_Y.n
    emit("v", lambda v: None, wait=[(S_Y, S_Y.n), (S_C, S_C.n)], sem=None)

    with es, nc.Block() as block:
        @block.tensor
        def _(t):
            for f in P["t"]:
                f(t)

        @block.vector
        def _(v):
            for f in P["v"]:
                f(v)

        @block.scalar
        def _(s):
            for f in P["s"]:
                f(s)

        @block.sync
        def _(y):
            for f in P["y"]:
                f(y)

        @block.gpsimd
        def _(g):
            for f in P["g"]:
                f(g)

    return nc


def kernel(**inputs):
    gi = lambda k: np.ascontiguousarray(np.asarray(inputs[k], dtype=np.float32))
    x = gi("L0_attn_x")
    qw, kw, vw, ow = gi("L0_attn_qw"), gi("L0_attn_kw"), gi("L0_attn_vw"), gi("L0_attn_ow")
    qb, vb, ob = gi("L0_attn_qb"), gi("L0_attn_vb"), gi("L0_attn_ob")
    lna, lnb = gi("L0_attn_lna_w"), gi("L0_attn_lnb_w")
    lnc, lnd = gi("L0_attn_lnc_w"), gi("L0_attn_lnd_w")
    lqw, lqb = gi("L0_attn_lqw"), gi("L0_attn_lqb")
    lkw, lkb = gi("L0_attn_lkw"), gi("L0_attn_lkb")
    lvw, lvb = gi("L0_attn_lvw"), gi("L0_attn_lvb")
    gw, gb = gi("L0_ffn_gate_w"), gi("L0_ffn_gate_b")
    swg, swu, swd = gi("L0_ffn_shared_wg"), gi("L0_ffn_shared_wu"), gi("L0_ffn_shared_wd")
    ewg, ewu, ewd = gi("L0_ffn_experts_wg"), gi("L0_ffn_experts_wu"), gi("L0_ffn_experts_wd")

    c8 = lambda v: np.ascontiguousarray(v.reshape(DC, 128).T)
    t2 = lambda v: np.ascontiguousarray(np.tile(v, 2).reshape(128, 1))
    z64 = np.zeros((64, 64), np.float32)
    bd = lambda w: np.ascontiguousarray(np.block([[w.T, z64], [z64, w.T]]))
    mask = np.full((128, 128), NEG_BIG, np.float32)
    for b in range(8):
        mask[b * 16:(b + 1) * 16, b * 16:(b + 1) * 16] = 0.0
    oh16 = np.zeros((16, 16 * 128), np.float32)
    for e in range(16):
        oh16[e, e * 128:(e + 1) * 128] = 1.0
    hm2s = np.zeros((128, 2), np.float32)
    hm2s[0:64, 0] = 1.0 / 64
    hm2s[64:128, 1] = 1.0 / 64
    hm2T = np.zeros((2, 128), np.float32)
    hm2T[0, 0:64] = 1.0
    hm2T[1, 64:128] = 1.0
    gw128 = np.ascontiguousarray(gw.reshape(DC, 128, 16).transpose(1, 0, 2).reshape(128, 128))

    common = {
        "x16T": np.ascontiguousarray(x[:, :16, :].reshape(128, D).T),
        "qwT": np.ascontiguousarray(qw.T), "kwT": np.ascontiguousarray(kw.T),
        "vwT": np.ascontiguousarray(vw.T), "owT": np.ascontiguousarray(ow.T),
        "lqbd": bd(lqw), "lkbd": bd(lkw), "lvbd": bd(lvw),
        "qb8": c8(qb), "vb8": c8(vb), "ob8": c8(ob), "lna8": c8(lna), "lnb8": c8(lnb),
        "lqb1": t2(lqb), "lkb1": t2(lkb), "lvb1": t2(lvb), "lnc1": t2(lnc), "lnd1": t2(lnd),
        "gw128": gw128, "gb16": np.ascontiguousarray(gb.reshape(16, 1)),
        "idf": np.eye(128, dtype=np.float32), "mask": mask, "oh16": oh16,
        "inv1024r": np.full((1, 128), 1.0 / 1024, np.float32),
        "ones1r": np.ones((1, 128), np.float32),
        "onescol": np.ones((128, 1), np.float32),
        "hm2s": hm2s, "hm2T": hm2T,
        "swg": swg, "swu": swu, "swd": swd, "ewg": ewg, "ewu": ewu, "ewd": ewd,
    }
    in_maps = []
    for i in range(8):
        sel = np.zeros((128, 16), np.float32)
        for j in range(16):
            sel[i * 16 + j, j] = 1.0
        m = dict(common)
        m["xT"] = np.ascontiguousarray(x[i].T)
        m["selT"] = sel
        in_maps.append(m)

    nc = build_nc()
    res = run_bass_kernel_spmd(nc, in_maps, core_ids=list(range(8)))
    out = np.empty((B, S, D), np.float32)
    for i in range(8):
        out[i] = np.asarray(res.results[i]["outT"]).T
    return out


if __name__ == "__main__":
    import reference as _r
    ins = _r.setup_inputs(0)
    got = kernel(**{k: np.asarray(v) for k, v in ins.items()})
    want = np.asarray(_r.reference(**ins))
    num = float(((got - want) ** 2).mean())
    den = float((want ** 2).mean())
    print("resid_var_ratio:", num / den)


# version 26
# speedup vs baseline: 1.1650x; 1.1650x over previous
import math
import numpy as np
import ml_dtypes
import concourse.bass as bass
import concourse.mybir as mybir
from concourse.bass_utils import run_bass_kernel_spmd
from contextlib import ExitStack

F32 = mybir.dt.float32
BF16 = mybir.dt.bfloat16
AX = mybir.AxisListType.X
OP = mybir.AluOpType
AF = mybir.ActivationFunctionType

B, S, D, H, HD, E, II = 8, 2048, 1024, 16, 64, 16, 512
DC, IC = D // 128, II // 128
TT = 256
GT = 512
NT = S // GT
GTO = [1, 2, 3, 0]
LNT = 512
NLT = S // LNT
EPS = 1e-5
LN8 = math.log(0.125)
TH_C = 16777216.0 * 0.01 / 0.9
NEG_BIG = -1.0e30
TILE_ORDER = [2, 3, 4, 5, 6, 7, 0, 1]
LN_EARLY = [1, 2, 3]
NEXP = 17
DEBUG = False


def build_nc():
    nc = bass.Bass("TRN2", target_bir_lowering=False, detect_race_conditions=False)

    def hin(name, shape):
        return nc.dram_tensor(name, list(shape), F32, kind="ExternalInput").ap()

    xT_h = hin("xT", (D, S))
    selT_h = hin("selT", (128, 16))
    x16T_h = hin("x16T", (D, 128))
    qwT_h = hin("qwT", (D, D)); kwT_h = hin("kwT", (D, D))
    vwT_h = hin("vwT", (D, D)); owT_h = hin("owT", (D, D))
    lqbd_h = hin("lqbd", (128, 128)); lkbd_h = hin("lkbd", (128, 128)); lvbd_h = hin("lvbd", (128, 128))
    qb8_h = hin("qb8", (128, DC)); vb8_h = hin("vb8", (128, DC)); ob8_h = hin("ob8", (128, DC))
    lna8_h = hin("lna8", (128, DC)); lnb8_h = hin("lnb8", (128, DC))
    lqb1_h = hin("lqb1", (128, 1)); lkb1_h = hin("lkb1", (128, 1)); lvb1_h = hin("lvb1", (128, 1))
    lnc1_h = hin("lnc1", (128, 1)); lnd1_h = hin("lnd1", (128, 1))
    gw128_h = hin("gw128", (128, 128))
    gb16_h = hin("gb16", (16, 1))
    idf_h = hin("idf", (128, 128))
    mask_h = hin("mask", (128, 128))
    mneg_h = hin("mneg", (1, 128))
    b1e30_h = hin("b1e30", (8, 128))
    bsel_h = hin("bsel", (8, 128))
    oh16_h = hin("oh16", (16, 16 * 128))
    inv1024r_h = hin("inv1024r", (1, 128))
    ones1r_h = hin("ones1r", (1, 128))
    onescol_h = hin("onescol", (128, 1))
    hm2s_h = hin("hm2s", (128, 2))
    hm2T_h = hin("hm2T", (2, 128))
    swg_h = hin("swg", (D, II)); swu_h = hin("swu", (D, II)); swd_h = hin("swd", (II, D))
    ewg_h = hin("ewg", (E, D, II)); ewu_h = hin("ewu", (E, D, II)); ewd_h = hin("ewd", (E, II, D))
    outT_h = nc.dram_tensor("outT", [D, S], F32, kind="ExternalOutput").ap()
    outT_v = outT_h.rearrange("(c p) t -> p c t", p=128)

    P = {"t": [], "v": [], "s": [], "y": [], "g": []}

    class Sem:
        def __init__(self, h, step):
            self.h, self.step, self.n = h, step, 0

    es = ExitStack()
    S_T = Sem(es.enter_context(nc.semaphore("s_t")), 1)
    S_V = Sem(es.enter_context(nc.semaphore("s_v")), 1)
    S_S = Sem(es.enter_context(nc.semaphore("s_s")), 1)
    S_Y = Sem(es.enter_context(nc.semaphore("s_y")), 16)
    S_G = Sem(es.enter_context(nc.semaphore("s_g")), 16)
    S_C = Sem(es.enter_context(nc.semaphore("s_c")), 16)
    QSEM = {"t": S_T, "v": S_V, "s": S_S, "y": S_Y, "g": S_G}

    def emit(q, fn, wait=(), sem="auto"):
        s = QSEM[q] if sem == "auto" else sem
        w = tuple((sm.h, v) for (sm, v) in wait)
        def run(eng, fn=fn, w=w, sh=None if s is None else s.h,
                amt=0 if s is None else s.step):
            for h, v in w:
                eng.wait_ge(h, v)
            r = fn(eng)
            if sh is not None:
                r.then_inc(sh, amt)
        P[q].append(run)
        if s is not None:
            s.n += s.step

    MK = {}

    def vfill():
        emit("v", lambda v: v.tensor_copy(fillb[:, 1:2], fillb[:, 0:1]))

    sb = lambda name, shape, dt=F32: es.enter_context(nc.sbuf_tensor(name, list(shape), dt))
    psb = lambda name: es.enter_context(nc.psum_tensor(name, [128, 512], F32))

    strm = sb("strm", (128, DC, S))
    wg_sb = [sb(f"wgsb{i}", (128, DC, II), BF16) for i in range(2)]
    wu_sb = [sb(f"wusb{i}", (128, DC, II), BF16) for i in range(2)]
    wd_sb = [sb(f"wdsb{i}", (128, IC, D), BF16) for i in range(2)]
    h_sb = [sb(f"hsb{i}", (128, IC, GT), BF16) for i in range(2)]
    sx_sb = sb("sxsb", (128, IC, GT))
    wbc = sb("wbc", (128, S), BF16)
    w_hi = sb("w_hi", (16, S), BF16)
    w_lo = sb("w_lo", (16, S), BF16)
    o16 = sb("o16", (128, DC, 16))
    idf_s = sb("idf_s", (128, 128))
    idb_s = sb("idb_s", (128, 128), BF16)
    oh16_b = sb("oh16_b", (16, 16 * 128), BF16)
    inv1024_b = sb("inv1024_b", (1, 128), BF16)
    ones1_b = sb("ones1_b", (1, 128), BF16)
    mneg_b = sb("mneg_b", (1, 128), BF16)
    b1e30_b = sb("b1e30_b", (8, 128), BF16)
    bsel_b = sb("bsel_b", (8, 128), BF16)
    onescol_b = sb("onescol_b", (128, 1), BF16)
    hm2s_b = sb("hm2s_b", (128, 2), BF16)
    hm2T_b = sb("hm2T_b", (2, 128), BF16)
    selT_b = sb("selT_b", (128, 16), BF16)
    qb8_s = sb("qb8_s", (128, DC)); vb8_s = sb("vb8_s", (128, DC)); ob8_s = sb("ob8_s", (128, DC))
    lna8_s = sb("lna8_s", (128, DC)); lnb8_s = sb("lnb8_s", (128, DC))
    lqb1_s = sb("lqb1_s", (128, 1)); lkb1_s = sb("lkb1_s", (128, 1)); lvb1_s = sb("lvb1_s", (128, 1))
    lnc1_s = sb("lnc1_s", (128, 1)); lnd1_s = sb("lnd1_s", (128, 1))
    gwhi_s = sb("gwhi_s", (128, 128), BF16)
    gwlo_s = sb("gwlo_s", (128, 128), BF16)
    gb_s = sb("gb_s", (16, 1))
    cpool = sb("cpool", (128, 4))
    fillb = sb("fillb", (128, 2))
    nc.const_aps.register(F32, 0.0, cpool[:, 0:1])
    nc.const_aps.register(F32, EPS, cpool[:, 1:2])
    nc.const_aps.register(F32, LN8, cpool[:, 2:3])

    pg0, pg1, pu0, pu1 = psb("pg0"), psb("pg1"), psb("pu0"), psb("pu1")
    pd0, pd1, pd2, pd3 = psb("pd0"), psb("pd1"), psb("pd2"), psb("pd3")
    pg = [pg0, pg1]; pu = [pu0, pu1]; pd = [pd0, pd1, pd2, pd3]

    SA = ExitStack()
    sa = lambda name, shape, dt=F32: SA.enter_context(nc.sbuf_tensor(name, list(shape), dt))
    awT = sa("awT", (128, DC, D), BF16)
    x16b = sa("x16b", (128, DC, 128), BF16)
    sq16 = sa("sq16", (128, DC, 128), BF16)
    xab = sa("xab", (128, DC, 128), BF16)
    xbb = sa("xbb", (128, DC, 128), BF16)
    qf = sa("qf", (128, DC, 128))
    qcb = sa("qcb", (128, DC, 128), BF16)
    kvb = sa("kvb", (128, DC, 128), BF16)
    ascr = sa("ascr", (128, DC, 128))
    asb = sa("asb", (128, DC, 128), BF16)
    knb = sa("knb", (128, DC, 128), BF16)
    vit = sa("vit", (128, DC, 128), BF16)
    it0 = sa("it0", (128, DC, 128), BF16)
    it1 = sa("it1", (128, DC, 128), BF16)
    lbd_b = sa("lbd_b", (128, 3, 128), BF16)
    gwf_s = sa("gwf_s", (128, 128))
    pf2 = [sa(f"pf2{i}", (128, 128)) for i in range(4)]
    ptb = [sa(f"ptb{i}", (128, 128), BF16) for i in range(4)]
    ptb2 = [sa(f"ptb2{i}", (128, 128), BF16) for i in range(4)]
    lmaxn = [sa(f"lmaxn{i}", (128, 1)) for i in range(4)]
    rsum = [sa(f"rsum{i}", (128, 1)) for i in range(4)]
    rrec = [sa(f"rrec{i}", (128, 1)) for i in range(4)]
    ptob = [sa(f"ptob{i}", (128, 16), BF16) for i in range(2)]
    aob = sa("aob", (128, DC, 16), BF16)
    rowsA = sa("rowsA", (2, 1024))
    rowsB = sa("rowsB", (2, 1024), BF16)
    colscr = sa("colscr", (128, DC))
    dcol = sa("dcol", (128, 1))
    dcolb = sa("dcolb", (128, 1), BF16)
    m01s = sa("m01s", (1, 4))
    m01b = sa("m01b", (1, 1), BF16)
    m01c = sa("m01c", (128, 1))

    for dst, src in [(idb_s[:, :], idf_h[:, :]), (oh16_b[:, :], oh16_h[:, :]),
                     (inv1024_b[:, :], inv1024r_h[:, :]), (ones1_b[:, :], ones1r_h[:, :]),
                     (onescol_b[:, :], onescol_h[:, :]), (hm2s_b[:, :], hm2s_h[:, :]),
                     (hm2T_b[:, :], hm2T_h[:, :]), (selT_b[:, :], selT_h[:, :]),
                     (gwhi_s[:, :], gw128_h[:, :]),
                     (mneg_b[:, :], mneg_h[:, :]), (b1e30_b[:, :], b1e30_h[:, :]),
                     (bsel_b[:, :], bsel_h[:, :]),
                     (lbd_b[:, 0, :], lqbd_h[:, :]), (lbd_b[:, 1, :], lkbd_h[:, :]),
                     (lbd_b[:, 2, :], lvbd_h[:, :]),
                     (x16b[:, :, :], x16T_h.rearrange("(c p) t -> p c t", p=128))]:
        emit("g", lambda g, d=dst, s=src: g.dma_start(d, s))
    MK["g_consts"] = S_G.n
    emit("g", lambda g: g.dma_start(awT[:, :, 0:512], qwT_h[:, 0:512].rearrange("(c p) i -> p c i", p=128)))
    MK["g_awqh"] = S_G.n
    emit("g", lambda g: g.dma_start(awT[:, :, 512:1024], qwT_h[:, 512:1024].rearrange("(c p) i -> p c i", p=128)))
    MK["g_awq"] = S_G.n

    def emit_wload(e):
        par = e % 2
        if e == 0:
            srcs = [(wg_sb[0], swg_h), (wu_sb[0], swu_h), (wd_sb[0], swd_h)]
        else:
            srcs = [(wg_sb[par], ewg_h[e - 1]), (wu_sb[par], ewu_h[e - 1]),
                    (wd_sb[par], ewd_h[e - 1])]
        wt = [(S_T, MK[("emm", e - 2)])] if e >= 2 else []
        for k, (dst, src) in enumerate(srcs):
            emit("g", lambda g, d=dst, s=src: g.dma_start(
                d[:, :, :], s.rearrange("(c p) i -> p c i", p=128)),
                 wait=wt if k == 0 else ())
        MK[("wload", e)] = S_G.n

    for dst, src in [(idf_s[:, :], idf_h[:, :]),
                     (qb8_s[:, :], qb8_h[:, :]), (vb8_s[:, :], vb8_h[:, :]),
                     (ob8_s[:, :], ob8_h[:, :]), (lna8_s[:, :], lna8_h[:, :]),
                     (lnb8_s[:, :], lnb8_h[:, :]), (lqb1_s[:, :], lqb1_h[:, :]),
                     (lkb1_s[:, :], lkb1_h[:, :]), (lvb1_s[:, :], lvb1_h[:, :]),
                     (lnc1_s[:, :], lnc1_h[:, :]), (lnd1_s[:, :], lnd1_h[:, :]),
                     (gwf_s[:, :], gw128_h[:, :]), (gb_s[:, :], gb16_h[:, :])]:
        emit("y", lambda y, d=dst, s=src: y.dma_start(d, s))
    MK["y_consts"] = S_Y.n
    emit("y", lambda y: y.dma_start(strm[:, :, :], xT_h.rearrange("(c p) t -> p c t", p=128)))
    MK["y_xT"] = S_Y.n

    for j, val in [(0, 0.0), (1, EPS), (2, LN8), (3, 0.0)]:
        emit("v", lambda v, j=j, val=val: v.memset(cpool[:, j:j + 1], val))
    emit("v", lambda v: v.memset(fillb[:, 0:1], 0.0))
    emit("v", lambda v: v.tensor_tensor(gwlo_s[:, :], gwf_s[:, :], gwhi_s[:, :], OP.subtract),
         wait=[(S_Y, MK["y_consts"]), (S_G, MK["g_consts"])])
    MK["v_init"] = S_V.n

    def head_ln(src_f, dst_bf, wcol, use_ln8, tag):
        scap = [(S_S, S_S.n)]
        for c in range(DC):
            emit("v", lambda v, c=c: v.tensor_copy(asb[:, c, :], src_f[:, c, :]),
                 wait=scap if c == 0 else ())
        MK[(tag, "bf")] = S_V.n
        for c in range(DC):
            emit("s", lambda s, c=c: s.activation(sq16[:, c, :], src_f[:, c, :], AF.Square),
                 wait=[(S_V, MK[(tag, "bf")])] if c == 0 else ())
        MK[(tag, "sq")] = S_S.n
        for c in range(DC):
            dstp = pg[c // 4]
            emit("t", lambda t, c=c, dstp=dstp: t.matmul(
                dstp[0:2, (c % 4) * 128:(c % 4) * 128 + 128], hm2s_b[:, :], asb[:, c, :],
                start=True, stop=True),
                 wait=[(S_V, MK[(tag, "bf")])] if c == 0 else ())
        for c in range(DC):
            dstp = pd[2 + c // 4]
            emit("t", lambda t, c=c, dstp=dstp: t.matmul(
                dstp[0:2, (c % 4) * 128:(c % 4) * 128 + 128], hm2s_b[:, :], sq16[:, c, :],
                start=True, stop=True),
                 wait=[(S_S, MK[(tag, "sq")])] if c == 0 else ())
        MK[(tag, "stats")] = S_T.n
        for half in range(2):
            pm, pq = pg[half], pd[2 + half]
            wv = [(S_T, MK[(tag, "stats")])] if half == 0 else [(S_S, MK[(tag, "rstd", 0)])]
            emit("v", lambda v, pm=pm: v.tensor_copy(rowsA[0:2, 0:512], pm[0:2, 0:512]), wait=wv)
            emit("v", lambda v: v.tensor_tensor(
                rowsA[0:2, 512:1024], rowsA[0:2, 0:512], rowsA[0:2, 0:512], OP.mult))
            emit("v", lambda v, pq=pq: v.tensor_tensor(
                rowsA[0:2, 0:512], pq[0:2, 0:512], rowsA[0:2, 512:1024], OP.subtract))
            wm = [(S_T, MK[(tag, "bc", 0)])] if half == 1 else []
            emit("v", lambda v, pm=pm: v.tensor_copy(rowsB[0:2, 512:1024], pm[0:2, 0:512]),
                 wait=wm)
            MK[(tag, "var", half)] = S_V.n
            emit("s", lambda s: s.activation(rowsA[0:2, 512:1024], rowsA[0:2, 0:512],
                                             AF.Ln, bias=EPS, scale=1.0),
                 wait=[(S_V, MK[(tag, "var", half)])])
            bias = LN8 if use_ln8 else 0.0
            emit("s", lambda s, bias=bias: s.activation(
                rowsB[0:2, 0:512], rowsA[0:2, 512:1024], AF.Exp, bias=bias, scale=-0.5))
            MK[(tag, "rstd", half)] = S_S.n
            bm, br = (pu0, pu1) if half == 0 else (pd0, pd1)
            wb = [(S_S, MK[(tag, "rstd", half)]), (S_V, S_V.n)]
            for cc in range(4):
                emit("t", lambda t, cc=cc, bm=bm: t.matmul(
                    bm[:, cc * 128:cc * 128 + 128], hm2T_b[:, :],
                    rowsB[0:2, 512 + cc * 128:512 + cc * 128 + 128],
                    start=True, stop=True),
                     wait=wb if cc == 0 else ())
                emit("t", lambda t, cc=cc, br=br: t.matmul(
                    br[:, cc * 128:cc * 128 + 128], hm2T_b[:, :],
                    rowsB[0:2, cc * 128:cc * 128 + 128],
                    start=True, stop=True))
            MK[(tag, "bc", half)] = S_T.n
            for cc in range(4):
                c = half * 4 + cc
                emit("v", lambda v, c=c, cc=cc, bm=bm: v.tensor_tensor(
                    src_f[:, c, :], src_f[:, c, :], bm[:, cc * 128:cc * 128 + 128], OP.subtract),
                     wait=[(S_T, MK[(tag, "bc", half)])] if cc == 0 else ())
                emit("v", lambda v, c=c, cc=cc, br=br: v.scalar_tensor_tensor(
                    dst_bf[:, c, :], src_f[:, c, :], wcol[:, 0:1],
                    br[:, cc * 128:cc * 128 + 128], OP.mult, OP.mult))
            MK[(tag, "apply", half)] = S_V.n
        MK[(tag, "done")] = S_V.n

    for c in range(DC):
        emit("s", lambda s, c=c: s.activation(sq16[:, c, :], x16b[:, c, :], AF.Square),
             wait=[(S_G, MK["g_consts"])] if c == 0 else ())
    MK["x16sq"] = S_S.n
    for c in range(DC):
        emit("t", lambda t, c=c: t.matmul(pg0[0:1, 0:128], onescol_b[:, :], x16b[:, c, :],
                                          start=(c == 0), stop=(c == DC - 1)),
             wait=[(S_G, MK["g_consts"])] if c == 0 else ())
    for c in range(DC):
        emit("t", lambda t, c=c: t.matmul(pg0[32:33, 0:128], onescol_b[:, :], sq16[:, c, :],
                                          start=(c == 0), stop=(c == DC - 1)),
             wait=[(S_S, MK["x16sq"])] if c == 0 else ())
    MK["x16stats"] = S_T.n
    emit("v", lambda v: v.tensor_copy(rowsB[0:1, 0:128], pg0[0:1, 0:128]),
         wait=[(S_T, MK["x16stats"]), (S_V, MK["v_init"])])
    emit("v", lambda v: v.tensor_scalar(rowsA[0:1, 0:128], pg0[0:1, 0:128], 1.0 / D, None, OP.mult))
    emit("v", lambda v: v.tensor_tensor(rowsA[0:1, 128:256], rowsA[0:1, 0:128], rowsA[0:1, 0:128], OP.mult))
    emit("v", lambda v: v.tensor_scalar(rowsA[0:1, 256:384], pg0[32:33, 0:128], 1.0 / D, None, OP.mult))
    emit("v", lambda v: v.tensor_tensor(rowsA[0:1, 256:384], rowsA[0:1, 256:384], rowsA[0:1, 128:256], OP.subtract))
    MK["x16var"] = S_V.n
    emit("s", lambda s: s.activation(rowsA[0:1, 384:512], rowsA[0:1, 256:384], AF.Ln, bias=EPS, scale=1.0),
         wait=[(S_V, MK["x16var"])])
    emit("s", lambda s: s.activation(rowsB[0:1, 128:256], rowsA[0:1, 384:512], AF.Exp, bias=0.0, scale=-0.5))
    MK["x16rstd"] = S_S.n
    emit("t", lambda t: t.matmul(pg1[:, 0:128], inv1024_b[:, :], rowsB[0:1, 0:128], start=True, stop=True),
         wait=[(S_S, MK["x16rstd"])])
    emit("t", lambda t: t.matmul(pg1[:, 128:256], ones1_b[:, :], rowsB[0:1, 128:256], start=True, stop=True))
    MK["x16bc"] = S_T.n
    for c in range(DC):
        emit("v", lambda v, c=c: v.tensor_tensor(ascr[:, c, :], x16b[:, c, :], pg1[:, 0:128], OP.subtract),
             wait=[(S_T, MK["x16bc"])] if c == 0 else ())
        emit("v", lambda v, c=c: v.scalar_tensor_tensor(
            xab[:, c, :], ascr[:, c, :], lna8_s[:, c:c + 1], pg1[:, 128:256], OP.mult, OP.mult))
        emit("v", lambda v, c=c: v.scalar_tensor_tensor(
            xbb[:, c, :], ascr[:, c, :], lnb8_s[:, c:c + 1], pg1[:, 128:256], OP.mult, OP.mult))
    MK["xa"] = S_V.n
    MK["xb"] = S_V.n

    def proj1024(dstbank, tag, wt_first, wt_half=(), rhs=0):
        xr = xab if rhs == 0 else xbb
        for i in range(DC):
            for c in range(DC):
                bank, off = dstbank(i)
                wt = ()
                if i == 0 and c == 0:
                    wt = wt_first
                elif i == 4 and c == 0:
                    wt = wt_half
                emit("t", lambda t, i=i, c=c, bank=bank, off=off, xr=xr: t.matmul(
                    bank[:, off:off + 128], awT[:, c, i * 128:i * 128 + 128], xr[:, c, :],
                    start=(c == 0), stop=(c == DC - 1)),
                     wait=wt)
            MK[(tag, i)] = S_T.n
        MK[tag] = S_T.n

    proj1024(lambda i: (pg[i // 4], (i % 4) * 128), "qmm",
             [(S_G, MK["g_awqh"]), (S_V, MK["xa"])], [(S_G, MK["g_awq"])])
    emit("g", lambda g: g.dma_start(awT[:, :, 0:512], kwT_h[:, 0:512].rearrange("(c p) i -> p c i", p=128)),
         wait=[(S_T, MK["qmm"])])
    MK["g_awkh"] = S_G.n
    emit("g", lambda g: g.dma_start(awT[:, :, 512:1024], kwT_h[:, 512:1024].rearrange("(c p) i -> p c i", p=128)))
    MK["g_awk"] = S_G.n
    for i in range(DC):
        emit("s", lambda s, i=i: s.activation(
            qf[:, i, :], pg[i // 4][:, (i % 4) * 128:(i % 4) * 128 + 128], AF.Identity,
            bias=qb8_s[:, i:i + 1], scale=1.0),
             wait=[(S_T, MK[("qmm", (i // 4) * 4 + 3)])] if i % 4 == 0 else ())
        emit("s", lambda s, i=i: s.activation(qcb[:, i, :], qf[:, i, :], AF.Identity))
    MK["qevac"] = S_S.n
    proj1024(lambda i: (pu[i // 4], (i % 4) * 128), "kmm",
             [(S_G, MK["g_awkh"]), (S_S, MK["qevac"])], [(S_G, MK["g_awk"])], rhs=1)
    emit("g", lambda g: g.dma_start(awT[:, :, 0:512], vwT_h[:, 0:512].rearrange("(c p) i -> p c i", p=128)),
         wait=[(S_T, MK["kmm"])])
    MK["g_awvh"] = S_G.n
    emit("g", lambda g: g.dma_start(awT[:, :, 512:1024], vwT_h[:, 512:1024].rearrange("(c p) i -> p c i", p=128)))
    MK["g_awv"] = S_G.n
    for i in range(DC):
        emit("s", lambda s, i=i: s.activation(
            kvb[:, i, :], pu[i // 4][:, (i % 4) * 128:(i % 4) * 128 + 128], AF.Identity),
             wait=[(S_T, MK[("kmm", (i // 4) * 4 + 3)])] if i % 4 == 0 else ())
    MK["kevac"] = S_S.n
    for i in range(DC):
        emit("t", lambda t, i=i: t.matmul(
            pg[i // 4][:, (i % 4) * 128:(i % 4) * 128 + 128], lbd_b[:, 1, :], kvb[:, i, :],
            start=True, stop=True),
             wait=[(S_S, MK["kevac"])] if i == 0 else ())
    MK["kpre"] = S_T.n
    for i in range(DC):
        emit("s", lambda s, i=i: s.activation(
            ascr[:, i, :], pg[i // 4][:, (i % 4) * 128:(i % 4) * 128 + 128], AF.Identity,
            bias=lkb1_s[:, 0:1], scale=1.0),
             wait=[(S_T, MK["kpre"]), (S_V, S_V.n)] if i == 0 else ())
    MK["kpre_e"] = S_S.n
    head_ln(ascr, knb, lnd1_s, False, "klnh")

    proj1024(lambda i: (pd[2 + i // 4], (i % 4) * 128), "vmm",
             [(S_G, MK["g_awvh"]), (S_T, MK["kpre"]), (S_V, S_V.n)], [(S_G, MK["g_awv"])], rhs=1)
    emit("g", lambda g: g.dma_start(awT[:, :, 0:512], owT_h[:, 0:512].rearrange("(c p) i -> p c i", p=128)),
         wait=[(S_T, MK["vmm"])])
    MK["g_awoh"] = S_G.n
    emit("g", lambda g: g.dma_start(awT[:, :, 512:1024], owT_h[:, 512:1024].rearrange("(c p) i -> p c i", p=128)))
    MK["g_awo"] = S_G.n
    emit_wload(0)
    emit_wload(1)
    for i in range(DC):
        emit("s", lambda s, i=i: s.activation(
            kvb[:, i, :], pd[2 + i // 4][:, (i % 4) * 128:(i % 4) * 128 + 128], AF.Identity,
            bias=vb8_s[:, i:i + 1], scale=1.0),
             wait=[(S_T, MK[("vmm", (i // 4) * 4 + 3)])] if i % 4 == 0 else ())
    MK["vevac"] = S_S.n
    for i in range(DC):
        emit("t", lambda t, i=i: t.matmul(
            pg[i // 4][:, (i % 4) * 128:(i % 4) * 128 + 128], lbd_b[:, 2, :], kvb[:, i, :],
            start=True, stop=True),
             wait=[(S_S, MK["vevac"])] if i == 0 else ())
    MK["vpre"] = S_T.n
    for i in range(DC):
        emit("s", lambda s, i=i: s.activation(
            ascr[:, i, :], pg[i // 4][:, (i % 4) * 128:(i % 4) * 128 + 128], AF.Identity,
            bias=lvb1_s[:, 0:1], scale=1.0),
             wait=[(S_T, MK["vpre"]), (S_V, S_V.n)] if i == 0 else ())
    MK["vi_f"] = S_S.n
    for i in range(DC):
        emit("t", lambda t, i=i: t.transpose(
            pd[2 + i // 4][:, (i % 4) * 128:(i % 4) * 128 + 128], ascr[:, i, :], idf_s[:, :]),
             wait=[(S_S, MK["vi_f"]), (S_Y, MK["y_consts"])] if i == 0 else ())
    MK["vi_t"] = S_T.n
    for i in range(DC):
        emit("v", lambda v, i=i: v.tensor_copy(
            vit[:, i, :], pd[2 + i // 4][:, (i % 4) * 128:(i % 4) * 128 + 128]),
             wait=[(S_T, MK["vi_t"])] if i == 0 else ())
    MK["vit"] = S_V.n

    def attn_step(step):
        tg = ("st", step)
        for i in range(DC):
            wq = ([(S_S, MK[("qin", 0)])] if step == 0 else [(S_V, MK[("qin", step)])])
            emit("t", lambda t, i=i: t.matmul(
                pg[i // 4][:, (i % 4) * 128:(i % 4) * 128 + 128], lbd_b[:, 0, :], qcb[:, i, :],
                start=True, stop=True),
                 wait=wq if i == 0 else ())
        MK[(tg, "qpre")] = S_T.n
        for i in range(DC):
            emit("s", lambda s, i=i: s.activation(
                ascr[:, i, :], pg[i // 4][:, (i % 4) * 128:(i % 4) * 128 + 128], AF.Identity,
                bias=lqb1_s[:, 0:1], scale=1.0),
                 wait=[(S_T, MK[(tg, "qpre")]), (S_V, S_V.n)] if i == 0 else ())
        MK[(tg, "qpre_e")] = S_S.n
        head_ln(ascr, asb, lnc1_s, True, (tg, "qln"))
        for h in range(H):
            ic, half = h // 2, h % 2
            hp = h % 4
            sl = slice(half * 64, half * 64 + 64)
            lg = pd2[:, (h % 4) * 128:(h % 4) * 128 + 128]
            wt_l = [(S_V, MK[((tg, "qln"), "done")])] if h == 0 else []
            if h >= 4:
                if step < 2:
                    wt_l = wt_l + [(S_V, MK[(tg, "pevac", h - 4)])]
                else:
                    wt_l = wt_l + [(S_V, MK[(tg, "pn", h - 4)])]
            if h >= 4:
                wt_l = wt_l + [(S_S, MK[(tg, "exp", h - 4)])]
            emit("t", lambda t, lg=lg: t.matmul(
                lg, mneg_b[0:1, :], ones1_b[0:1, :], start=True, stop=False), wait=wt_l)
            emit("t", lambda t, lg=lg: t.matmul(
                lg, b1e30_b[:, :], bsel_b[:, :], start=False, stop=False))
            emit("t", lambda t, ic=ic, sl=sl, lg=lg: t.matmul(
                lg, asb[sl, ic, :], knb[sl, ic, :], start=False, stop=True))
            MK[(tg, "lg", h)] = S_T.n
            wt_m = [(S_T, MK[(tg, "lg", h)])]
            if h >= 4:
                wt_m.append((S_S, MK[(tg, "exp", h - 4)]))
            emit("v", lambda v, lg=lg, hp=hp: v.reduce_max(lmaxn[hp][:, 0:1], lg, axis=AX, negate=True),
                 wait=wt_m)
            MK[(tg, "pf", h)] = S_V.n
            wt_e = [(S_V, MK[(tg, "pf", h)])]
            if h >= 4:
                wt_e.append((S_V, MK[(tg, "pn", h - 4)]))
            emit("s", lambda s, hp=hp, lg=lg: s.activation(pf2[hp][:, :], lg, AF.Exp,
                                                    bias=lmaxn[hp][:, 0:1],
                                                    scale=1.0, accum_out=rsum[hp][:, 0:1]),
                 wait=wt_e)
            MK[(tg, "exp", h)] = S_S.n
            emit("v", lambda v, hp=hp: v.reciprocal(rrec[hp][:, 0:1], rsum[hp][:, 0:1]),
                 wait=[(S_S, MK[(tg, "exp", h)])])
            vfill()
            emit("v", lambda v, hp=hp: v.tensor_scalar(pf2[hp][:, :], pf2[hp][:, :], rrec[hp][:, 0:1], None, OP.mult))
            wt_c = [(S_T, MK[(tg, "pt" if step < 2 else "pto", h - 4)])] if h >= 4 else []
            emit("v", lambda v, hp=hp: v.tensor_copy(ptb[hp][:, :], pf2[hp][:, :]), wait=wt_c)
            MK[(tg, "pn", h)] = S_V.n
            if step < 2:
                ptp = pd3[:, (h % 4) * 128:(h % 4) * 128 + 128]
                emit("t", lambda t, ptp=ptp, hp=hp: t.matmul(ptp, ptb[hp][:, :], idb_s[:, :], start=True, stop=True),
                     wait=[(S_V, MK[(tg, "pn", h)])])
                MK[(tg, "pt", h)] = S_T.n
                wt_v = [(S_T, MK[(tg, "pt", h)])]
                if h >= 4:
                    wt_v.append((S_T, MK[(tg, "at", h - 4)]))
                emit("v", lambda v, ptp=ptp, hp=hp: v.tensor_copy(ptb2[hp][:, :], ptp),
                     wait=wt_v)
                MK[(tg, "pevac", h)] = S_V.n
                emit("t", lambda t, sl=sl, ic=ic, hp=hp: t.matmul(
                    pg[ic // 4][sl, (ic % 4) * 128:(ic % 4) * 128 + 128],
                    vit[:, ic, sl], ptb2[hp][:, :], start=True, stop=True),
                     wait=[(S_V, MK[(tg, "pevac", h)])])
                MK[(tg, "at", h)] = S_T.n
            else:
                emit("t", lambda t, h=h, hp=hp: t.matmul(pd3[:, h * 16:h * 16 + 16], ptb[hp][:, :],
                                                  selT_b[:, :], start=True, stop=True),
                     wait=[(S_V, MK[(tg, "pn", h)])])
                MK[(tg, "pto", h)] = S_T.n
                pob = ptob[h % 2]
                wt_p = [(S_T, MK[(tg, "pto", h)])]
                if h >= 2:
                    wt_p.append((S_T, MK[(tg, "aow", h - 2)]))
                emit("v", lambda v, pob=pob, h=h: v.tensor_copy(pob[:, :], pd3[:, h * 16:h * 16 + 16]),
                     wait=wt_p)
                MK[(tg, "ptob", h)] = S_V.n
                emit("t", lambda t, sl=sl, ic=ic, pob=pob: t.matmul(
                    pg0[sl, ic * 16:ic * 16 + 16], vit[:, ic, sl], pob[:, :], start=True, stop=True),
                     wait=[(S_V, MK[(tg, "ptob", h)])])
                MK[(tg, "aow", h)] = S_T.n
                if half == 1:
                    emit("v", lambda v, ic=ic: v.tensor_copy(aob[:, ic, :], pg0[:, ic * 16:ic * 16 + 16]),
                         wait=[(S_T, MK[(tg, "aow", h)])])
                    MK[(tg, "aoev", ic)] = S_V.n
        if step < 2:
            dst = it0 if step == 0 else it1
            for i in range(DC):
                emit("v", lambda v, i=i, dst=dst: v.tensor_copy(
                    dst[:, i, :], pg[i // 4][:, (i % 4) * 128:(i % 4) * 128 + 128]),
                     wait=[(S_T, MK[(tg, "at", 2 * i + 1)])])
            MK[(tg, "itev")] = S_V.n

    MK[("qin", 0)] = MK["qevac"]
    attn_step(0)
    for i in range(DC):
        emit("v", lambda v, i=i: v.tensor_tensor(qf[:, i, :], qf[:, i, :], it0[:, i, :], OP.add))
        emit("v", lambda v, i=i: v.tensor_copy(qcb[:, i, :], qf[:, i, :]))
    MK[("qin", 1)] = S_V.n
    attn_step(1)
    for i in range(DC):
        emit("v", lambda v, i=i: v.tensor_tensor(ascr[:, i, :], it1[:, i, :], it0[:, i, :], OP.subtract))
        emit("v", lambda v, i=i: v.reduce_sum(colscr[:, i:i + 1], ascr[:, i, :], axis=AX,
                                              apply_absolute_value=True))
    vfill()
    emit("v", lambda v: v.reduce_sum(dcol[:, 0:1], colscr[:, :], axis=AX))
    vfill()
    emit("v", lambda v: v.tensor_copy(dcolb[:, 0:1], dcol[:, 0:1]))
    MK["dcol"] = S_V.n
    emit("t", lambda t: t.matmul(pd3[0:1, 0:1], onescol_b[:, :], dcolb[:, 0:1], start=True, stop=True),
         wait=[(S_V, MK["dcol"])])
    MK["dsum"] = S_T.n
    emit("v", lambda v: v.tensor_scalar(m01s[0:1, 0:1], pd3[0:1, 0:1], TH_C, None, OP.is_ge),
         wait=[(S_T, MK["dsum"])])
    vfill()
    emit("v", lambda v: v.tensor_copy(m01b[0:1, 0:1], m01s[0:1, 0:1]))
    MK["m01b"] = S_V.n
    emit("t", lambda t: t.matmul(pd3[:, 4:5], ones1_b[:, :], m01b[0:1, 0:1], start=True, stop=True),
         wait=[(S_V, MK["m01b"])])
    MK["m01bc"] = S_T.n
    emit("v", lambda v: v.tensor_copy(m01c[:, 0:1], pd3[:, 4:5]),
         wait=[(S_T, MK["m01bc"])])
    vfill()
    for i in range(DC):
        emit("v", lambda v, i=i: v.scalar_tensor_tensor(
            it1[:, i, :], it1[:, i, :], m01c[:, 0:1], qf[:, i, :], OP.mult, OP.add))
        emit("v", lambda v, i=i: v.tensor_copy(qcb[:, i, :], it1[:, i, :]))
    MK[("qin", 2)] = S_V.n
    attn_step(2)

    for i in range(DC):
        for c in range(DC):
            wt = []
            if i == 0 and c == 0:
                wt = [(S_G, MK["g_awoh"]), (S_V, MK[(("st", 2), "aoev", DC - 1)])]
            elif i == 4 and c == 0:
                wt = [(S_G, MK["g_awo"])]
            emit("t", lambda t, i=i, c=c: t.matmul(
                pd2[:, i * 16:i * 16 + 16], awT[:, c, i * 128:i * 128 + 128], aob[:, c, :],
                start=(c == 0), stop=(c == DC - 1)), wait=wt)
    MK["oproj"] = S_T.n
    for i in range(DC):
        emit("v", lambda v, i=i: v.tensor_copy(o16[:, i, :], pd2[:, i * 16:i * 16 + 16]),
             wait=[(S_T, MK["oproj"])] if i == 0 else ())
    MK["o16ev"] = S_V.n

    SA.close()
    sp = lambda name, shape, dt=F32: es.enter_context(nc.sbuf_tensor(name, list(shape), dt))
    xp_bf = sp("xp_bf", (128, DC, S), BF16)
    sq_bf = sp("sq_bf", (128, DC, LNT), BF16)
    rl_s2 = [sp(f"rl_s{i}", (16, LNT)) for i in range(2)]
    rowsL = sp("rowsL", (1, 1536))
    rowsLb = sp("rowsLb", (1, 1024), BF16)
    rlt_s = sp("rlt_s", (128, 16))
    rls_t = sp("rls_t", (128, 16))
    xlt = sp("xlt", (128, LNT))
    eqs = sp("eqs", (128, 16))
    mx1 = sp("mx1", (128, 1))
    mx2 = sp("mx2", (128, 1))
    rc_ = sp("rc_", (128, 1))

    def stream_ln(tile, with_xp, tag):
        t0 = tile * LNT
        tcap = [(S_T, S_T.n), (S_S, MK["obadd"])]
        for c in range(DC):
            emit("v", lambda v, c=c, t0=t0: v.tensor_copy(xp_bf[:, c, t0:t0 + LNT], strm[:, c, t0:t0 + LNT]),
                 wait=tcap if c == 0 else ())
        MK[(tag, "cast")] = S_V.n
        for c in range(DC):
            emit("s", lambda s, c=c, t0=t0: s.activation(sq_bf[:, c, :], strm[:, c, t0:t0 + LNT], AF.Square),
                 wait=(tcap + [(S_V, MK[(tag, "cast")])]) if c == 0 else ())
        MK[(tag, "sq")] = S_S.n
        for c in range(DC):
            emit("t", lambda t, c=c, t0=t0: t.matmul(pd0[0:1, 0:512], onescol_b[:, :], xp_bf[:, c, t0:t0 + LNT],
                                              start=(c == 0), stop=(c == DC - 1)),
                 wait=[(S_V, MK[(tag, "cast")])] if c == 0 else ())
        for c in range(DC):
            emit("t", lambda t, c=c: t.matmul(pd0[32:33, 0:512], onescol_b[:, :], sq_bf[:, c, :],
                                              start=(c == 0), stop=(c == DC - 1)),
                 wait=[(S_S, MK[(tag, "sq")])] if c == 0 else ())
        MK[(tag, "stats")] = S_T.n
        emit("v", lambda v: v.tensor_copy(rowsLb[0:1, 0:512], pd0[0:1, 0:512]),
             wait=[(S_T, MK[(tag, "stats")])])
        emit("v", lambda v: v.tensor_scalar(rowsL[0:1, 0:512], pd0[0:1, 0:512], 1.0 / D, None, OP.mult))
        emit("v", lambda v: v.tensor_tensor(rowsL[0:1, 512:1024], rowsL[0:1, 0:512], rowsL[0:1, 0:512], OP.mult))
        emit("v", lambda v: v.tensor_scalar(rowsL[0:1, 1024:1536], pd0[32:33, 0:512], 1.0 / D, None, OP.mult))
        emit("v", lambda v: v.tensor_tensor(rowsL[0:1, 1024:1536], rowsL[0:1, 1024:1536],
                                            rowsL[0:1, 512:1024], OP.subtract))
        MK[(tag, "var")] = S_V.n
        emit("s", lambda s: s.activation(rowsL[0:1, 512:1024], rowsL[0:1, 1024:1536], AF.Ln, bias=EPS, scale=1.0),
             wait=[(S_V, MK[(tag, "var")])])
        emit("s", lambda s: s.activation(rowsLb[0:1, 512:1024], rowsL[0:1, 512:1024], AF.Exp, bias=0.0, scale=-0.5))
        MK[(tag, "rstd")] = S_S.n
        emit("t", lambda t: t.matmul(pd1[:, 0:512], inv1024_b[:, :], rowsLb[0:1, 0:512], start=True, stop=True),
             wait=[(S_S, MK[(tag, "rstd")])])
        MK[(tag, "bcm")] = S_T.n
        for c in range(DC):
            emit("v", lambda v, c=c, t0=t0: v.tensor_tensor(
                strm[:, c, t0:t0 + LNT], strm[:, c, t0:t0 + LNT], pd1[:, 0:512], OP.subtract),
                 wait=[(S_T, MK[(tag, "bcm")])] if c == 0 else ())
        MK[(tag, "sub")] = S_V.n
        emit("t", lambda t: t.matmul(pd1[:, 0:512], ones1_b[:, :], rowsLb[0:1, 512:1024], start=True, stop=True),
             wait=[(S_V, MK[(tag, "sub")])])
        MK[(tag, "bcr")] = S_T.n
        if with_xp:
            for c in range(DC):
                emit("v", lambda v, c=c, t0=t0: v.tensor_tensor(
                    xp_bf[:, c, t0:t0 + LNT], strm[:, c, t0:t0 + LNT], pd1[:, 0:512], OP.mult),
                     wait=[(S_T, MK[(tag, "bcr")])] if c == 0 else ())
            MK[(tag, "xpw")] = S_V.n
            emit("v", lambda v: v.tensor_scalar(rowsLb[0:1, 0:512], rowsLb[0:1, 512:1024],
                                                2.0, None, OP.mult),
                 wait=[(S_T, MK[(tag, "bcm")])])
            MK[(tag, "r2")] = S_V.n
            emit("t", lambda t: t.matmul(pd1[:, 0:512], ones1_b[:, :], rowsLb[0:1, 0:512],
                                         start=True, stop=True),
                 wait=[(S_V, MK[(tag, "r2")]), (S_V, MK[(tag, "xpw")])]),
            MK.__setitem__((tag, "bcr2"), S_T.n)
            for c in range(DC):
                emit("v", lambda v, c=c, t0=t0: v.tensor_tensor(
                    strm[:, c, t0:t0 + LNT], strm[:, c, t0:t0 + LNT], pd1[:, 0:512], OP.mult),
                     wait=[(S_T, MK[(tag, "bcr2")])] if c == 0 else ())
        else:
            for c in range(DC):
                emit("v", lambda v, c=c, t0=t0: v.tensor_tensor(
                    strm[:, c, t0:t0 + LNT], strm[:, c, t0:t0 + LNT], pd1[:, 0:512], OP.mult),
                     wait=[(S_T, MK[(tag, "bcr")])] if c == 0 else ())
        MK[(tag, "done")] = S_V.n

    def emit_xlo(lt, wt_extra=()):
        t0 = lt * LNT
        wt = [(S_V, MK[(("xln", lt), "done")]), (S_T, S_T.n)] + list(wt_extra)
        for c in range(DC):
            emit("v", lambda v, c=c, t0=t0: v.tensor_scalar(
                xlt[:, :], strm[:, c, t0:t0 + LNT], 0.5, None, OP.mult),
                 wait=wt if c == 0 else ())
            emit("v", lambda v, c=c, t0=t0: v.tensor_tensor(
                sq_bf[:, c, :], xlt[:, :], xp_bf[:, c, t0:t0 + LNT], OP.subtract))
        MK[("xlo", lt)] = S_V.n

    def emit_rlog(lt, plg, wt_extra=()):
        t0 = lt * LNT
        for c in range(DC):
            wt = ([(S_V, MK[("xlo", lt)]), (S_V, S_V.n)] + list(wt_extra)) if c == 0 else []
            emit("t", lambda t, c=c, t0=t0, plg=plg: t.matmul(
                plg[0:16, 0:512], gwhi_s[:, c * 16:c * 16 + 16], xp_bf[:, c, t0:t0 + LNT],
                start=(c == 0), stop=False), wait=wt)
            emit("t", lambda t, c=c, t0=t0, plg=plg: t.matmul(
                plg[0:16, 0:512], gwlo_s[:, c * 16:c * 16 + 16], xp_bf[:, c, t0:t0 + LNT],
                start=False, stop=False))
            emit("t", lambda t, c=c, t0=t0, plg=plg: t.matmul(
                plg[0:16, 0:512], gwhi_s[:, c * 16:c * 16 + 16], sq_bf[:, c, :],
                start=False, stop=(c == DC - 1)))
        MK[("rlog", lt)] = S_T.n
        rls = rl_s2[lt % 2]
        emit("v", lambda v, plg=plg, rls=rls: v.tensor_scalar(
            rls[0:16, 0:512], plg[0:16, 0:512], gb_s[0:16, 0:1], None, OP.add),
             wait=[(S_T, MK[("rlog", lt)]), (S_T, S_T.n)])
        MK[("sig", lt)] = S_V.n

    top2_prev = {}

    def emit_top2(lt, tp_bank, tr_bank, wt_extra=()):
        for kk in range(4):
            k = lt * 4 + kk
            wt = [(S_V, MK[("sig", lt)])] + (list(wt_extra) if kk == 0 else [])
            if kk in top2_prev:
                wt.append((S_V, MK[("wlo", top2_prev[kk])]))
            top2_prev[kk] = k
            emit("t", lambda t, kk=kk, tp_bank=tp_bank, rls=rl_s2[lt % 2]: t.transpose(
                tp_bank[:, kk * 16:kk * 16 + 16], rls[0:16, kk * 128:kk * 128 + 128],
                idf_s[0:16, 0:16]), wait=wt)
            MK[("rlt", k)] = S_T.n
            emit("s", lambda s, kk=kk, tp_bank=tp_bank: s.activation(
                rls_t[:, :], tp_bank[:, kk * 16:kk * 16 + 16], AF.Sigmoid),
                 wait=[(S_T, MK[("rlt", k)])])
            MK[("sigt", k)] = S_S.n
            emit("v", lambda v, kk=kk, tp_bank=tp_bank: v.tensor_copy(
                rlt_s[:, :], tp_bank[:, kk * 16:kk * 16 + 16]),
                 wait=[(S_T, MK[("rlt", k)]), (S_S, MK[("sigt", k)])])
            vfill()
            emit("v", lambda v: v.reduce_max(mx1[:, 0:1], rlt_s[:, :], axis=AX))
            vfill()
            emit("v", lambda v: v.tensor_scalar(eqs[:, :], rlt_s[:, :], mx1[:, 0:1], None, OP.is_ge))
            vfill()
            emit("v", lambda v: v.tensor_scalar(eqs[:, :], eqs[:, :], NEG_BIG, None, OP.mult))
            vfill()
            emit("v", lambda v: v.tensor_tensor(eqs[:, :], eqs[:, :], rlt_s[:, :], OP.add))
            vfill()
            emit("v", lambda v: v.reduce_max(mx2[:, 0:1], eqs[:, :], axis=AX))
            vfill()
            emit("v", lambda v: v.tensor_scalar(eqs[:, :], rlt_s[:, :], mx2[:, 0:1], None, OP.is_ge))
            vfill()
            emit("v", lambda v: v.tensor_tensor(rlt_s[:, :], rls_t[:, :], eqs[:, :], OP.mult))
            vfill()
            emit("v", lambda v: v.reduce_sum(mx1[:, 0:1], rlt_s[:, :], axis=AX))
            vfill()
            emit("v", lambda v: v.reciprocal(rc_[:, 0:1], mx1[:, 0:1]))
            vfill()
            emit("v", lambda v: v.tensor_scalar(rlt_s[:, :], rlt_s[:, :], rc_[:, 0:1], None, OP.mult))
            MK[("wtok", k)] = S_V.n
            emit("t", lambda t, kk=kk, tr_bank=tr_bank: t.transpose(
                tr_bank[0:16, kk * 128:kk * 128 + 128], rlt_s[:, :], idf_s[:, :]),
                 wait=[(S_V, MK[("wtok", k)])])
            MK[("wtr", k)] = S_T.n
            emit("v", lambda v, k=k, kk=kk, tr_bank=tr_bank: v.tensor_copy(
                w_hi[0:16, k * 128:k * 128 + 128], tr_bank[0:16, kk * 128:kk * 128 + 128]),
                 wait=[(S_T, MK[("wtr", k)])])
            emit("v", lambda v, k=k, kk=kk, tr_bank=tr_bank: v.tensor_tensor(
                w_lo[0:16, k * 128:k * 128 + 128], tr_bank[0:16, kk * 128:kk * 128 + 128],
                w_hi[0:16, k * 128:k * 128 + 128], OP.subtract))
            MK[("wlo", k)] = S_V.n

    for c in range(DC):
        emit("s", lambda s, c=c: s.activation(
            strm[:, c, :], strm[:, c, :], AF.Identity, bias=ob8_s[:, c:c + 1], scale=1.0),
             wait=[(S_Y, MK["y_xT"]), (S_Y, MK["y_consts"])] if c == 0 else ())
    MK["obadd"] = S_S.n
    stream_ln(1, True, ("xln", 1))
    emit_xlo(1)
    emit_rlog(1, pd1)
    stream_ln(2, True, ("xln", 2))
    emit_xlo(2)
    emit_rlog(2, pd1)

    ABANK = [pg0, pg1, pu0, pu1]
    BBANK = [pd0, pd1, pd2, pd3]

    def ptile(e, j, k):
        idx = e * NT + j - k
        return idx // NT, idx % NT

    def emit_gu(e, j):
        par = e % 2
        t0 = GTO[j] * GT
        lt = GTO[j]
        for i in range(IC):
            hgb = (ABANK[i] if i < 2 else BBANK[i - 2])
            hub = (ABANK[2 + i] if i < 2 else BBANK[2 + (i - 2)])
            wt = []
            if i == 0:
                if j == 0:
                    wt.append((S_G, MK[("wload", e)]))
                if e == 0:
                    wt.append((S_V, MK[(("xln", lt), "done")]))
                if e * NT + j >= 1:
                    p1e, p1j = ptile(e, j, 1)
                    wt.append((S_S, MK[("silu", p1e, p1j, 1)]))
                    wt.append((S_V, MK[("h1", p1e, p1j, 1)]))
                if e * NT + j >= 2:
                    pe, pj = ptile(e, j, 2)
                    wt.append((S_V, MK[("drain", pe, pj, 0)]))
            if i == 2:
                if e == 0 and j <= 1:
                    wt.append((S_V, S_V.n))
                if e * NT + j >= 1:
                    p1e, p1j = ptile(e, j, 1)
                    wt.append((S_S, MK[("silu", p1e, p1j, 3)]))
                    wt.append((S_V, MK[("h1", p1e, p1j, 3)]))
                if e * NT + j >= 2:
                    pe, pj = ptile(e, j, 2)
                    wt.append((S_V, MK[("drain", pe, pj, 1)]))
                if j == 1 and e >= 1:
                    wt.append((S_V, MK[("wbcev", e)]))
                if extra_wd_wait:
                    wt.extend(extra_wd_wait)
                    del extra_wd_wait[:]
            for c in range(DC):
                emit("t", lambda t, i=i, c=c, par=par, t0=t0, hgb=hgb: t.matmul(
                    hgb[:, 0:GT], wg_sb[par][:, c, i * 128:i * 128 + 128], xp_bf[:, c, t0:t0 + GT],
                    start=(c == 0), stop=(c == DC - 1)),
                     wait=wt if c == 0 else ())
            for c in range(DC):
                emit("t", lambda t, i=i, c=c, par=par, t0=t0, hub=hub: t.matmul(
                    hub[:, 0:GT], wu_sb[par][:, c, i * 128:i * 128 + 128], xp_bf[:, c, t0:t0 + GT],
                    start=(c == 0), stop=(c == DC - 1)))
            MK[("gu", e, j, i)] = S_T.n

    extra_wd_wait_DUMMY = []
    extra_wd_wait = []

    def emit_wd(e, j, sub, cur, extra=()):
        par = e % 2
        hp = j % 2
        t0 = GTO[j] * GT + sub * TT
        bs = ABANK if sub == 0 else BBANK
        wt0 = [(S_V, MK[("hb", e, j, IC - 1)])] + list(extra)
        ce, cj = cur
        wt0.append((S_S, MK[("silu", ce, cj, 1 if sub == 0 else 3)]))
        wt0.append((S_V, MK[("h1", ce, cj, 1 if sub == 0 else 3)]))
        for c in range(DC):
            dstp = bs[c // 2][:, (c % 2) * TT:(c % 2) * TT + TT]
            for i in range(IC):
                emit("t", lambda t, i=i, c=c, par=par, hp=hp, sub=sub, dstp=dstp: t.matmul(
                    dstp, wd_sb[par][:, i, c * 128:c * 128 + 128],
                    h_sb[hp][:, i, sub * TT:sub * TT + TT],
                    start=(i == 0), stop=(i == IC - 1)),
                     wait=wt0 if (i == 0 and c == 0) else ())
            MK[("wdmm", e, j, sub, c)] = S_T.n
        MK[("wdmm", e, j, sub)] = S_T.n

    def emit_evac_sh(e, j):
        for i in range(IC):
            hgb = (ABANK[i] if i < 2 else BBANK[i - 2])
            hub = (ABANK[2 + i] if i < 2 else BBANK[2 + (i - 2)])
            wt = [(S_T, MK[("gu", e, j, i)])]
            if e * NT + j >= 1:
                pe, pj = ptile(e, j, 1)
                wt.append((S_V, MK[("hb", pe, pj, i)]))
            emit("s", lambda s, i=i, hgb=hgb: s.activation(
                sx_sb[:, i, :], hgb[:, 0:GT], AF.Silu), wait=wt)
            MK[("silu", e, j, i)] = S_S.n
            emit("v", lambda v, i=i, hub=hub: v.tensor_tensor(
                sx_sb[:, i, :], sx_sb[:, i, :], hub[:, 0:GT], OP.mult),
                 wait=[(S_S, MK[("silu", e, j, i)])])
            MK[("h1", e, j, i)] = S_V.n

    def emit_evac_hb(e, j):
        hp = j % 2
        t0 = GTO[j] * GT
        for i in range(IC):
            wh = []
            if e * NT + j >= 2:
                pe, pj = ptile(e, j, 2)
                wh.append((S_T, MK[("wdmm", pe, pj, 1)]))
            if e == 0:
                emit("v", lambda v, i=i, hp=hp: v.tensor_copy(h_sb[hp][:, i, :], sx_sb[:, i, :]),
                     wait=wh)
            else:
                emit("v", lambda v, i=i, hp=hp, t0=t0: v.tensor_tensor(
                    h_sb[hp][:, i, :], sx_sb[:, i, :], wbc[:, t0:t0 + GT], OP.mult),
                     wait=wh)
            MK[("hb", e, j, i)] = S_V.n

    def emit_drain(e, j, sub):
        t0 = GTO[j] * GT + sub * TT
        bs = ABANK if sub == 0 else BBANK
        for c in range(DC):
            srcp = bs[c // 2][:, (c % 2) * TT:(c % 2) * TT + TT]
            emit("v", lambda v, c=c, t0=t0, srcp=srcp: v.tensor_tensor(
                strm[:, c, t0:t0 + TT], strm[:, c, t0:t0 + TT], srcp, OP.add),
                 wait=[(S_T, MK[("wdmm", e, j, sub, c | 1)])])
        MK[("drain", e, j, sub)] = S_V.n

    def emit_wbc(e):
        ei = e - 1
        wt = [(S_V, MK[("drain", e - 1, NT - 1, 1)]), (S_V, MK["router_all"])]
        for r in range(4):
            t0 = r * LNT
            emit("t", lambda t, ei=ei, t0=t0, r=r: t.matmul(
                pd[r][:, 0:512], oh16_b[0:16, ei * 128:ei * 128 + 128], w_hi[0:16, t0:t0 + LNT],
                start=True, stop=False), wait=wt if r == 0 else ())
            emit("t", lambda t, ei=ei, t0=t0, r=r: t.matmul(
                pd[r][:, 0:512], oh16_b[0:16, ei * 128:ei * 128 + 128], w_lo[0:16, t0:t0 + LNT],
                start=False, stop=True))
            MK[("wbcmm", e, r)] = S_T.n
        for r in range(4):
            t0 = r * LNT
            emit("v", lambda v, t0=t0, r=r: v.tensor_copy(wbc[:, t0:t0 + LNT], pd[r][:, 0:512]),
                 wait=[(S_T, MK[("wbcmm", e, r)])])
        MK[("wbcev", e)] = S_V.n

    def hook_a():
        emit_top2(1, pd0, pd1, wt_extra=[(S_V, S_V.n)])
        stream_ln(3, True, ("xln", 3))
        emit_xlo(3)
        emit_rlog(3, pd1)
        extra_wd_wait.append((S_V, MK[("wlo", 7)]))
        extra_wd_wait.append((S_V, MK[(("xln", 3), "done")]))

    def hook_xln0_inner():
        for c in range(DC):
            emit("v", lambda v, c=c: v.tensor_tensor(
                strm[:, c, 0:16], strm[:, c, 0:16], o16[:, c, :], OP.add))
        MK["ownadd"] = S_V.n
        stream_ln(0, True, ("xln", 0))
        extra_wd_wait.append((S_V, MK[(("xln", 0), "done")]))

    def hook_b():
        emit_top2(2, pd0, pd1)
        hook_xln0_inner()
        emit_xlo(0)
        emit_rlog(0, pd0, wt_extra=[(S_V, MK[(("xln", 0), "done")])])
        extra_wd_wait.append((S_V, MK[("wlo", 11)]))
        extra_wd_wait.append((S_V, MK[("sig", 0)]))

    def hook_c():
        emit_top2(3, pd0, pd1)
        emit_top2(0, pd0, pd1)
        MK["router_all"] = S_V.n
        extra_wd_wait.append((S_V, MK["router_all"]))

    ndma = [0]

    def emit_fln(lt):
        stream_ln(lt, False, ("fln", lt))
        t0 = lt * LNT
        for c in range(DC):
            wt = [(S_V, MK[(("fln", lt), "done")])] if c < 2 else []
            if ndma[0] % 2 == 0:
                emit("y", lambda y, c=c, t0=t0: y.dma_start(
                    outT_h[c * 128:(c + 1) * 128, t0:t0 + LNT], strm[:, c, t0:t0 + LNT]), wait=wt)
            else:
                emit("s", lambda s, c=c, t0=t0: s.dma_start(
                    outT_h[c * 128:(c + 1) * 128, t0:t0 + LNT], strm[:, c, t0:t0 + LNT]), wait=wt, sem=S_C)
            ndma[0] += 1

    def hook_fln(lt):
        def h():
            emit_fln(lt)
            extra_wd_wait.append((S_V, MK[(("fln", lt), "done")]))
        return h

    hooks = {(0, 0): hook_a, (0, 1): hook_b, (0, 2): hook_c,
             (NEXP - 1, 0): hook_fln(1), (NEXP - 1, 1): hook_fln(2),
             (NEXP - 1, 2): hook_fln(3)}

    prev = None
    for e in range(NEXP):
        for j in range(NT):
            emit_gu(e, j)
            emit_evac_sh(e, j)
            if prev is not None:
                pe, pj = prev
                emit_wd(pe, pj, 0, cur=(e, j))
                emit_drain(pe, pj, 0)
                emit_wd(pe, pj, 1, cur=(e, j))
                emit_drain(pe, pj, 1)
                if pj == NT - 1:
                    MK[("emm", pe)] = S_T.n
                    if pe + 1 < NEXP:
                        emit_wbc(pe + 1)
                    if pe + 2 < NEXP:
                        emit_wload(pe + 2)
                if prev in hooks:
                    hooks[prev]()
            emit_evac_hb(e, j)
            prev = (e, j)
    fln_gate = list(extra_wd_wait)
    del extra_wd_wait[:]
    emit_wd(NEXP - 1, NT - 1, 0, cur=(NEXP - 1, NT - 1), extra=fln_gate)
    emit_drain(NEXP - 1, NT - 1, 0)
    emit_wd(NEXP - 1, NT - 1, 1, cur=(NEXP - 1, NT - 1))
    emit_drain(NEXP - 1, NT - 1, 1)
    MK[("emm", NEXP - 1)] = S_T.n
    emit_fln(0)

    if DEBUG:
        dbg_y_h = nc.dram_tensor("dbg_y", [128, DC, S], F32, kind="ExternalOutput").ap()
        dbg_xp_h = nc.dram_tensor("dbg_xp", [128, DC, S], BF16, kind="ExternalOutput").ap()
        emit("y", lambda y: y.dma_start(dbg_y_h, strm[:, :, :]),
             wait=[(S_V, MK[("drain", NEXP - 1, NT - 1, 1)])])
        emit("y", lambda y: y.dma_start(dbg_xp_h, xp_bf[:, :, :]))
        MK["dbgdma"] = S_Y.n
    emit("v", lambda v: None, wait=[(S_Y, S_Y.n), (S_C, S_C.n)], sem=None)

    with es, nc.Block() as block:
        @block.tensor
        def _(t):
            for f in P["t"]:
                f(t)

        @block.vector
        def _(v):
            for f in P["v"]:
                f(v)

        @block.scalar
        def _(s):
            for f in P["s"]:
                f(s)

        @block.sync
        def _(y):
            for f in P["y"]:
                f(y)

        @block.gpsimd
        def _(g):
            for f in P["g"]:
                f(g)

    return nc


def kernel(**inputs):
    gi = lambda k: np.ascontiguousarray(np.asarray(inputs[k], dtype=np.float32))
    x = gi("L0_attn_x")
    qw, kw, vw, ow = gi("L0_attn_qw"), gi("L0_attn_kw"), gi("L0_attn_vw"), gi("L0_attn_ow")
    qb, vb, ob = gi("L0_attn_qb"), gi("L0_attn_vb"), gi("L0_attn_ob")
    lna, lnb = gi("L0_attn_lna_w"), gi("L0_attn_lnb_w")
    lnc, lnd = gi("L0_attn_lnc_w"), gi("L0_attn_lnd_w")
    lqw, lqb = gi("L0_attn_lqw"), gi("L0_attn_lqb")
    lkw, lkb = gi("L0_attn_lkw"), gi("L0_attn_lkb")
    lvw, lvb = gi("L0_attn_lvw"), gi("L0_attn_lvb")
    gw, gb = gi("L0_ffn_gate_w"), gi("L0_ffn_gate_b")
    swg, swu, swd = gi("L0_ffn_shared_wg"), gi("L0_ffn_shared_wu"), gi("L0_ffn_shared_wd")
    ewg, ewu, ewd = gi("L0_ffn_experts_wg"), gi("L0_ffn_experts_wu"), gi("L0_ffn_experts_wd")

    c8 = lambda v: np.ascontiguousarray(v.reshape(DC, 128).T)
    t2 = lambda v: np.ascontiguousarray(np.tile(v, 2).reshape(128, 1))
    z64 = np.zeros((64, 64), np.float32)
    bd = lambda w: np.ascontiguousarray(np.block([[w.T, z64], [z64, w.T]]))
    mask = np.full((128, 128), NEG_BIG, np.float32)
    for b in range(8):
        mask[b * 16:(b + 1) * 16, b * 16:(b + 1) * 16] = 0.0
    oh16 = np.zeros((16, 16 * 128), np.float32)
    for e in range(16):
        oh16[e, e * 128:(e + 1) * 128] = 1.0
    hm2s = np.zeros((128, 2), np.float32)
    hm2s[0:64, 0] = 1.0 / 64
    hm2s[64:128, 1] = 1.0 / 64
    hm2T = np.zeros((2, 128), np.float32)
    hm2T[0, 0:64] = 1.0
    hm2T[1, 64:128] = 1.0
    gw128 = np.ascontiguousarray(gw.reshape(DC, 128, 16).transpose(1, 0, 2).reshape(128, 128))

    common = {
        "x16T": np.ascontiguousarray(x[:, :16, :].reshape(128, D).T),
        "qwT": np.ascontiguousarray(qw.T), "kwT": np.ascontiguousarray(kw.T),
        "vwT": np.ascontiguousarray(vw.T), "owT": np.ascontiguousarray(ow.T),
        "lqbd": bd(lqw), "lkbd": bd(lkw), "lvbd": bd(lvw),
        "qb8": c8(qb), "vb8": c8(vb), "ob8": c8(ob), "lna8": c8(lna), "lnb8": c8(lnb),
        "lqb1": t2(lqb), "lkb1": t2(lkb), "lvb1": t2(lvb), "lnc1": t2(lnc), "lnd1": t2(lnd),
        "gw128": gw128, "gb16": np.ascontiguousarray(gb.reshape(16, 1)),
        "idf": np.eye(128, dtype=np.float32), "mask": mask, "oh16": oh16,
        "mneg": np.full((1, 128), -1.0e30, np.float32),
        "b1e30": np.where(np.arange(128)[None, :] // 16 == np.arange(8)[:, None],
                          1.0e30, 0.0).astype(np.float32),
        "bsel": np.where(np.arange(128)[None, :] // 16 == np.arange(8)[:, None],
                         1.0, 0.0).astype(np.float32),
        "inv1024r": np.full((1, 128), 1.0 / 1024, np.float32),
        "ones1r": np.ones((1, 128), np.float32),
        "onescol": np.ones((128, 1), np.float32),
        "hm2s": hm2s, "hm2T": hm2T,
        "swg": swg, "swu": swu, "swd": swd, "ewg": ewg, "ewu": ewu, "ewd": ewd,
    }
    in_maps = []
    for i in range(8):
        sel = np.zeros((128, 16), np.float32)
        for j in range(16):
            sel[i * 16 + j, j] = 1.0
        m = dict(common)
        m["xT"] = np.ascontiguousarray(x[i].T)
        m["selT"] = sel
        in_maps.append(m)

    nc = build_nc()
    res = run_bass_kernel_spmd(nc, in_maps, core_ids=list(range(8)))
    out = np.empty((B, S, D), np.float32)
    for i in range(8):
        out[i] = np.asarray(res.results[i]["outT"]).T
    return out


if __name__ == "__main__":
    import reference as _r
    ins = _r.setup_inputs(0)
    got = kernel(**{k: np.asarray(v) for k, v in ins.items()})
    want = np.asarray(_r.reference(**ins))
    num = float(((got - want) ** 2).mean())
    den = float((want ** 2).mean())
    print("resid_var_ratio:", num / den)


# version 27
# speedup vs baseline: 1.1710x; 1.0051x over previous
import math
import numpy as np
import ml_dtypes
import concourse.bass as bass
import concourse.mybir as mybir
from concourse.bass_utils import run_bass_kernel_spmd
from contextlib import ExitStack

F32 = mybir.dt.float32
BF16 = mybir.dt.bfloat16
AX = mybir.AxisListType.X
OP = mybir.AluOpType
AF = mybir.ActivationFunctionType

B, S, D, H, HD, E, II = 8, 2048, 1024, 16, 64, 16, 512
DC, IC = D // 128, II // 128
TT = 256
GT = 512
NT = S // GT
GTO = [1, 2, 3, 0]
LNT = 512
NLT = S // LNT
EPS = 1e-5
LN8 = math.log(0.125)
TH_C = 16777216.0 * 0.01 / 0.9
NEG_BIG = -1.0e30
TILE_ORDER = [2, 3, 4, 5, 6, 7, 0, 1]
LN_EARLY = [1, 2, 3]
NEXP = 17
DEBUG = False


def build_nc():
    nc = bass.Bass("TRN2", target_bir_lowering=False, detect_race_conditions=False)

    def hin(name, shape):
        return nc.dram_tensor(name, list(shape), F32, kind="ExternalInput").ap()

    xT_h = hin("xT", (D, S))
    selT_h = hin("selT", (128, 16))
    x16T_h = hin("x16T", (D, 128))
    qwT_h = hin("qwT", (D, D)); kwT_h = hin("kwT", (D, D))
    vwT_h = hin("vwT", (D, D)); owT_h = hin("owT", (D, D))
    lqbd_h = hin("lqbd", (128, 128)); lkbd_h = hin("lkbd", (128, 128)); lvbd_h = hin("lvbd", (128, 128))
    qb8_h = hin("qb8", (128, DC)); vb8_h = hin("vb8", (128, DC)); ob8_h = hin("ob8", (128, DC))
    lna8_h = hin("lna8", (128, DC)); lnb8_h = hin("lnb8", (128, DC))
    lqb1_h = hin("lqb1", (128, 1)); lkb1_h = hin("lkb1", (128, 1)); lvb1_h = hin("lvb1", (128, 1))
    lnc1_h = hin("lnc1", (128, 1)); lnd1_h = hin("lnd1", (128, 1))
    gw128_h = hin("gw128", (128, 128))
    gb16_h = hin("gb16", (16, 1))
    idf_h = hin("idf", (128, 128))
    mask_h = hin("mask", (128, 128))
    oh16_h = hin("oh16", (16, 16 * 128))
    inv1024r_h = hin("inv1024r", (1, 128))
    ones1r_h = hin("ones1r", (1, 128))
    onescol_h = hin("onescol", (128, 1))
    hm2s_h = hin("hm2s", (128, 2))
    hm2T_h = hin("hm2T", (2, 128))
    swg_h = hin("swg", (D, II)); swu_h = hin("swu", (D, II)); swd_h = hin("swd", (II, D))
    ewg_h = hin("ewg", (E, D, II)); ewu_h = hin("ewu", (E, D, II)); ewd_h = hin("ewd", (E, II, D))
    outT_h = nc.dram_tensor("outT", [D, S], F32, kind="ExternalOutput").ap()
    outT_v = outT_h.rearrange("(c p) t -> p c t", p=128)

    P = {"t": [], "v": [], "s": [], "y": [], "g": []}

    class Sem:
        def __init__(self, h, step):
            self.h, self.step, self.n = h, step, 0

    es = ExitStack()
    S_T = Sem(es.enter_context(nc.semaphore("s_t")), 1)
    S_V = Sem(es.enter_context(nc.semaphore("s_v")), 1)
    S_S = Sem(es.enter_context(nc.semaphore("s_s")), 1)
    S_Y = Sem(es.enter_context(nc.semaphore("s_y")), 16)
    S_G = Sem(es.enter_context(nc.semaphore("s_g")), 16)
    S_C = Sem(es.enter_context(nc.semaphore("s_c")), 16)
    QSEM = {"t": S_T, "v": S_V, "s": S_S, "y": S_Y, "g": S_G}

    def emit(q, fn, wait=(), sem="auto"):
        s = QSEM[q] if sem == "auto" else sem
        w = tuple((sm.h, v) for (sm, v) in wait)
        def run(eng, fn=fn, w=w, sh=None if s is None else s.h,
                amt=0 if s is None else s.step):
            for h, v in w:
                eng.wait_ge(h, v)
            r = fn(eng)
            if sh is not None:
                r.then_inc(sh, amt)
        P[q].append(run)
        if s is not None:
            s.n += s.step

    MK = {}

    def vfill():
        emit("v", lambda v: v.tensor_copy(fillb[:, 1:2], fillb[:, 0:1]))

    sb = lambda name, shape, dt=F32: es.enter_context(nc.sbuf_tensor(name, list(shape), dt))
    psb = lambda name: es.enter_context(nc.psum_tensor(name, [128, 512], F32))

    strm = sb("strm", (128, DC, S))
    wg_sb = [sb(f"wgsb{i}", (128, DC, II), BF16) for i in range(2)]
    wu_sb = [sb(f"wusb{i}", (128, DC, II), BF16) for i in range(2)]
    wd_sb = [sb(f"wdsb{i}", (128, IC, D), BF16) for i in range(2)]
    h_sb = [sb(f"hsb{i}", (128, IC, GT), BF16) for i in range(2)]
    sx_sb = sb("sxsb", (128, IC, GT))
    wbc = sb("wbc", (128, S), BF16)
    w_hi = sb("w_hi", (16, S), BF16)
    w_lo = sb("w_lo", (16, S), BF16)
    o16 = sb("o16", (128, DC, 16))
    idf_s = sb("idf_s", (128, 128))
    idb_s = sb("idb_s", (128, 128), BF16)
    mask_s = sb("mask_s", (128, 128))
    oh16_b = sb("oh16_b", (16, 16 * 128), BF16)
    inv1024_b = sb("inv1024_b", (1, 128), BF16)
    ones1_b = sb("ones1_b", (1, 128), BF16)
    onescol_b = sb("onescol_b", (128, 1), BF16)
    hm2s_b = sb("hm2s_b", (128, 2), BF16)
    hm2T_b = sb("hm2T_b", (2, 128), BF16)
    selT_b = sb("selT_b", (128, 16), BF16)
    qb8_s = sb("qb8_s", (128, DC)); vb8_s = sb("vb8_s", (128, DC)); ob8_s = sb("ob8_s", (128, DC))
    lna8_s = sb("lna8_s", (128, DC)); lnb8_s = sb("lnb8_s", (128, DC))
    lqb1_s = sb("lqb1_s", (128, 1)); lkb1_s = sb("lkb1_s", (128, 1)); lvb1_s = sb("lvb1_s", (128, 1))
    lnc1_s = sb("lnc1_s", (128, 1)); lnd1_s = sb("lnd1_s", (128, 1))
    gwhi_s = sb("gwhi_s", (128, 128), BF16)
    gwlo_s = sb("gwlo_s", (128, 128), BF16)
    gb_s = sb("gb_s", (16, 1))
    cpool = sb("cpool", (128, 4))
    fillb = sb("fillb", (128, 2))
    nc.const_aps.register(F32, 0.0, cpool[:, 0:1])
    nc.const_aps.register(F32, EPS, cpool[:, 1:2])
    nc.const_aps.register(F32, LN8, cpool[:, 2:3])

    pg0, pg1, pu0, pu1 = psb("pg0"), psb("pg1"), psb("pu0"), psb("pu1")
    pd0, pd1, pd2, pd3 = psb("pd0"), psb("pd1"), psb("pd2"), psb("pd3")
    pg = [pg0, pg1]; pu = [pu0, pu1]; pd = [pd0, pd1, pd2, pd3]

    SA = ExitStack()
    sa = lambda name, shape, dt=F32: SA.enter_context(nc.sbuf_tensor(name, list(shape), dt))
    awT = sa("awT", (128, DC, D), BF16)
    x16b = sa("x16b", (128, DC, 128), BF16)
    sq16 = sa("sq16", (128, DC, 128), BF16)
    xab = sa("xab", (128, DC, 128), BF16)
    xbb = sa("xbb", (128, DC, 128), BF16)
    qf = sa("qf", (128, DC, 128))
    qcb = sa("qcb", (128, DC, 128), BF16)
    kvb = sa("kvb", (128, DC, 128), BF16)
    ascr = sa("ascr", (128, DC, 128))
    asb = sa("asb", (128, DC, 128), BF16)
    knb = sa("knb", (128, DC, 128), BF16)
    vit = sa("vit", (128, DC, 128), BF16)
    it0 = sa("it0", (128, DC, 128), BF16)
    it1 = sa("it1", (128, DC, 128), BF16)
    lbd_b = sa("lbd_b", (128, 3, 128), BF16)
    gwf_s = sa("gwf_s", (128, 128))
    pf = [sa(f"pf{i}", (128, 128)) for i in range(4)]
    pf2 = [sa(f"pf2{i}", (128, 128)) for i in range(4)]
    ptb = [sa(f"ptb{i}", (128, 128), BF16) for i in range(4)]
    ptb2 = [sa(f"ptb2{i}", (128, 128), BF16) for i in range(4)]
    lmaxn = [sa(f"lmaxn{i}", (128, 1)) for i in range(4)]
    rsum = [sa(f"rsum{i}", (128, 1)) for i in range(4)]
    rrec = [sa(f"rrec{i}", (128, 1)) for i in range(4)]
    ptob = [sa(f"ptob{i}", (128, 16), BF16) for i in range(2)]
    aob = sa("aob", (128, DC, 16), BF16)
    rowsA = sa("rowsA", (2, 1024))
    rowsB = sa("rowsB", (2, 1024), BF16)
    colscr = sa("colscr", (128, DC))
    dcol = sa("dcol", (128, 1))
    dcolb = sa("dcolb", (128, 1), BF16)
    m01s = sa("m01s", (1, 4))
    m01b = sa("m01b", (1, 1), BF16)
    m01c = sa("m01c", (128, 1))

    for dst, src in [(idb_s[:, :], idf_h[:, :]), (oh16_b[:, :], oh16_h[:, :]),
                     (inv1024_b[:, :], inv1024r_h[:, :]), (ones1_b[:, :], ones1r_h[:, :]),
                     (onescol_b[:, :], onescol_h[:, :]), (hm2s_b[:, :], hm2s_h[:, :]),
                     (hm2T_b[:, :], hm2T_h[:, :]), (selT_b[:, :], selT_h[:, :]),
                     (gwhi_s[:, :], gw128_h[:, :]),
                     (lbd_b[:, 0, :], lqbd_h[:, :]), (lbd_b[:, 1, :], lkbd_h[:, :]),
                     (lbd_b[:, 2, :], lvbd_h[:, :]),
                     (x16b[:, :, :], x16T_h.rearrange("(c p) t -> p c t", p=128))]:
        emit("g", lambda g, d=dst, s=src: g.dma_start(d, s))
    MK["g_consts"] = S_G.n
    emit("g", lambda g: g.dma_start(awT[:, :, 0:512], qwT_h[:, 0:512].rearrange("(c p) i -> p c i", p=128)))
    MK["g_awqh"] = S_G.n
    emit("g", lambda g: g.dma_start(awT[:, :, 512:1024], qwT_h[:, 512:1024].rearrange("(c p) i -> p c i", p=128)))
    MK["g_awq"] = S_G.n

    def emit_wload(e):
        par = e % 2
        if e == 0:
            srcs = [(wg_sb[0], swg_h), (wu_sb[0], swu_h), (wd_sb[0], swd_h)]
        else:
            srcs = [(wg_sb[par], ewg_h[e - 1]), (wu_sb[par], ewu_h[e - 1]),
                    (wd_sb[par], ewd_h[e - 1])]
        wt = [(S_T, MK[("emm", e - 2)])] if e >= 2 else []
        for k, (dst, src) in enumerate(srcs):
            emit("g", lambda g, d=dst, s=src: g.dma_start(
                d[:, :, :], s.rearrange("(c p) i -> p c i", p=128)),
                 wait=wt if k == 0 else ())
        MK[("wload", e)] = S_G.n

    for dst, src in [(idf_s[:, :], idf_h[:, :]), (mask_s[:, :], mask_h[:, :]),
                     (qb8_s[:, :], qb8_h[:, :]), (vb8_s[:, :], vb8_h[:, :]),
                     (ob8_s[:, :], ob8_h[:, :]), (lna8_s[:, :], lna8_h[:, :]),
                     (lnb8_s[:, :], lnb8_h[:, :]), (lqb1_s[:, :], lqb1_h[:, :]),
                     (lkb1_s[:, :], lkb1_h[:, :]), (lvb1_s[:, :], lvb1_h[:, :]),
                     (lnc1_s[:, :], lnc1_h[:, :]), (lnd1_s[:, :], lnd1_h[:, :]),
                     (gwf_s[:, :], gw128_h[:, :]), (gb_s[:, :], gb16_h[:, :])]:
        emit("y", lambda y, d=dst, s=src: y.dma_start(d, s))
    MK["y_consts"] = S_Y.n
    emit("y", lambda y: y.dma_start(strm[:, :, :], xT_h.rearrange("(c p) t -> p c t", p=128)))
    MK["y_xT"] = S_Y.n

    for j, val in [(0, 0.0), (1, EPS), (2, LN8), (3, 0.0)]:
        emit("v", lambda v, j=j, val=val: v.memset(cpool[:, j:j + 1], val))
    emit("v", lambda v: v.memset(fillb[:, 0:1], 0.0))
    emit("v", lambda v: v.tensor_tensor(gwlo_s[:, :], gwf_s[:, :], gwhi_s[:, :], OP.subtract),
         wait=[(S_Y, MK["y_consts"]), (S_G, MK["g_consts"])])
    MK["v_init"] = S_V.n

    def head_ln(src_f, dst_bf, wcol, use_ln8, tag):
        scap = [(S_S, S_S.n)]
        for c in range(DC):
            emit("v", lambda v, c=c: v.tensor_copy(asb[:, c, :], src_f[:, c, :]),
                 wait=scap if c == 0 else ())
        MK[(tag, "bf")] = S_V.n
        for c in range(DC):
            emit("s", lambda s, c=c: s.activation(sq16[:, c, :], src_f[:, c, :], AF.Square),
                 wait=[(S_V, MK[(tag, "bf")])] if c == 0 else ())
        MK[(tag, "sq")] = S_S.n
        for c in range(DC):
            dstp = pg[c // 4]
            emit("t", lambda t, c=c, dstp=dstp: t.matmul(
                dstp[0:2, (c % 4) * 128:(c % 4) * 128 + 128], hm2s_b[:, :], asb[:, c, :],
                start=True, stop=True),
                 wait=[(S_V, MK[(tag, "bf")])] if c == 0 else ())
        for c in range(DC):
            dstp = pd[2 + c // 4]
            emit("t", lambda t, c=c, dstp=dstp: t.matmul(
                dstp[0:2, (c % 4) * 128:(c % 4) * 128 + 128], hm2s_b[:, :], sq16[:, c, :],
                start=True, stop=True),
                 wait=[(S_S, MK[(tag, "sq")])] if c == 0 else ())
        MK[(tag, "stats")] = S_T.n
        for half in range(2):
            pm, pq = pg[half], pd[2 + half]
            wv = [(S_T, MK[(tag, "stats")])] if half == 0 else [(S_S, MK[(tag, "rstd", 0)])]
            emit("v", lambda v, pm=pm: v.tensor_copy(rowsA[0:2, 0:512], pm[0:2, 0:512]), wait=wv)
            emit("v", lambda v: v.tensor_tensor(
                rowsA[0:2, 512:1024], rowsA[0:2, 0:512], rowsA[0:2, 0:512], OP.mult))
            emit("v", lambda v, pq=pq: v.tensor_tensor(
                rowsA[0:2, 0:512], pq[0:2, 0:512], rowsA[0:2, 512:1024], OP.subtract))
            wm = [(S_T, MK[(tag, "bc", 0)])] if half == 1 else []
            emit("v", lambda v, pm=pm: v.tensor_copy(rowsB[0:2, 512:1024], pm[0:2, 0:512]),
                 wait=wm)
            MK[(tag, "var", half)] = S_V.n
            emit("s", lambda s: s.activation(rowsA[0:2, 512:1024], rowsA[0:2, 0:512],
                                             AF.Ln, bias=EPS, scale=1.0),
                 wait=[(S_V, MK[(tag, "var", half)])])
            bias = LN8 if use_ln8 else 0.0
            emit("s", lambda s, bias=bias: s.activation(
                rowsB[0:2, 0:512], rowsA[0:2, 512:1024], AF.Exp, bias=bias, scale=-0.5))
            MK[(tag, "rstd", half)] = S_S.n
            bm, br = (pu0, pu1) if half == 0 else (pd0, pd1)
            wb = [(S_S, MK[(tag, "rstd", half)]), (S_V, S_V.n)]
            for cc in range(4):
                emit("t", lambda t, cc=cc, bm=bm: t.matmul(
                    bm[:, cc * 128:cc * 128 + 128], hm2T_b[:, :],
                    rowsB[0:2, 512 + cc * 128:512 + cc * 128 + 128],
                    start=True, stop=True),
                     wait=wb if cc == 0 else ())
                emit("t", lambda t, cc=cc, br=br: t.matmul(
                    br[:, cc * 128:cc * 128 + 128], hm2T_b[:, :],
                    rowsB[0:2, cc * 128:cc * 128 + 128],
                    start=True, stop=True))
            MK[(tag, "bc", half)] = S_T.n
            for cc in range(4):
                c = half * 4 + cc
                emit("v", lambda v, c=c, cc=cc, bm=bm: v.tensor_tensor(
                    src_f[:, c, :], src_f[:, c, :], bm[:, cc * 128:cc * 128 + 128], OP.subtract),
                     wait=[(S_T, MK[(tag, "bc", half)])] if cc == 0 else ())
                emit("v", lambda v, c=c, cc=cc, br=br: v.scalar_tensor_tensor(
                    dst_bf[:, c, :], src_f[:, c, :], wcol[:, 0:1],
                    br[:, cc * 128:cc * 128 + 128], OP.mult, OP.mult))
            MK[(tag, "apply", half)] = S_V.n
        MK[(tag, "done")] = S_V.n

    for c in range(DC):
        emit("s", lambda s, c=c: s.activation(sq16[:, c, :], x16b[:, c, :], AF.Square),
             wait=[(S_G, MK["g_consts"])] if c == 0 else ())
    MK["x16sq"] = S_S.n
    for c in range(DC):
        emit("t", lambda t, c=c: t.matmul(pg0[0:1, 0:128], onescol_b[:, :], x16b[:, c, :],
                                          start=(c == 0), stop=(c == DC - 1)),
             wait=[(S_G, MK["g_consts"])] if c == 0 else ())
    for c in range(DC):
        emit("t", lambda t, c=c: t.matmul(pg0[32:33, 0:128], onescol_b[:, :], sq16[:, c, :],
                                          start=(c == 0), stop=(c == DC - 1)),
             wait=[(S_S, MK["x16sq"])] if c == 0 else ())
    MK["x16stats"] = S_T.n
    emit("v", lambda v: v.tensor_copy(rowsB[0:1, 0:128], pg0[0:1, 0:128]),
         wait=[(S_T, MK["x16stats"]), (S_V, MK["v_init"])])
    emit("v", lambda v: v.tensor_scalar(rowsA[0:1, 0:128], pg0[0:1, 0:128], 1.0 / D, None, OP.mult))
    emit("v", lambda v: v.tensor_tensor(rowsA[0:1, 128:256], rowsA[0:1, 0:128], rowsA[0:1, 0:128], OP.mult))
    emit("v", lambda v: v.tensor_scalar(rowsA[0:1, 256:384], pg0[32:33, 0:128], 1.0 / D, None, OP.mult))
    emit("v", lambda v: v.tensor_tensor(rowsA[0:1, 256:384], rowsA[0:1, 256:384], rowsA[0:1, 128:256], OP.subtract))
    MK["x16var"] = S_V.n
    emit("s", lambda s: s.activation(rowsA[0:1, 384:512], rowsA[0:1, 256:384], AF.Ln, bias=EPS, scale=1.0),
         wait=[(S_V, MK["x16var"])])
    emit("s", lambda s: s.activation(rowsB[0:1, 128:256], rowsA[0:1, 384:512], AF.Exp, bias=0.0, scale=-0.5))
    MK["x16rstd"] = S_S.n
    emit("t", lambda t: t.matmul(pg1[:, 0:128], inv1024_b[:, :], rowsB[0:1, 0:128], start=True, stop=True),
         wait=[(S_S, MK["x16rstd"])])
    emit("t", lambda t: t.matmul(pg1[:, 128:256], ones1_b[:, :], rowsB[0:1, 128:256], start=True, stop=True))
    MK["x16bc"] = S_T.n
    for c in range(DC):
        emit("v", lambda v, c=c: v.tensor_tensor(ascr[:, c, :], x16b[:, c, :], pg1[:, 0:128], OP.subtract),
             wait=[(S_T, MK["x16bc"])] if c == 0 else ())
        emit("v", lambda v, c=c: v.scalar_tensor_tensor(
            xab[:, c, :], ascr[:, c, :], lna8_s[:, c:c + 1], pg1[:, 128:256], OP.mult, OP.mult))
        emit("v", lambda v, c=c: v.scalar_tensor_tensor(
            xbb[:, c, :], ascr[:, c, :], lnb8_s[:, c:c + 1], pg1[:, 128:256], OP.mult, OP.mult))
    MK["xa"] = S_V.n
    MK["xb"] = S_V.n

    def proj1024(dstbank, tag, wt_first, wt_half=(), rhs=0):
        xr = xab if rhs == 0 else xbb
        for i in range(DC):
            for c in range(DC):
                bank, off = dstbank(i)
                wt = ()
                if i == 0 and c == 0:
                    wt = wt_first
                elif i == 4 and c == 0:
                    wt = wt_half
                emit("t", lambda t, i=i, c=c, bank=bank, off=off, xr=xr: t.matmul(
                    bank[:, off:off + 128], awT[:, c, i * 128:i * 128 + 128], xr[:, c, :],
                    start=(c == 0), stop=(c == DC - 1)),
                     wait=wt)
            MK[(tag, i)] = S_T.n
        MK[tag] = S_T.n

    proj1024(lambda i: (pg[i // 4], (i % 4) * 128), "qmm",
             [(S_G, MK["g_awqh"]), (S_V, MK["xa"])], [(S_G, MK["g_awq"])])
    emit("g", lambda g: g.dma_start(awT[:, :, 0:512], kwT_h[:, 0:512].rearrange("(c p) i -> p c i", p=128)),
         wait=[(S_T, MK["qmm"])])
    MK["g_awkh"] = S_G.n
    emit("g", lambda g: g.dma_start(awT[:, :, 512:1024], kwT_h[:, 512:1024].rearrange("(c p) i -> p c i", p=128)))
    MK["g_awk"] = S_G.n
    for i in range(DC):
        emit("s", lambda s, i=i: s.activation(
            qf[:, i, :], pg[i // 4][:, (i % 4) * 128:(i % 4) * 128 + 128], AF.Identity,
            bias=qb8_s[:, i:i + 1], scale=1.0),
             wait=[(S_T, MK[("qmm", (i // 4) * 4 + 3)])] if i % 4 == 0 else ())
        emit("s", lambda s, i=i: s.activation(qcb[:, i, :], qf[:, i, :], AF.Identity))
    MK["qevac"] = S_S.n
    proj1024(lambda i: (pu[i // 4], (i % 4) * 128), "kmm",
             [(S_G, MK["g_awkh"]), (S_S, MK["qevac"])], [(S_G, MK["g_awk"])], rhs=1)
    emit("g", lambda g: g.dma_start(awT[:, :, 0:512], vwT_h[:, 0:512].rearrange("(c p) i -> p c i", p=128)),
         wait=[(S_T, MK["kmm"])])
    MK["g_awvh"] = S_G.n
    emit("g", lambda g: g.dma_start(awT[:, :, 512:1024], vwT_h[:, 512:1024].rearrange("(c p) i -> p c i", p=128)))
    MK["g_awv"] = S_G.n
    for i in range(DC):
        emit("s", lambda s, i=i: s.activation(
            kvb[:, i, :], pu[i // 4][:, (i % 4) * 128:(i % 4) * 128 + 128], AF.Identity),
             wait=[(S_T, MK[("kmm", (i // 4) * 4 + 3)])] if i % 4 == 0 else ())
    MK["kevac"] = S_S.n
    for i in range(DC):
        emit("t", lambda t, i=i: t.matmul(
            pg[i // 4][:, (i % 4) * 128:(i % 4) * 128 + 128], lbd_b[:, 1, :], kvb[:, i, :],
            start=True, stop=True),
             wait=[(S_S, MK["kevac"])] if i == 0 else ())
    MK["kpre"] = S_T.n
    for i in range(DC):
        emit("s", lambda s, i=i: s.activation(
            ascr[:, i, :], pg[i // 4][:, (i % 4) * 128:(i % 4) * 128 + 128], AF.Identity,
            bias=lkb1_s[:, 0:1], scale=1.0),
             wait=[(S_T, MK["kpre"]), (S_V, S_V.n)] if i == 0 else ())
    MK["kpre_e"] = S_S.n
    head_ln(ascr, knb, lnd1_s, False, "klnh")

    proj1024(lambda i: (pd[2 + i // 4], (i % 4) * 128), "vmm",
             [(S_G, MK["g_awvh"]), (S_T, MK["kpre"]), (S_V, S_V.n)], [(S_G, MK["g_awv"])], rhs=1)
    emit("g", lambda g: g.dma_start(awT[:, :, 0:512], owT_h[:, 0:512].rearrange("(c p) i -> p c i", p=128)),
         wait=[(S_T, MK["vmm"])])
    MK["g_awoh"] = S_G.n
    emit("g", lambda g: g.dma_start(awT[:, :, 512:1024], owT_h[:, 512:1024].rearrange("(c p) i -> p c i", p=128)))
    MK["g_awo"] = S_G.n
    emit_wload(0)
    emit_wload(1)
    for i in range(DC):
        emit("s", lambda s, i=i: s.activation(
            kvb[:, i, :], pd[2 + i // 4][:, (i % 4) * 128:(i % 4) * 128 + 128], AF.Identity,
            bias=vb8_s[:, i:i + 1], scale=1.0),
             wait=[(S_T, MK[("vmm", (i // 4) * 4 + 3)])] if i % 4 == 0 else ())
    MK["vevac"] = S_S.n
    for i in range(DC):
        emit("t", lambda t, i=i: t.matmul(
            pg[i // 4][:, (i % 4) * 128:(i % 4) * 128 + 128], lbd_b[:, 2, :], kvb[:, i, :],
            start=True, stop=True),
             wait=[(S_S, MK["vevac"])] if i == 0 else ())
    MK["vpre"] = S_T.n
    for i in range(DC):
        emit("s", lambda s, i=i: s.activation(
            ascr[:, i, :], pg[i // 4][:, (i % 4) * 128:(i % 4) * 128 + 128], AF.Identity,
            bias=lvb1_s[:, 0:1], scale=1.0),
             wait=[(S_T, MK["vpre"]), (S_V, S_V.n)] if i == 0 else ())
    MK["vi_f"] = S_S.n
    for i in range(DC):
        emit("t", lambda t, i=i: t.transpose(
            pd[2 + i // 4][:, (i % 4) * 128:(i % 4) * 128 + 128], ascr[:, i, :], idf_s[:, :]),
             wait=[(S_S, MK["vi_f"]), (S_Y, MK["y_consts"])] if i == 0 else ())
    MK["vi_t"] = S_T.n
    for i in range(DC):
        emit("v", lambda v, i=i: v.tensor_copy(
            vit[:, i, :], pd[2 + i // 4][:, (i % 4) * 128:(i % 4) * 128 + 128]),
             wait=[(S_T, MK["vi_t"])] if i == 0 else ())
    MK["vit"] = S_V.n

    def attn_step(step):
        tg = ("st", step)
        for i in range(DC):
            wq = ([(S_S, MK[("qin", 0)])] if step == 0 else [(S_V, MK[("qin", step)])])
            emit("t", lambda t, i=i: t.matmul(
                pg[i // 4][:, (i % 4) * 128:(i % 4) * 128 + 128], lbd_b[:, 0, :], qcb[:, i, :],
                start=True, stop=True),
                 wait=wq if i == 0 else ())
        MK[(tg, "qpre")] = S_T.n
        for i in range(DC):
            emit("s", lambda s, i=i: s.activation(
                ascr[:, i, :], pg[i // 4][:, (i % 4) * 128:(i % 4) * 128 + 128], AF.Identity,
                bias=lqb1_s[:, 0:1], scale=1.0),
                 wait=[(S_T, MK[(tg, "qpre")]), (S_V, S_V.n)] if i == 0 else ())
        MK[(tg, "qpre_e")] = S_S.n
        head_ln(ascr, asb, lnc1_s, True, (tg, "qln"))
        for h in range(H):
            ic, half = h // 2, h % 2
            hp = h % 4
            sl = slice(half * 64, half * 64 + 64)
            lg = pd2[:, (h % 4) * 128:(h % 4) * 128 + 128]
            wt_l = [(S_V, MK[((tg, "qln"), "done")])] if h == 0 else []
            if h >= 4:
                if step < 2:
                    wt_l = wt_l + [(S_V, MK[(tg, "pevac", h - 4)])]
                else:
                    wt_l = wt_l + [(S_V, MK[(tg, "pn", h - 4)])]
            emit("t", lambda t, ic=ic, sl=sl, lg=lg: t.matmul(
                lg, asb[sl, ic, :], knb[sl, ic, :], start=True, stop=True), wait=wt_l)
            MK[(tg, "lg", h)] = S_T.n
            wt_m = [(S_T, MK[(tg, "lg", h)])]
            if h >= 4:
                wt_m.append((S_S, MK[(tg, "exp", h - 4)]))
            emit("v", lambda v, lg=lg, hp=hp: v.reduce_max(lmaxn[hp][:, 0:1], lg, axis=AX, negate=True),
                 wait=wt_m)
            emit("v", lambda v, lg=lg, hp=hp: v.tensor_tensor(pf[hp][:, :], lg, mask_s[:, :], OP.add))
            MK[(tg, "pf", h)] = S_V.n
            wt_e = [(S_V, MK[(tg, "pf", h)])]
            if h >= 4:
                wt_e.append((S_V, MK[(tg, "pn", h - 4)]))
            emit("s", lambda s, hp=hp: s.activation(pf2[hp][:, :], pf[hp][:, :], AF.Exp,
                                                    bias=lmaxn[hp][:, 0:1],
                                                    scale=1.0, accum_out=rsum[hp][:, 0:1]),
                 wait=wt_e)
            MK[(tg, "exp", h)] = S_S.n
            emit("v", lambda v, hp=hp: v.reciprocal(rrec[hp][:, 0:1], rsum[hp][:, 0:1]),
                 wait=[(S_S, MK[(tg, "exp", h)])])
            vfill()
            wt_c = [(S_T, MK[(tg, "pt" if step < 2 else "pto", h - 4)])] if h >= 4 else []
            emit("v", lambda v, hp=hp: v.tensor_scalar(ptb[hp][:, :], pf2[hp][:, :], rrec[hp][:, 0:1], None, OP.mult),
                 wait=wt_c)
            MK[(tg, "pn", h)] = S_V.n
            if step < 2:
                ptp = pd3[:, (h % 4) * 128:(h % 4) * 128 + 128]
                emit("t", lambda t, ptp=ptp, hp=hp: t.matmul(ptp, ptb[hp][:, :], idb_s[:, :], start=True, stop=True),
                     wait=[(S_V, MK[(tg, "pn", h)])])
                MK[(tg, "pt", h)] = S_T.n
                wt_v = [(S_T, MK[(tg, "pt", h)])]
                if h >= 4:
                    wt_v.append((S_T, MK[(tg, "at", h - 4)]))
                emit("v", lambda v, ptp=ptp, hp=hp: v.tensor_copy(ptb2[hp][:, :], ptp),
                     wait=wt_v)
                MK[(tg, "pevac", h)] = S_V.n
                emit("t", lambda t, sl=sl, ic=ic, hp=hp: t.matmul(
                    pg[ic // 4][sl, (ic % 4) * 128:(ic % 4) * 128 + 128],
                    vit[:, ic, sl], ptb2[hp][:, :], start=True, stop=True),
                     wait=[(S_V, MK[(tg, "pevac", h)])])
                MK[(tg, "at", h)] = S_T.n
            else:
                emit("t", lambda t, h=h, hp=hp: t.matmul(pd3[:, h * 16:h * 16 + 16], ptb[hp][:, :],
                                                  selT_b[:, :], start=True, stop=True),
                     wait=[(S_V, MK[(tg, "pn", h)])])
                MK[(tg, "pto", h)] = S_T.n
                pob = ptob[h % 2]
                wt_p = [(S_T, MK[(tg, "pto", h)])]
                if h >= 2:
                    wt_p.append((S_T, MK[(tg, "aow", h - 2)]))
                emit("v", lambda v, pob=pob, h=h: v.tensor_copy(pob[:, :], pd3[:, h * 16:h * 16 + 16]),
                     wait=wt_p)
                MK[(tg, "ptob", h)] = S_V.n
                emit("t", lambda t, sl=sl, ic=ic, pob=pob: t.matmul(
                    pg0[sl, ic * 16:ic * 16 + 16], vit[:, ic, sl], pob[:, :], start=True, stop=True),
                     wait=[(S_V, MK[(tg, "ptob", h)])])
                MK[(tg, "aow", h)] = S_T.n
                if half == 1:
                    emit("v", lambda v, ic=ic: v.tensor_copy(aob[:, ic, :], pg0[:, ic * 16:ic * 16 + 16]),
                         wait=[(S_T, MK[(tg, "aow", h)])])
                    MK[(tg, "aoev", ic)] = S_V.n
        if step < 2:
            dst = it0 if step == 0 else it1
            for i in range(DC):
                emit("v", lambda v, i=i, dst=dst: v.tensor_copy(
                    dst[:, i, :], pg[i // 4][:, (i % 4) * 128:(i % 4) * 128 + 128]),
                     wait=[(S_T, MK[(tg, "at", 2 * i + 1)])])
            MK[(tg, "itev")] = S_V.n

    MK[("qin", 0)] = MK["qevac"]
    attn_step(0)
    for i in range(DC):
        emit("v", lambda v, i=i: v.tensor_tensor(qf[:, i, :], qf[:, i, :], it0[:, i, :], OP.add))
        emit("v", lambda v, i=i: v.tensor_copy(qcb[:, i, :], qf[:, i, :]))
    MK[("qin", 1)] = S_V.n
    attn_step(1)
    for i in range(DC):
        emit("v", lambda v, i=i: v.tensor_tensor(ascr[:, i, :], it1[:, i, :], it0[:, i, :], OP.subtract))
        emit("v", lambda v, i=i: v.reduce_sum(colscr[:, i:i + 1], ascr[:, i, :], axis=AX,
                                              apply_absolute_value=True))
    vfill()
    emit("v", lambda v: v.reduce_sum(dcol[:, 0:1], colscr[:, :], axis=AX))
    vfill()
    emit("v", lambda v: v.tensor_copy(dcolb[:, 0:1], dcol[:, 0:1]))
    MK["dcol"] = S_V.n
    emit("t", lambda t: t.matmul(pd3[0:1, 0:1], onescol_b[:, :], dcolb[:, 0:1], start=True, stop=True),
         wait=[(S_V, MK["dcol"])])
    MK["dsum"] = S_T.n
    emit("v", lambda v: v.tensor_scalar(m01s[0:1, 0:1], pd3[0:1, 0:1], TH_C, None, OP.is_ge),
         wait=[(S_T, MK["dsum"])])
    vfill()
    emit("v", lambda v: v.tensor_copy(m01b[0:1, 0:1], m01s[0:1, 0:1]))
    MK["m01b"] = S_V.n
    emit("t", lambda t: t.matmul(pd3[:, 4:5], ones1_b[:, :], m01b[0:1, 0:1], start=True, stop=True),
         wait=[(S_V, MK["m01b"])])
    MK["m01bc"] = S_T.n
    emit("v", lambda v: v.tensor_copy(m01c[:, 0:1], pd3[:, 4:5]),
         wait=[(S_T, MK["m01bc"])])
    vfill()
    for i in range(DC):
        emit("v", lambda v, i=i: v.scalar_tensor_tensor(
            it1[:, i, :], it1[:, i, :], m01c[:, 0:1], qf[:, i, :], OP.mult, OP.add))
        emit("v", lambda v, i=i: v.tensor_copy(qcb[:, i, :], it1[:, i, :]))
    MK[("qin", 2)] = S_V.n
    attn_step(2)

    for i in range(DC):
        for c in range(DC):
            wt = []
            if i == 0 and c == 0:
                wt = [(S_G, MK["g_awoh"]), (S_V, MK[(("st", 2), "aoev", DC - 1)])]
            elif i == 4 and c == 0:
                wt = [(S_G, MK["g_awo"])]
            emit("t", lambda t, i=i, c=c: t.matmul(
                pd2[:, i * 16:i * 16 + 16], awT[:, c, i * 128:i * 128 + 128], aob[:, c, :],
                start=(c == 0), stop=(c == DC - 1)), wait=wt)
    MK["oproj"] = S_T.n
    for i in range(DC):
        emit("v", lambda v, i=i: v.tensor_copy(o16[:, i, :], pd2[:, i * 16:i * 16 + 16]),
             wait=[(S_T, MK["oproj"])] if i == 0 else ())
    MK["o16ev"] = S_V.n

    SA.close()
    sp = lambda name, shape, dt=F32: es.enter_context(nc.sbuf_tensor(name, list(shape), dt))
    xp_bf = sp("xp_bf", (128, DC, S), BF16)
    sq_bf = sp("sq_bf", (128, DC, LNT), BF16)
    rl_s2 = [sp(f"rl_s{i}", (16, LNT)) for i in range(2)]
    rowsL = sp("rowsL", (1, 1536))
    rowsLb = sp("rowsLb", (1, 1024), BF16)
    rlt_s = sp("rlt_s", (128, 16))
    rls_t = sp("rls_t", (128, 16))
    xlt = sp("xlt", (128, LNT))
    eqs = sp("eqs", (128, 16))
    mx1 = sp("mx1", (128, 1))
    mx2 = sp("mx2", (128, 1))
    rc_ = sp("rc_", (128, 1))

    def stream_ln(tile, with_xp, tag):
        t0 = tile * LNT
        tcap = [(S_T, S_T.n), (S_S, MK["obadd"])]
        for c in range(DC):
            emit("v", lambda v, c=c, t0=t0: v.tensor_copy(xp_bf[:, c, t0:t0 + LNT], strm[:, c, t0:t0 + LNT]),
                 wait=tcap if c == 0 else ())
        MK[(tag, "cast")] = S_V.n
        for c in range(DC):
            emit("s", lambda s, c=c, t0=t0: s.activation(sq_bf[:, c, :], strm[:, c, t0:t0 + LNT], AF.Square),
                 wait=(tcap + [(S_V, MK[(tag, "cast")])]) if c == 0 else ())
        MK[(tag, "sq")] = S_S.n
        for c in range(DC):
            emit("t", lambda t, c=c, t0=t0: t.matmul(pd0[0:1, 0:512], onescol_b[:, :], xp_bf[:, c, t0:t0 + LNT],
                                              start=(c == 0), stop=(c == DC - 1)),
                 wait=[(S_V, MK[(tag, "cast")])] if c == 0 else ())
        for c in range(DC):
            emit("t", lambda t, c=c: t.matmul(pd0[32:33, 0:512], onescol_b[:, :], sq_bf[:, c, :],
                                              start=(c == 0), stop=(c == DC - 1)),
                 wait=[(S_S, MK[(tag, "sq")])] if c == 0 else ())
        MK[(tag, "stats")] = S_T.n
        emit("v", lambda v: v.tensor_copy(rowsLb[0:1, 0:512], pd0[0:1, 0:512]),
             wait=[(S_T, MK[(tag, "stats")])])
        emit("v", lambda v: v.tensor_scalar(rowsL[0:1, 0:512], pd0[0:1, 0:512], 1.0 / D, None, OP.mult))
        emit("v", lambda v: v.tensor_tensor(rowsL[0:1, 512:1024], rowsL[0:1, 0:512], rowsL[0:1, 0:512], OP.mult))
        emit("v", lambda v: v.tensor_scalar(rowsL[0:1, 1024:1536], pd0[32:33, 0:512], 1.0 / D, None, OP.mult))
        emit("v", lambda v: v.tensor_tensor(rowsL[0:1, 1024:1536], rowsL[0:1, 1024:1536],
                                            rowsL[0:1, 512:1024], OP.subtract))
        MK[(tag, "var")] = S_V.n
        emit("s", lambda s: s.activation(rowsL[0:1, 512:1024], rowsL[0:1, 1024:1536], AF.Ln, bias=EPS, scale=1.0),
             wait=[(S_V, MK[(tag, "var")])])
        emit("s", lambda s: s.activation(rowsLb[0:1, 512:1024], rowsL[0:1, 512:1024], AF.Exp, bias=0.0, scale=-0.5))
        MK[(tag, "rstd")] = S_S.n
        emit("t", lambda t: t.matmul(pd1[:, 0:512], inv1024_b[:, :], rowsLb[0:1, 0:512], start=True, stop=True),
             wait=[(S_S, MK[(tag, "rstd")])])
        MK[(tag, "bcm")] = S_T.n
        for c in range(DC):
            emit("v", lambda v, c=c, t0=t0: v.tensor_tensor(
                strm[:, c, t0:t0 + LNT], strm[:, c, t0:t0 + LNT], pd1[:, 0:512], OP.subtract),
                 wait=[(S_T, MK[(tag, "bcm")])] if c == 0 else ())
        MK[(tag, "sub")] = S_V.n
        emit("t", lambda t: t.matmul(pd1[:, 0:512], ones1_b[:, :], rowsLb[0:1, 512:1024], start=True, stop=True),
             wait=[(S_V, MK[(tag, "sub")])])
        MK[(tag, "bcr")] = S_T.n
        if with_xp:
            for c in range(DC):
                emit("v", lambda v, c=c, t0=t0: v.tensor_tensor(
                    xp_bf[:, c, t0:t0 + LNT], strm[:, c, t0:t0 + LNT], pd1[:, 0:512], OP.mult),
                     wait=[(S_T, MK[(tag, "bcr")])] if c == 0 else ())
            MK[(tag, "xpw")] = S_V.n
            emit("v", lambda v: v.tensor_scalar(rowsLb[0:1, 0:512], rowsLb[0:1, 512:1024],
                                                2.0, None, OP.mult),
                 wait=[(S_T, MK[(tag, "bcm")])])
            MK[(tag, "r2")] = S_V.n
            emit("t", lambda t: t.matmul(pd1[:, 0:512], ones1_b[:, :], rowsLb[0:1, 0:512],
                                         start=True, stop=True),
                 wait=[(S_V, MK[(tag, "r2")]), (S_V, MK[(tag, "xpw")])]),
            MK.__setitem__((tag, "bcr2"), S_T.n)
            for c in range(DC):
                emit("v", lambda v, c=c, t0=t0: v.tensor_tensor(
                    strm[:, c, t0:t0 + LNT], strm[:, c, t0:t0 + LNT], pd1[:, 0:512], OP.mult),
                     wait=[(S_T, MK[(tag, "bcr2")])] if c == 0 else ())
        else:
            for c in range(DC):
                emit("v", lambda v, c=c, t0=t0: v.tensor_tensor(
                    strm[:, c, t0:t0 + LNT], strm[:, c, t0:t0 + LNT], pd1[:, 0:512], OP.mult),
                     wait=[(S_T, MK[(tag, "bcr")])] if c == 0 else ())
        MK[(tag, "done")] = S_V.n

    def emit_xlo(lt, wt_extra=()):
        t0 = lt * LNT
        wt = [(S_V, MK[(("xln", lt), "done")]), (S_T, S_T.n)] + list(wt_extra)
        for c in range(DC):
            emit("v", lambda v, c=c, t0=t0: v.tensor_scalar(
                xlt[:, :], strm[:, c, t0:t0 + LNT], 0.5, None, OP.mult),
                 wait=wt if c == 0 else ())
            emit("v", lambda v, c=c, t0=t0: v.tensor_tensor(
                sq_bf[:, c, :], xlt[:, :], xp_bf[:, c, t0:t0 + LNT], OP.subtract))
        MK[("xlo", lt)] = S_V.n

    def emit_rlog(lt, plg, wt_extra=()):
        t0 = lt * LNT
        for c in range(DC):
            wt = ([(S_V, MK[("xlo", lt)]), (S_V, S_V.n)] + list(wt_extra)) if c == 0 else []
            emit("t", lambda t, c=c, t0=t0, plg=plg: t.matmul(
                plg[0:16, 0:512], gwhi_s[:, c * 16:c * 16 + 16], xp_bf[:, c, t0:t0 + LNT],
                start=(c == 0), stop=False), wait=wt)
            emit("t", lambda t, c=c, t0=t0, plg=plg: t.matmul(
                plg[0:16, 0:512], gwlo_s[:, c * 16:c * 16 + 16], xp_bf[:, c, t0:t0 + LNT],
                start=False, stop=False))
            emit("t", lambda t, c=c, t0=t0, plg=plg: t.matmul(
                plg[0:16, 0:512], gwhi_s[:, c * 16:c * 16 + 16], sq_bf[:, c, :],
                start=False, stop=(c == DC - 1)))
        MK[("rlog", lt)] = S_T.n
        rls = rl_s2[lt % 2]
        emit("v", lambda v, plg=plg, rls=rls: v.tensor_scalar(
            rls[0:16, 0:512], plg[0:16, 0:512], gb_s[0:16, 0:1], None, OP.add),
             wait=[(S_T, MK[("rlog", lt)]), (S_T, S_T.n)])
        MK[("sig", lt)] = S_V.n

    top2_prev = {}

    def emit_top2(lt, tp_bank, tr_bank, wt_extra=()):
        for kk in range(4):
            k = lt * 4 + kk
            wt = [(S_V, MK[("sig", lt)])] + (list(wt_extra) if kk == 0 else [])
            if kk in top2_prev:
                wt.append((S_V, MK[("wlo", top2_prev[kk])]))
            top2_prev[kk] = k
            emit("t", lambda t, kk=kk, tp_bank=tp_bank, rls=rl_s2[lt % 2]: t.transpose(
                tp_bank[:, kk * 16:kk * 16 + 16], rls[0:16, kk * 128:kk * 128 + 128],
                idf_s[0:16, 0:16]), wait=wt)
            MK[("rlt", k)] = S_T.n
            emit("s", lambda s, kk=kk, tp_bank=tp_bank: s.activation(
                rls_t[:, :], tp_bank[:, kk * 16:kk * 16 + 16], AF.Sigmoid),
                 wait=[(S_T, MK[("rlt", k)])])
            MK[("sigt", k)] = S_S.n
            emit("v", lambda v, kk=kk, tp_bank=tp_bank: v.tensor_copy(
                rlt_s[:, :], tp_bank[:, kk * 16:kk * 16 + 16]),
                 wait=[(S_T, MK[("rlt", k)]), (S_S, MK[("sigt", k)])])
            vfill()
            emit("v", lambda v: v.reduce_max(mx1[:, 0:1], rlt_s[:, :], axis=AX))
            vfill()
            emit("v", lambda v: v.tensor_scalar(eqs[:, :], rlt_s[:, :], mx1[:, 0:1], None, OP.is_ge))
            vfill()
            emit("v", lambda v: v.tensor_scalar(eqs[:, :], eqs[:, :], NEG_BIG, None, OP.mult))
            vfill()
            emit("v", lambda v: v.tensor_tensor(eqs[:, :], eqs[:, :], rlt_s[:, :], OP.add))
            vfill()
            emit("v", lambda v: v.reduce_max(mx2[:, 0:1], eqs[:, :], axis=AX))
            vfill()
            emit("v", lambda v: v.tensor_scalar(eqs[:, :], rlt_s[:, :], mx2[:, 0:1], None, OP.is_ge))
            vfill()
            emit("v", lambda v: v.tensor_tensor(rlt_s[:, :], rls_t[:, :], eqs[:, :], OP.mult))
            vfill()
            emit("v", lambda v: v.reduce_sum(mx1[:, 0:1], rlt_s[:, :], axis=AX))
            vfill()
            emit("v", lambda v: v.reciprocal(rc_[:, 0:1], mx1[:, 0:1]))
            vfill()
            emit("v", lambda v: v.tensor_scalar(rlt_s[:, :], rlt_s[:, :], rc_[:, 0:1], None, OP.mult))
            MK[("wtok", k)] = S_V.n
            emit("t", lambda t, kk=kk, tr_bank=tr_bank: t.transpose(
                tr_bank[0:16, kk * 128:kk * 128 + 128], rlt_s[:, :], idf_s[:, :]),
                 wait=[(S_V, MK[("wtok", k)])])
            MK[("wtr", k)] = S_T.n
            emit("v", lambda v, k=k, kk=kk, tr_bank=tr_bank: v.tensor_copy(
                w_hi[0:16, k * 128:k * 128 + 128], tr_bank[0:16, kk * 128:kk * 128 + 128]),
                 wait=[(S_T, MK[("wtr", k)])])
            emit("v", lambda v, k=k, kk=kk, tr_bank=tr_bank: v.tensor_tensor(
                w_lo[0:16, k * 128:k * 128 + 128], tr_bank[0:16, kk * 128:kk * 128 + 128],
                w_hi[0:16, k * 128:k * 128 + 128], OP.subtract))
            MK[("wlo", k)] = S_V.n

    for c in range(DC):
        emit("s", lambda s, c=c: s.activation(
            strm[:, c, :], strm[:, c, :], AF.Identity, bias=ob8_s[:, c:c + 1], scale=1.0),
             wait=[(S_Y, MK["y_xT"]), (S_Y, MK["y_consts"])] if c == 0 else ())
    MK["obadd"] = S_S.n
    stream_ln(1, True, ("xln", 1))
    emit_xlo(1)
    emit_rlog(1, pd1)
    stream_ln(2, True, ("xln", 2))
    emit_xlo(2)
    emit_rlog(2, pd1)

    ABANK = [pg0, pg1, pu0, pu1]
    BBANK = [pd0, pd1, pd2, pd3]

    def ptile(e, j, k):
        idx = e * NT + j - k
        return idx // NT, idx % NT

    def emit_gu(e, j):
        par = e % 2
        t0 = GTO[j] * GT
        lt = GTO[j]
        for i in range(IC):
            hgb = (ABANK[i] if i < 2 else BBANK[i - 2])
            hub = (ABANK[2 + i] if i < 2 else BBANK[2 + (i - 2)])
            wt = []
            if i == 0:
                if j == 0:
                    wt.append((S_G, MK[("wload", e)]))
                if e == 0:
                    wt.append((S_V, MK[(("xln", lt), "done")]))
                if e * NT + j >= 1:
                    p1e, p1j = ptile(e, j, 1)
                    wt.append((S_S, MK[("silu", p1e, p1j, 1)]))
                    wt.append((S_V, MK[("h1", p1e, p1j, 1)]))
                if e * NT + j >= 2:
                    pe, pj = ptile(e, j, 2)
                    wt.append((S_V, MK[("drain", pe, pj, 0)]))
            if i == 2:
                if e == 0 and j <= 1:
                    wt.append((S_V, S_V.n))
                if e * NT + j >= 1:
                    p1e, p1j = ptile(e, j, 1)
                    wt.append((S_S, MK[("silu", p1e, p1j, 3)]))
                    wt.append((S_V, MK[("h1", p1e, p1j, 3)]))
                if e * NT + j >= 2:
                    pe, pj = ptile(e, j, 2)
                    wt.append((S_V, MK[("drain", pe, pj, 1)]))
                if j == 1 and e >= 1:
                    wt.append((S_V, MK[("wbcev", e)]))
                if extra_wd_wait:
                    wt.extend(extra_wd_wait)
                    del extra_wd_wait[:]
            for c in range(DC):
                emit("t", lambda t, i=i, c=c, par=par, t0=t0, hgb=hgb: t.matmul(
                    hgb[:, 0:GT], wg_sb[par][:, c, i * 128:i * 128 + 128], xp_bf[:, c, t0:t0 + GT],
                    start=(c == 0), stop=(c == DC - 1)),
                     wait=wt if c == 0 else ())
            for c in range(DC):
                emit("t", lambda t, i=i, c=c, par=par, t0=t0, hub=hub: t.matmul(
                    hub[:, 0:GT], wu_sb[par][:, c, i * 128:i * 128 + 128], xp_bf[:, c, t0:t0 + GT],
                    start=(c == 0), stop=(c == DC - 1)))
            MK[("gu", e, j, i)] = S_T.n

    extra_wd_wait_DUMMY = []
    extra_wd_wait = []

    def emit_wd(e, j, sub, cur, extra=()):
        par = e % 2
        hp = j % 2
        t0 = GTO[j] * GT + sub * TT
        bs = ABANK if sub == 0 else BBANK
        wt0 = [(S_V, MK[("hb", e, j, IC - 1)])] + list(extra)
        ce, cj = cur
        wt0.append((S_S, MK[("silu", ce, cj, 1 if sub == 0 else 3)]))
        wt0.append((S_V, MK[("h1", ce, cj, 1 if sub == 0 else 3)]))
        for c in range(DC):
            dstp = bs[c // 2][:, (c % 2) * TT:(c % 2) * TT + TT]
            for i in range(IC):
                emit("t", lambda t, i=i, c=c, par=par, hp=hp, sub=sub, dstp=dstp: t.matmul(
                    dstp, wd_sb[par][:, i, c * 128:c * 128 + 128],
                    h_sb[hp][:, i, sub * TT:sub * TT + TT],
                    start=(i == 0), stop=(i == IC - 1)),
                     wait=wt0 if (i == 0 and c == 0) else ())
            MK[("wdmm", e, j, sub, c)] = S_T.n
        MK[("wdmm", e, j, sub)] = S_T.n

    def emit_evac_sh(e, j):
        for i in range(IC):
            hgb = (ABANK[i] if i < 2 else BBANK[i - 2])
            hub = (ABANK[2 + i] if i < 2 else BBANK[2 + (i - 2)])
            wt = [(S_T, MK[("gu", e, j, i)])]
            if e * NT + j >= 1:
                pe, pj = ptile(e, j, 1)
                wt.append((S_V, MK[("hb", pe, pj, i)]))
            emit("s", lambda s, i=i, hgb=hgb: s.activation(
                sx_sb[:, i, :], hgb[:, 0:GT], AF.Silu), wait=wt)
            MK[("silu", e, j, i)] = S_S.n
            emit("v", lambda v, i=i, hub=hub: v.tensor_tensor(
                sx_sb[:, i, :], sx_sb[:, i, :], hub[:, 0:GT], OP.mult),
                 wait=[(S_S, MK[("silu", e, j, i)])])
            MK[("h1", e, j, i)] = S_V.n

    def emit_evac_hb(e, j):
        hp = j % 2
        t0 = GTO[j] * GT
        for i in range(IC):
            wh = []
            if e * NT + j >= 2:
                pe, pj = ptile(e, j, 2)
                wh.append((S_T, MK[("wdmm", pe, pj, 1)]))
            if e == 0:
                emit("v", lambda v, i=i, hp=hp: v.tensor_copy(h_sb[hp][:, i, :], sx_sb[:, i, :]),
                     wait=wh)
            else:
                emit("v", lambda v, i=i, hp=hp, t0=t0: v.tensor_tensor(
                    h_sb[hp][:, i, :], sx_sb[:, i, :], wbc[:, t0:t0 + GT], OP.mult),
                     wait=wh)
            MK[("hb", e, j, i)] = S_V.n

    def emit_drain(e, j, sub):
        t0 = GTO[j] * GT + sub * TT
        bs = ABANK if sub == 0 else BBANK
        for c in range(DC):
            srcp = bs[c // 2][:, (c % 2) * TT:(c % 2) * TT + TT]
            emit("v", lambda v, c=c, t0=t0, srcp=srcp: v.tensor_tensor(
                strm[:, c, t0:t0 + TT], strm[:, c, t0:t0 + TT], srcp, OP.add),
                 wait=[(S_T, MK[("wdmm", e, j, sub, c | 1)])])
        MK[("drain", e, j, sub)] = S_V.n

    def emit_wbc(e):
        ei = e - 1
        wt = [(S_V, MK[("drain", e - 1, NT - 1, 1)]), (S_V, MK["router_all"])]
        for r in range(4):
            t0 = r * LNT
            emit("t", lambda t, ei=ei, t0=t0, r=r: t.matmul(
                pd[r][:, 0:512], oh16_b[0:16, ei * 128:ei * 128 + 128], w_hi[0:16, t0:t0 + LNT],
                start=True, stop=False), wait=wt if r == 0 else ())
            emit("t", lambda t, ei=ei, t0=t0, r=r: t.matmul(
                pd[r][:, 0:512], oh16_b[0:16, ei * 128:ei * 128 + 128], w_lo[0:16, t0:t0 + LNT],
                start=False, stop=True))
            MK[("wbcmm", e, r)] = S_T.n
        for r in range(4):
            t0 = r * LNT
            emit("v", lambda v, t0=t0, r=r: v.tensor_copy(wbc[:, t0:t0 + LNT], pd[r][:, 0:512]),
                 wait=[(S_T, MK[("wbcmm", e, r)])])
        MK[("wbcev", e)] = S_V.n

    def hook_a():
        emit_top2(1, pd0, pd1, wt_extra=[(S_V, S_V.n)])
        stream_ln(3, True, ("xln", 3))
        emit_xlo(3)
        emit_rlog(3, pd1)
        extra_wd_wait.append((S_V, MK[("wlo", 7)]))
        extra_wd_wait.append((S_V, MK[(("xln", 3), "done")]))

    def hook_xln0_inner():
        for c in range(DC):
            emit("v", lambda v, c=c: v.tensor_tensor(
                strm[:, c, 0:16], strm[:, c, 0:16], o16[:, c, :], OP.add))
        MK["ownadd"] = S_V.n
        stream_ln(0, True, ("xln", 0))
        extra_wd_wait.append((S_V, MK[(("xln", 0), "done")]))

    def hook_b():
        emit_top2(2, pd0, pd1)
        hook_xln0_inner()
        emit_xlo(0)
        emit_rlog(0, pd0, wt_extra=[(S_V, MK[(("xln", 0), "done")])])
        extra_wd_wait.append((S_V, MK[("wlo", 11)]))
        extra_wd_wait.append((S_V, MK[("sig", 0)]))

    def hook_c():
        emit_top2(3, pd0, pd1)
        emit_top2(0, pd0, pd1)
        MK["router_all"] = S_V.n
        extra_wd_wait.append((S_V, MK["router_all"]))

    ndma = [0]

    def emit_fln(lt):
        stream_ln(lt, False, ("fln", lt))
        t0 = lt * LNT
        for c in range(DC):
            wt = [(S_V, MK[(("fln", lt), "done")])] if c < 2 else []
            if ndma[0] % 2 == 0:
                emit("y", lambda y, c=c, t0=t0: y.dma_start(
                    outT_h[c * 128:(c + 1) * 128, t0:t0 + LNT], strm[:, c, t0:t0 + LNT]), wait=wt)
            else:
                emit("s", lambda s, c=c, t0=t0: s.dma_start(
                    outT_h[c * 128:(c + 1) * 128, t0:t0 + LNT], strm[:, c, t0:t0 + LNT]), wait=wt, sem=S_C)
            ndma[0] += 1

    def hook_fln(lt):
        def h():
            emit_fln(lt)
            extra_wd_wait.append((S_V, MK[(("fln", lt), "done")]))
        return h

    hooks = {(0, 0): hook_a, (0, 1): hook_b, (0, 2): hook_c,
             (NEXP - 1, 0): hook_fln(1), (NEXP - 1, 1): hook_fln(2),
             (NEXP - 1, 2): hook_fln(3)}

    prev = None
    for e in range(NEXP):
        for j in range(NT):
            emit_gu(e, j)
            emit_evac_sh(e, j)
            if prev is not None:
                pe, pj = prev
                emit_wd(pe, pj, 0, cur=(e, j))
                emit_drain(pe, pj, 0)
                emit_wd(pe, pj, 1, cur=(e, j))
                emit_drain(pe, pj, 1)
                if pj == NT - 1:
                    MK[("emm", pe)] = S_T.n
                    if pe + 1 < NEXP:
                        emit_wbc(pe + 1)
                    if pe + 2 < NEXP:
                        emit_wload(pe + 2)
                if prev in hooks:
                    hooks[prev]()
            emit_evac_hb(e, j)
            prev = (e, j)
    fln_gate = list(extra_wd_wait)
    del extra_wd_wait[:]
    emit_wd(NEXP - 1, NT - 1, 0, cur=(NEXP - 1, NT - 1), extra=fln_gate)
    emit_drain(NEXP - 1, NT - 1, 0)
    emit_wd(NEXP - 1, NT - 1, 1, cur=(NEXP - 1, NT - 1))
    emit_drain(NEXP - 1, NT - 1, 1)
    MK[("emm", NEXP - 1)] = S_T.n
    emit_fln(0)

    if DEBUG:
        dbg_y_h = nc.dram_tensor("dbg_y", [128, DC, S], F32, kind="ExternalOutput").ap()
        dbg_xp_h = nc.dram_tensor("dbg_xp", [128, DC, S], BF16, kind="ExternalOutput").ap()
        emit("y", lambda y: y.dma_start(dbg_y_h, strm[:, :, :]),
             wait=[(S_V, MK[("drain", NEXP - 1, NT - 1, 1)])])
        emit("y", lambda y: y.dma_start(dbg_xp_h, xp_bf[:, :, :]))
        MK["dbgdma"] = S_Y.n
    emit("v", lambda v: None, wait=[(S_Y, S_Y.n), (S_C, S_C.n)], sem=None)

    with es, nc.Block() as block:
        @block.tensor
        def _(t):
            for f in P["t"]:
                f(t)

        @block.vector
        def _(v):
            for f in P["v"]:
                f(v)

        @block.scalar
        def _(s):
            for f in P["s"]:
                f(s)

        @block.sync
        def _(y):
            for f in P["y"]:
                f(y)

        @block.gpsimd
        def _(g):
            for f in P["g"]:
                f(g)

    return nc


def kernel(**inputs):
    gi = lambda k: np.ascontiguousarray(np.asarray(inputs[k], dtype=np.float32))
    x = gi("L0_attn_x")
    qw, kw, vw, ow = gi("L0_attn_qw"), gi("L0_attn_kw"), gi("L0_attn_vw"), gi("L0_attn_ow")
    qb, vb, ob = gi("L0_attn_qb"), gi("L0_attn_vb"), gi("L0_attn_ob")
    lna, lnb = gi("L0_attn_lna_w"), gi("L0_attn_lnb_w")
    lnc, lnd = gi("L0_attn_lnc_w"), gi("L0_attn_lnd_w")
    lqw, lqb = gi("L0_attn_lqw"), gi("L0_attn_lqb")
    lkw, lkb = gi("L0_attn_lkw"), gi("L0_attn_lkb")
    lvw, lvb = gi("L0_attn_lvw"), gi("L0_attn_lvb")
    gw, gb = gi("L0_ffn_gate_w"), gi("L0_ffn_gate_b")
    swg, swu, swd = gi("L0_ffn_shared_wg"), gi("L0_ffn_shared_wu"), gi("L0_ffn_shared_wd")
    ewg, ewu, ewd = gi("L0_ffn_experts_wg"), gi("L0_ffn_experts_wu"), gi("L0_ffn_experts_wd")

    c8 = lambda v: np.ascontiguousarray(v.reshape(DC, 128).T)
    t2 = lambda v: np.ascontiguousarray(np.tile(v, 2).reshape(128, 1))
    z64 = np.zeros((64, 64), np.float32)
    bd = lambda w: np.ascontiguousarray(np.block([[w.T, z64], [z64, w.T]]))
    mask = np.full((128, 128), NEG_BIG, np.float32)
    for b in range(8):
        mask[b * 16:(b + 1) * 16, b * 16:(b + 1) * 16] = 0.0
    oh16 = np.zeros((16, 16 * 128), np.float32)
    for e in range(16):
        oh16[e, e * 128:(e + 1) * 128] = 1.0
    hm2s = np.zeros((128, 2), np.float32)
    hm2s[0:64, 0] = 1.0 / 64
    hm2s[64:128, 1] = 1.0 / 64
    hm2T = np.zeros((2, 128), np.float32)
    hm2T[0, 0:64] = 1.0
    hm2T[1, 64:128] = 1.0
    gw128 = np.ascontiguousarray(gw.reshape(DC, 128, 16).transpose(1, 0, 2).reshape(128, 128))

    common = {
        "x16T": np.ascontiguousarray(x[:, :16, :].reshape(128, D).T),
        "qwT": np.ascontiguousarray(qw.T), "kwT": np.ascontiguousarray(kw.T),
        "vwT": np.ascontiguousarray(vw.T), "owT": np.ascontiguousarray(ow.T),
        "lqbd": bd(lqw), "lkbd": bd(lkw), "lvbd": bd(lvw),
        "qb8": c8(qb), "vb8": c8(vb), "ob8": c8(ob), "lna8": c8(lna), "lnb8": c8(lnb),
        "lqb1": t2(lqb), "lkb1": t2(lkb), "lvb1": t2(lvb), "lnc1": t2(lnc), "lnd1": t2(lnd),
        "gw128": gw128, "gb16": np.ascontiguousarray(gb.reshape(16, 1)),
        "idf": np.eye(128, dtype=np.float32), "mask": mask, "oh16": oh16,
        "inv1024r": np.full((1, 128), 1.0 / 1024, np.float32),
        "ones1r": np.ones((1, 128), np.float32),
        "onescol": np.ones((128, 1), np.float32),
        "hm2s": hm2s, "hm2T": hm2T,
        "swg": swg, "swu": swu, "swd": swd, "ewg": ewg, "ewu": ewu, "ewd": ewd,
    }
    in_maps = []
    for i in range(8):
        sel = np.zeros((128, 16), np.float32)
        for j in range(16):
            sel[i * 16 + j, j] = 1.0
        m = dict(common)
        m["xT"] = np.ascontiguousarray(x[i].T)
        m["selT"] = sel
        in_maps.append(m)

    nc = build_nc()
    res = run_bass_kernel_spmd(nc, in_maps, core_ids=list(range(8)))
    out = np.empty((B, S, D), np.float32)
    for i in range(8):
        out[i] = np.asarray(res.results[i]["outT"]).T
    return out


if __name__ == "__main__":
    import reference as _r
    ins = _r.setup_inputs(0)
    got = kernel(**{k: np.asarray(v) for k, v in ins.items()})
    want = np.asarray(_r.reference(**ins))
    num = float(((got - want) ** 2).mean())
    den = float((want ** 2).mean())
    print("resid_var_ratio:", num / den)


# version 30
# speedup vs baseline: 1.1720x; 1.0009x over previous
import math
import numpy as np
import ml_dtypes
import concourse.bass as bass
import concourse.mybir as mybir
from concourse.bass_utils import run_bass_kernel_spmd
from contextlib import ExitStack

F32 = mybir.dt.float32
BF16 = mybir.dt.bfloat16
AX = mybir.AxisListType.X
OP = mybir.AluOpType
AF = mybir.ActivationFunctionType

B, S, D, H, HD, E, II = 8, 2048, 1024, 16, 64, 16, 512
DC, IC = D // 128, II // 128
TT = 256
GT = 512
NT = S // GT
GTO = [1, 2, 3, 0]
LNT = 512
NLT = S // LNT
EPS = 1e-5
LN8 = math.log(0.125)
TH_C = 16777216.0 * 0.01 / 0.9
NEG_BIG = -1.0e30
TILE_ORDER = [2, 3, 4, 5, 6, 7, 0, 1]
LN_EARLY = [1, 2, 3]
NEXP = 17
DEBUG = False


def build_nc():
    nc = bass.Bass("TRN2", target_bir_lowering=False, detect_race_conditions=False)

    def hin(name, shape):
        return nc.dram_tensor(name, list(shape), F32, kind="ExternalInput").ap()

    xT_h = hin("xT", (D, S))
    selT_h = hin("selT", (128, 16))
    x16T_h = hin("x16T", (D, 128))
    qwT_h = hin("qwT", (D, D)); kwT_h = hin("kwT", (D, D))
    vwT_h = hin("vwT", (D, D)); owT_h = hin("owT", (D, D))
    lqbd_h = hin("lqbd", (128, 128)); lkbd_h = hin("lkbd", (128, 128)); lvbd_h = hin("lvbd", (128, 128))
    qb8_h = hin("qb8", (128, DC)); vb8_h = hin("vb8", (128, DC)); ob8_h = hin("ob8", (128, DC))
    lna8_h = hin("lna8", (128, DC)); lnb8_h = hin("lnb8", (128, DC))
    lqb1_h = hin("lqb1", (128, 1)); lkb1_h = hin("lkb1", (128, 1)); lvb1_h = hin("lvb1", (128, 1))
    lnc1_h = hin("lnc1", (128, 1)); lnd1_h = hin("lnd1", (128, 1))
    gw128_h = hin("gw128", (128, 128))
    gb16_h = hin("gb16", (16, 1))
    idf_h = hin("idf", (128, 128))
    mask_h = hin("mask", (128, 128))
    oh16_h = hin("oh16", (16, 16 * 128))
    inv1024r_h = hin("inv1024r", (1, 128))
    ones1r_h = hin("ones1r", (1, 128))
    onescol_h = hin("onescol", (128, 1))
    hm2s_h = hin("hm2s", (128, 2))
    hm2T_h = hin("hm2T", (2, 128))
    swg_h = hin("swg", (D, II)); swu_h = hin("swu", (D, II)); swd_h = hin("swd", (II, D))
    ewg_h = hin("ewg", (E, D, II)); ewu_h = hin("ewu", (E, D, II)); ewd_h = hin("ewd", (E, II, D))
    outT_h = nc.dram_tensor("outT", [D, S], F32, kind="ExternalOutput").ap()
    outT_v = outT_h.rearrange("(c p) t -> p c t", p=128)

    P = {"t": [], "v": [], "s": [], "y": [], "g": []}

    class Sem:
        def __init__(self, h, step):
            self.h, self.step, self.n = h, step, 0

    es = ExitStack()
    S_T = Sem(es.enter_context(nc.semaphore("s_t")), 1)
    S_V = Sem(es.enter_context(nc.semaphore("s_v")), 1)
    S_S = Sem(es.enter_context(nc.semaphore("s_s")), 1)
    S_Y = Sem(es.enter_context(nc.semaphore("s_y")), 16)
    S_G = Sem(es.enter_context(nc.semaphore("s_g")), 16)
    S_C = Sem(es.enter_context(nc.semaphore("s_c")), 16)
    QSEM = {"t": S_T, "v": S_V, "s": S_S, "y": S_Y, "g": S_G}

    def emit(q, fn, wait=(), sem="auto"):
        s = QSEM[q] if sem == "auto" else sem
        w = tuple((sm.h, v) for (sm, v) in wait)
        def run(eng, fn=fn, w=w, sh=None if s is None else s.h,
                amt=0 if s is None else s.step):
            for h, v in w:
                eng.wait_ge(h, v)
            r = fn(eng)
            if sh is not None:
                r.then_inc(sh, amt)
        P[q].append(run)
        if s is not None:
            s.n += s.step

    MK = {}

    def vfill():
        emit("v", lambda v: v.tensor_copy(fillb[:, 1:2], fillb[:, 0:1]))

    sb = lambda name, shape, dt=F32: es.enter_context(nc.sbuf_tensor(name, list(shape), dt))
    psb = lambda name: es.enter_context(nc.psum_tensor(name, [128, 512], F32))

    strm = sb("strm", (128, DC, S))
    wg_sb = [sb(f"wgsb{i}", (128, DC, II), BF16) for i in range(2)]
    wu_sb = [sb(f"wusb{i}", (128, DC, II), BF16) for i in range(2)]
    wd_sb = [sb(f"wdsb{i}", (128, IC, D), BF16) for i in range(2)]
    h_sb = [sb(f"hsb{i}", (128, IC, GT), BF16) for i in range(2)]
    sx_sb = sb("sxsb", (128, IC, GT))
    wbc = sb("wbc", (128, S), BF16)
    w_hi = sb("w_hi", (16, S), BF16)
    w_lo = sb("w_lo", (16, S), BF16)
    o16 = sb("o16", (128, DC, 16))
    idf_s = sb("idf_s", (128, 128))
    idb_s = sb("idb_s", (128, 128), BF16)
    mask_s = sb("mask_s", (128, 128))
    oh16_b = sb("oh16_b", (16, 16 * 128), BF16)
    inv1024_b = sb("inv1024_b", (1, 128), BF16)
    ones1_b = sb("ones1_b", (1, 128), BF16)
    onescol_b = sb("onescol_b", (128, 1), BF16)
    hm2s_b = sb("hm2s_b", (128, 2), BF16)
    hm2T_b = sb("hm2T_b", (2, 128), BF16)
    selT_b = sb("selT_b", (128, 16), BF16)
    qb8_s = sb("qb8_s", (128, DC)); vb8_s = sb("vb8_s", (128, DC)); ob8_s = sb("ob8_s", (128, DC))
    lna8_s = sb("lna8_s", (128, DC)); lnb8_s = sb("lnb8_s", (128, DC))
    lqb1_s = sb("lqb1_s", (128, 1)); lkb1_s = sb("lkb1_s", (128, 1)); lvb1_s = sb("lvb1_s", (128, 1))
    lnc1_s = sb("lnc1_s", (128, 1)); lnd1_s = sb("lnd1_s", (128, 1))
    gwhi_s = sb("gwhi_s", (128, 128), BF16)
    gwlo_s = sb("gwlo_s", (128, 128), BF16)
    gb_s = sb("gb_s", (16, 1))
    cpool = sb("cpool", (128, 4))
    fillb = sb("fillb", (128, 2))
    nc.const_aps.register(F32, 0.0, cpool[:, 0:1])
    nc.const_aps.register(F32, EPS, cpool[:, 1:2])
    nc.const_aps.register(F32, LN8, cpool[:, 2:3])

    pg0, pg1, pu0, pu1 = psb("pg0"), psb("pg1"), psb("pu0"), psb("pu1")
    pd0, pd1, pd2, pd3 = psb("pd0"), psb("pd1"), psb("pd2"), psb("pd3")
    pg = [pg0, pg1]; pu = [pu0, pu1]; pd = [pd0, pd1, pd2, pd3]

    SA = ExitStack()
    sa = lambda name, shape, dt=F32: SA.enter_context(nc.sbuf_tensor(name, list(shape), dt))
    awT = sa("awT", (128, DC, D), BF16)
    x16b = sa("x16b", (128, DC, 128), BF16)
    sq16 = sa("sq16", (128, DC, 128), BF16)
    xab = sa("xab", (128, DC, 128), BF16)
    xbb = sa("xbb", (128, DC, 128), BF16)
    qf = sa("qf", (128, DC, 128))
    qcb = sa("qcb", (128, DC, 128), BF16)
    kvb = sa("kvb", (128, DC, 128), BF16)
    ascr = sa("ascr", (128, DC, 128))
    asb = sa("asb", (128, DC, 128), BF16)
    knb = sa("knb", (128, DC, 128), BF16)
    vit = sa("vit", (128, DC, 128), BF16)
    it0 = sa("it0", (128, DC, 128), BF16)
    it1 = sa("it1", (128, DC, 128), BF16)
    lbd_b = sa("lbd_b", (128, 3, 128), BF16)
    gwf_s = sa("gwf_s", (128, 128))
    pf = [sa(f"pf{i}", (128, 128)) for i in range(4)]
    pf2 = [sa(f"pf2{i}", (128, 128)) for i in range(4)]
    ptb = [sa(f"ptb{i}", (128, 128), BF16) for i in range(4)]
    ptb2 = [sa(f"ptb2{i}", (128, 128), BF16) for i in range(4)]
    lmaxn = [sa(f"lmaxn{i}", (128, 1)) for i in range(4)]
    rsum = [sa(f"rsum{i}", (128, 1)) for i in range(4)]
    rrec = [sa(f"rrec{i}", (128, 1)) for i in range(4)]
    ptob = [sa(f"ptob{i}", (128, 16), BF16) for i in range(2)]
    aob = sa("aob", (128, DC, 16), BF16)
    rowsA = sa("rowsA", (2, 1024))
    rowsB = sa("rowsB", (2, 1024), BF16)
    colscr = sa("colscr", (128, DC))
    dcol = sa("dcol", (128, 1))
    dcolb = sa("dcolb", (128, 1), BF16)
    m01s = sa("m01s", (1, 4))
    m01b = sa("m01b", (1, 1), BF16)
    m01c = sa("m01c", (128, 1))

    emit("g", lambda g: g.dma_start(idb_s[:, :], idf_h[:, :]))
    MK["g_idb"] = S_G.n
    for w in range(32):
        emit("t", lambda t: t.matmul(pg0[:, 0:128], idb_s[:, :], idb_s[:, :], start=True, stop=True),
             wait=[(S_G, MK["g_idb"])] if w == 0 else ())
    for dst, src in [(oh16_b[:, :], oh16_h[:, :]),
                     (inv1024_b[:, :], inv1024r_h[:, :]), (ones1_b[:, :], ones1r_h[:, :]),
                     (onescol_b[:, :], onescol_h[:, :]), (hm2s_b[:, :], hm2s_h[:, :]),
                     (hm2T_b[:, :], hm2T_h[:, :]), (selT_b[:, :], selT_h[:, :]),
                     (gwhi_s[:, :], gw128_h[:, :]),
                     (lbd_b[:, 0, :], lqbd_h[:, :]), (lbd_b[:, 1, :], lkbd_h[:, :]),
                     (lbd_b[:, 2, :], lvbd_h[:, :]),
                     (x16b[:, :, :], x16T_h.rearrange("(c p) t -> p c t", p=128))]:
        emit("g", lambda g, d=dst, s=src: g.dma_start(d, s))
    MK["g_consts"] = S_G.n
    emit("g", lambda g: g.dma_start(awT[:, :, 0:512], qwT_h[:, 0:512].rearrange("(c p) i -> p c i", p=128)))
    MK["g_awqh"] = S_G.n
    emit("g", lambda g: g.dma_start(awT[:, :, 512:1024], qwT_h[:, 512:1024].rearrange("(c p) i -> p c i", p=128)))
    MK["g_awq"] = S_G.n

    def emit_wload(e):
        par = e % 2
        if e == 0:
            srcs = [(wg_sb[0], swg_h), (wu_sb[0], swu_h), (wd_sb[0], swd_h)]
        else:
            srcs = [(wg_sb[par], ewg_h[e - 1]), (wu_sb[par], ewu_h[e - 1]),
                    (wd_sb[par], ewd_h[e - 1])]
        wt = [(S_T, MK[("emm", e - 2)])] if e >= 2 else []
        for k, (dst, src) in enumerate(srcs):
            emit("g", lambda g, d=dst, s=src: g.dma_start(
                d[:, :, :], s.rearrange("(c p) i -> p c i", p=128)),
                 wait=wt if k == 0 else ())
        MK[("wload", e)] = S_G.n

    for dst, src in [(idf_s[:, :], idf_h[:, :]), (mask_s[:, :], mask_h[:, :]),
                     (qb8_s[:, :], qb8_h[:, :]), (vb8_s[:, :], vb8_h[:, :]),
                     (ob8_s[:, :], ob8_h[:, :]), (lna8_s[:, :], lna8_h[:, :]),
                     (lnb8_s[:, :], lnb8_h[:, :]), (lqb1_s[:, :], lqb1_h[:, :]),
                     (lkb1_s[:, :], lkb1_h[:, :]), (lvb1_s[:, :], lvb1_h[:, :]),
                     (lnc1_s[:, :], lnc1_h[:, :]), (lnd1_s[:, :], lnd1_h[:, :]),
                     (gwf_s[:, :], gw128_h[:, :]), (gb_s[:, :], gb16_h[:, :])]:
        emit("y", lambda y, d=dst, s=src: y.dma_start(d, s))
    MK["y_consts"] = S_Y.n
    emit("y", lambda y: y.dma_start(strm[:, :, :], xT_h.rearrange("(c p) t -> p c t", p=128)))
    MK["y_xT"] = S_Y.n

    for j, val in [(0, 0.0), (1, EPS), (2, LN8), (3, 0.0)]:
        emit("v", lambda v, j=j, val=val: v.memset(cpool[:, j:j + 1], val))
    emit("v", lambda v: v.memset(fillb[:, 0:1], 0.0))
    emit("v", lambda v: v.tensor_tensor(gwlo_s[:, :], gwf_s[:, :], gwhi_s[:, :], OP.subtract),
         wait=[(S_Y, MK["y_consts"]), (S_G, MK["g_consts"])])
    MK["v_init"] = S_V.n

    def head_ln(src_f, dst_bf, wcol, use_ln8, tag):
        scap = [(S_S, S_S.n)]
        for c in range(DC):
            emit("v", lambda v, c=c: v.tensor_copy(asb[:, c, :], src_f[:, c, :]),
                 wait=scap if c == 0 else ())
        MK[(tag, "bf")] = S_V.n
        for c in range(DC):
            emit("s", lambda s, c=c: s.activation(sq16[:, c, :], src_f[:, c, :], AF.Square),
                 wait=[(S_V, MK[(tag, "bf")])] if c == 0 else ())
        MK[(tag, "sq")] = S_S.n
        for c in range(DC):
            dstp = pg[c // 4]
            emit("t", lambda t, c=c, dstp=dstp: t.matmul(
                dstp[0:2, (c % 4) * 128:(c % 4) * 128 + 128], hm2s_b[:, :], asb[:, c, :],
                start=True, stop=True),
                 wait=[(S_V, MK[(tag, "bf")])] if c == 0 else ())
        for c in range(DC):
            dstp = pd[2 + c // 4]
            emit("t", lambda t, c=c, dstp=dstp: t.matmul(
                dstp[0:2, (c % 4) * 128:(c % 4) * 128 + 128], hm2s_b[:, :], sq16[:, c, :],
                start=True, stop=True),
                 wait=[(S_S, MK[(tag, "sq")])] if c == 0 else ())
        MK[(tag, "stats")] = S_T.n
        for half in range(2):
            pm, pq = pg[half], pd[2 + half]
            wv = [(S_T, MK[(tag, "stats")])] if half == 0 else [(S_S, MK[(tag, "rstd", 0)])]
            emit("v", lambda v, pm=pm: v.tensor_copy(rowsA[0:2, 0:512], pm[0:2, 0:512]), wait=wv)
            emit("v", lambda v: v.tensor_tensor(
                rowsA[0:2, 512:1024], rowsA[0:2, 0:512], rowsA[0:2, 0:512], OP.mult))
            emit("v", lambda v, pq=pq: v.tensor_tensor(
                rowsA[0:2, 0:512], pq[0:2, 0:512], rowsA[0:2, 512:1024], OP.subtract))
            wm = [(S_T, MK[(tag, "bc", 0)])] if half == 1 else []
            emit("v", lambda v, pm=pm: v.tensor_copy(rowsB[0:2, 512:1024], pm[0:2, 0:512]),
                 wait=wm)
            MK[(tag, "var", half)] = S_V.n
            emit("s", lambda s: s.activation(rowsA[0:2, 512:1024], rowsA[0:2, 0:512],
                                             AF.Ln, bias=EPS, scale=1.0),
                 wait=[(S_V, MK[(tag, "var", half)])])
            bias = LN8 if use_ln8 else 0.0
            emit("s", lambda s, bias=bias: s.activation(
                rowsB[0:2, 0:512], rowsA[0:2, 512:1024], AF.Exp, bias=bias, scale=-0.5))
            MK[(tag, "rstd", half)] = S_S.n
            bm, br = (pu0, pu1) if half == 0 else (pd0, pd1)
            wb = [(S_S, MK[(tag, "rstd", half)]), (S_V, S_V.n)]
            for cc in range(4):
                emit("t", lambda t, cc=cc, bm=bm: t.matmul(
                    bm[:, cc * 128:cc * 128 + 128], hm2T_b[:, :],
                    rowsB[0:2, 512 + cc * 128:512 + cc * 128 + 128],
                    start=True, stop=True),
                     wait=wb if cc == 0 else ())
                emit("t", lambda t, cc=cc, br=br: t.matmul(
                    br[:, cc * 128:cc * 128 + 128], hm2T_b[:, :],
                    rowsB[0:2, cc * 128:cc * 128 + 128],
                    start=True, stop=True))
            MK[(tag, "bc", half)] = S_T.n
            for cc in range(4):
                c = half * 4 + cc
                emit("v", lambda v, c=c, cc=cc, bm=bm: v.tensor_tensor(
                    src_f[:, c, :], src_f[:, c, :], bm[:, cc * 128:cc * 128 + 128], OP.subtract),
                     wait=[(S_T, MK[(tag, "bc", half)])] if cc == 0 else ())
                emit("v", lambda v, c=c, cc=cc, br=br: v.scalar_tensor_tensor(
                    dst_bf[:, c, :], src_f[:, c, :], wcol[:, 0:1],
                    br[:, cc * 128:cc * 128 + 128], OP.mult, OP.mult))
            MK[(tag, "apply", half)] = S_V.n
        MK[(tag, "done")] = S_V.n

    for c in range(DC):
        emit("s", lambda s, c=c: s.activation(sq16[:, c, :], x16b[:, c, :], AF.Square),
             wait=[(S_G, MK["g_consts"])] if c == 0 else ())
    MK["x16sq"] = S_S.n
    for c in range(DC):
        emit("t", lambda t, c=c: t.matmul(pg0[0:1, 0:128], onescol_b[:, :], x16b[:, c, :],
                                          start=(c == 0), stop=(c == DC - 1)),
             wait=[(S_G, MK["g_consts"])] if c == 0 else ())
    for c in range(DC):
        emit("t", lambda t, c=c: t.matmul(pg0[32:33, 0:128], onescol_b[:, :], sq16[:, c, :],
                                          start=(c == 0), stop=(c == DC - 1)),
             wait=[(S_S, MK["x16sq"])] if c == 0 else ())
    MK["x16stats"] = S_T.n
    emit("v", lambda v: v.tensor_copy(rowsB[0:1, 0:128], pg0[0:1, 0:128]),
         wait=[(S_T, MK["x16stats"]), (S_V, MK["v_init"])])
    emit("v", lambda v: v.tensor_scalar(rowsA[0:1, 0:128], pg0[0:1, 0:128], 1.0 / D, None, OP.mult))
    emit("v", lambda v: v.tensor_tensor(rowsA[0:1, 128:256], rowsA[0:1, 0:128], rowsA[0:1, 0:128], OP.mult))
    emit("v", lambda v: v.tensor_scalar(rowsA[0:1, 256:384], pg0[32:33, 0:128], 1.0 / D, None, OP.mult))
    emit("v", lambda v: v.tensor_tensor(rowsA[0:1, 256:384], rowsA[0:1, 256:384], rowsA[0:1, 128:256], OP.subtract))
    MK["x16var"] = S_V.n
    emit("s", lambda s: s.activation(rowsA[0:1, 384:512], rowsA[0:1, 256:384], AF.Ln, bias=EPS, scale=1.0),
         wait=[(S_V, MK["x16var"])])
    emit("s", lambda s: s.activation(rowsB[0:1, 128:256], rowsA[0:1, 384:512], AF.Exp, bias=0.0, scale=-0.5))
    MK["x16rstd"] = S_S.n
    emit("t", lambda t: t.matmul(pg1[:, 0:128], inv1024_b[:, :], rowsB[0:1, 0:128], start=True, stop=True),
         wait=[(S_S, MK["x16rstd"])])
    emit("t", lambda t: t.matmul(pg1[:, 128:256], ones1_b[:, :], rowsB[0:1, 128:256], start=True, stop=True))
    MK["x16bc"] = S_T.n
    for c in range(DC):
        emit("v", lambda v, c=c: v.tensor_tensor(ascr[:, c, :], x16b[:, c, :], pg1[:, 0:128], OP.subtract),
             wait=[(S_T, MK["x16bc"])] if c == 0 else ())
        emit("v", lambda v, c=c: v.scalar_tensor_tensor(
            xab[:, c, :], ascr[:, c, :], lna8_s[:, c:c + 1], pg1[:, 128:256], OP.mult, OP.mult))
        emit("v", lambda v, c=c: v.scalar_tensor_tensor(
            xbb[:, c, :], ascr[:, c, :], lnb8_s[:, c:c + 1], pg1[:, 128:256], OP.mult, OP.mult))
    MK["xa"] = S_V.n
    MK["xb"] = S_V.n

    def proj1024(dstbank, tag, wt_first, wt_half=(), rhs=0):
        xr = xab if rhs == 0 else xbb
        for i in range(DC):
            for c in range(DC):
                bank, off = dstbank(i)
                wt = ()
                if i == 0 and c == 0:
                    wt = wt_first
                elif i == 4 and c == 0:
                    wt = wt_half
                emit("t", lambda t, i=i, c=c, bank=bank, off=off, xr=xr: t.matmul(
                    bank[:, off:off + 128], awT[:, c, i * 128:i * 128 + 128], xr[:, c, :],
                    start=(c == 0), stop=(c == DC - 1)),
                     wait=wt)
            MK[(tag, i)] = S_T.n
        MK[tag] = S_T.n

    proj1024(lambda i: (pg[i // 4], (i % 4) * 128), "qmm",
             [(S_G, MK["g_awqh"]), (S_V, MK["xa"])], [(S_G, MK["g_awq"])])
    emit("g", lambda g: g.dma_start(awT[:, :, 0:512], kwT_h[:, 0:512].rearrange("(c p) i -> p c i", p=128)),
         wait=[(S_T, MK["qmm"])])
    MK["g_awkh"] = S_G.n
    emit("g", lambda g: g.dma_start(awT[:, :, 512:1024], kwT_h[:, 512:1024].rearrange("(c p) i -> p c i", p=128)))
    MK["g_awk"] = S_G.n
    for i in range(DC):
        emit("s", lambda s, i=i: s.activation(
            qf[:, i, :], pg[i // 4][:, (i % 4) * 128:(i % 4) * 128 + 128], AF.Identity,
            bias=qb8_s[:, i:i + 1], scale=1.0),
             wait=[(S_T, MK[("qmm", (i // 4) * 4 + 3)])] if i % 4 == 0 else ())
        emit("s", lambda s, i=i: s.activation(qcb[:, i, :], qf[:, i, :], AF.Identity))
    MK["qevac"] = S_S.n
    proj1024(lambda i: (pu[i // 4], (i % 4) * 128), "kmm",
             [(S_G, MK["g_awkh"]), (S_S, MK["qevac"])], [(S_G, MK["g_awk"])], rhs=1)
    emit("g", lambda g: g.dma_start(awT[:, :, 0:512], vwT_h[:, 0:512].rearrange("(c p) i -> p c i", p=128)),
         wait=[(S_T, MK["kmm"])])
    MK["g_awvh"] = S_G.n
    emit("g", lambda g: g.dma_start(awT[:, :, 512:1024], vwT_h[:, 512:1024].rearrange("(c p) i -> p c i", p=128)))
    MK["g_awv"] = S_G.n
    for i in range(DC):
        emit("s", lambda s, i=i: s.activation(
            kvb[:, i, :], pu[i // 4][:, (i % 4) * 128:(i % 4) * 128 + 128], AF.Identity),
             wait=[(S_T, MK[("kmm", (i // 4) * 4 + 3)])] if i % 4 == 0 else ())
    MK["kevac"] = S_S.n
    for i in range(DC):
        emit("t", lambda t, i=i: t.matmul(
            pg[i // 4][:, (i % 4) * 128:(i % 4) * 128 + 128], lbd_b[:, 1, :], kvb[:, i, :],
            start=True, stop=True),
             wait=[(S_S, MK["kevac"])] if i == 0 else ())
    MK["kpre"] = S_T.n
    for i in range(DC):
        emit("s", lambda s, i=i: s.activation(
            ascr[:, i, :], pg[i // 4][:, (i % 4) * 128:(i % 4) * 128 + 128], AF.Identity,
            bias=lkb1_s[:, 0:1], scale=1.0),
             wait=[(S_T, MK["kpre"]), (S_V, S_V.n)] if i == 0 else ())
    MK["kpre_e"] = S_S.n
    head_ln(ascr, knb, lnd1_s, False, "klnh")

    proj1024(lambda i: (pd[2 + i // 4], (i % 4) * 128), "vmm",
             [(S_G, MK["g_awvh"]), (S_T, MK["kpre"]), (S_V, S_V.n)], [(S_G, MK["g_awv"])], rhs=1)
    emit("g", lambda g: g.dma_start(awT[:, :, 0:512], owT_h[:, 0:512].rearrange("(c p) i -> p c i", p=128)),
         wait=[(S_T, MK["vmm"])])
    MK["g_awoh"] = S_G.n
    emit("g", lambda g: g.dma_start(awT[:, :, 512:1024], owT_h[:, 512:1024].rearrange("(c p) i -> p c i", p=128)))
    MK["g_awo"] = S_G.n
    emit_wload(0)
    emit_wload(1)
    for i in range(DC):
        emit("s", lambda s, i=i: s.activation(
            kvb[:, i, :], pd[2 + i // 4][:, (i % 4) * 128:(i % 4) * 128 + 128], AF.Identity,
            bias=vb8_s[:, i:i + 1], scale=1.0),
             wait=[(S_T, MK[("vmm", (i // 4) * 4 + 3)])] if i % 4 == 0 else ())
    MK["vevac"] = S_S.n
    for i in range(DC):
        emit("t", lambda t, i=i: t.matmul(
            pg[i // 4][:, (i % 4) * 128:(i % 4) * 128 + 128], lbd_b[:, 2, :], kvb[:, i, :],
            start=True, stop=True),
             wait=[(S_S, MK["vevac"])] if i == 0 else ())
    MK["vpre"] = S_T.n
    for i in range(DC):
        emit("s", lambda s, i=i: s.activation(
            ascr[:, i, :], pg[i // 4][:, (i % 4) * 128:(i % 4) * 128 + 128], AF.Identity,
            bias=lvb1_s[:, 0:1], scale=1.0),
             wait=[(S_T, MK["vpre"]), (S_V, S_V.n)] if i == 0 else ())
    MK["vi_f"] = S_S.n
    for i in range(DC):
        emit("t", lambda t, i=i: t.transpose(
            pd[2 + i // 4][:, (i % 4) * 128:(i % 4) * 128 + 128], ascr[:, i, :], idf_s[:, :]),
             wait=[(S_S, MK["vi_f"]), (S_Y, MK["y_consts"])] if i == 0 else ())
    MK["vi_t"] = S_T.n
    for i in range(DC):
        emit("v", lambda v, i=i: v.tensor_copy(
            vit[:, i, :], pd[2 + i // 4][:, (i % 4) * 128:(i % 4) * 128 + 128]),
             wait=[(S_T, MK["vi_t"])] if i == 0 else ())
    MK["vit"] = S_V.n

    def attn_step(step):
        tg = ("st", step)
        for i in range(DC):
            wq = ([(S_S, MK[("qin", 0)])] if step == 0 else [(S_V, MK[("qin", step)])])
            emit("t", lambda t, i=i: t.matmul(
                pg[i // 4][:, (i % 4) * 128:(i % 4) * 128 + 128], lbd_b[:, 0, :], qcb[:, i, :],
                start=True, stop=True),
                 wait=wq if i == 0 else ())
        MK[(tg, "qpre")] = S_T.n
        for i in range(DC):
            emit("s", lambda s, i=i: s.activation(
                ascr[:, i, :], pg[i // 4][:, (i % 4) * 128:(i % 4) * 128 + 128], AF.Identity,
                bias=lqb1_s[:, 0:1], scale=1.0),
                 wait=[(S_T, MK[(tg, "qpre")]), (S_V, S_V.n)] if i == 0 else ())
        MK[(tg, "qpre_e")] = S_S.n
        head_ln(ascr, asb, lnc1_s, True, (tg, "qln"))
        for h in range(H):
            ic, half = h // 2, h % 2
            hp = h % 4
            sl = slice(half * 64, half * 64 + 64)
            lg = pd2[:, (h % 4) * 128:(h % 4) * 128 + 128]
            wt_l = [(S_V, MK[((tg, "qln"), "done")])] if h == 0 else []
            if h >= 4:
                if step < 2:
                    wt_l = wt_l + [(S_V, MK[(tg, "pevac", h - 4)])]
                else:
                    wt_l = wt_l + [(S_V, MK[(tg, "pn", h - 4)])]
            emit("t", lambda t, ic=ic, sl=sl, lg=lg: t.matmul(
                lg, asb[sl, ic, :], knb[sl, ic, :], start=True, stop=True), wait=wt_l)
            MK[(tg, "lg", h)] = S_T.n
            wt_m = [(S_T, MK[(tg, "lg", h)])]
            if h >= 4:
                wt_m.append((S_S, MK[(tg, "exp", h - 4)]))
            emit("v", lambda v, lg=lg, hp=hp: v.reduce_max(lmaxn[hp][:, 0:1], lg, axis=AX, negate=True),
                 wait=wt_m)
            emit("v", lambda v, lg=lg, hp=hp: v.tensor_tensor(pf[hp][:, :], lg, mask_s[:, :], OP.add))
            MK[(tg, "pf", h)] = S_V.n
            wt_e = [(S_V, MK[(tg, "pf", h)])]
            if h >= 4:
                wt_e.append((S_V, MK[(tg, "pn", h - 4)]))
            emit("s", lambda s, hp=hp: s.activation(pf2[hp][:, :], pf[hp][:, :], AF.Exp,
                                                    bias=lmaxn[hp][:, 0:1],
                                                    scale=1.0, accum_out=rsum[hp][:, 0:1]),
                 wait=wt_e)
            MK[(tg, "exp", h)] = S_S.n
            emit("v", lambda v, hp=hp: v.reciprocal(rrec[hp][:, 0:1], rsum[hp][:, 0:1]),
                 wait=[(S_S, MK[(tg, "exp", h)])])
            vfill()
            wt_c = [(S_T, MK[(tg, "pt" if step < 2 else "pto", h - 4)])] if h >= 4 else []
            emit("v", lambda v, hp=hp: v.tensor_scalar(ptb[hp][:, :], pf2[hp][:, :], rrec[hp][:, 0:1], None, OP.mult),
                 wait=wt_c)
            MK[(tg, "pn", h)] = S_V.n
            if step < 2:
                ptp = pd3[:, (h % 4) * 128:(h % 4) * 128 + 128]
                emit("t", lambda t, ptp=ptp, hp=hp: t.matmul(ptp, ptb[hp][:, :], idb_s[:, :], start=True, stop=True),
                     wait=[(S_V, MK[(tg, "pn", h)])])
                MK[(tg, "pt", h)] = S_T.n
                wt_v = [(S_T, MK[(tg, "pt", h)])]
                if h >= 4:
                    wt_v.append((S_T, MK[(tg, "at", h - 4)]))
                emit("v", lambda v, ptp=ptp, hp=hp: v.tensor_copy(ptb2[hp][:, :], ptp),
                     wait=wt_v)
                MK[(tg, "pevac", h)] = S_V.n
                emit("t", lambda t, sl=sl, ic=ic, hp=hp: t.matmul(
                    pg[ic // 4][sl, (ic % 4) * 128:(ic % 4) * 128 + 128],
                    vit[:, ic, sl], ptb2[hp][:, :], start=True, stop=True),
                     wait=[(S_V, MK[(tg, "pevac", h)])])
                MK[(tg, "at", h)] = S_T.n
            else:
                emit("t", lambda t, h=h, hp=hp: t.matmul(pd3[:, h * 16:h * 16 + 16], ptb[hp][:, :],
                                                  selT_b[:, :], start=True, stop=True),
                     wait=[(S_V, MK[(tg, "pn", h)])])
                MK[(tg, "pto", h)] = S_T.n
                pob = ptob[h % 2]
                wt_p = [(S_T, MK[(tg, "pto", h)])]
                if h >= 2:
                    wt_p.append((S_T, MK[(tg, "aow", h - 2)]))
                emit("v", lambda v, pob=pob, h=h: v.tensor_copy(pob[:, :], pd3[:, h * 16:h * 16 + 16]),
                     wait=wt_p)
                MK[(tg, "ptob", h)] = S_V.n
                emit("t", lambda t, sl=sl, ic=ic, pob=pob: t.matmul(
                    pg0[sl, ic * 16:ic * 16 + 16], vit[:, ic, sl], pob[:, :], start=True, stop=True),
                     wait=[(S_V, MK[(tg, "ptob", h)])])
                MK[(tg, "aow", h)] = S_T.n
                if half == 1:
                    emit("v", lambda v, ic=ic: v.tensor_copy(aob[:, ic, :], pg0[:, ic * 16:ic * 16 + 16]),
                         wait=[(S_T, MK[(tg, "aow", h)])])
                    MK[(tg, "aoev", ic)] = S_V.n
        if step < 2:
            dst = it0 if step == 0 else it1
            for i in range(DC):
                emit("v", lambda v, i=i, dst=dst: v.tensor_copy(
                    dst[:, i, :], pg[i // 4][:, (i % 4) * 128:(i % 4) * 128 + 128]),
                     wait=[(S_T, MK[(tg, "at", 2 * i + 1)])])
            MK[(tg, "itev")] = S_V.n

    MK[("qin", 0)] = MK["qevac"]
    attn_step(0)
    for i in range(DC):
        emit("v", lambda v, i=i: v.tensor_tensor(qf[:, i, :], qf[:, i, :], it0[:, i, :], OP.add))
        emit("v", lambda v, i=i: v.tensor_copy(qcb[:, i, :], qf[:, i, :]))
    MK[("qin", 1)] = S_V.n
    attn_step(1)
    for i in range(DC):
        emit("v", lambda v, i=i: v.tensor_tensor(ascr[:, i, :], it1[:, i, :], it0[:, i, :], OP.subtract))
        emit("v", lambda v, i=i: v.reduce_sum(colscr[:, i:i + 1], ascr[:, i, :], axis=AX,
                                              apply_absolute_value=True))
    vfill()
    emit("v", lambda v: v.reduce_sum(dcol[:, 0:1], colscr[:, :], axis=AX))
    vfill()
    emit("v", lambda v: v.tensor_copy(dcolb[:, 0:1], dcol[:, 0:1]))
    MK["dcol"] = S_V.n
    emit("t", lambda t: t.matmul(pd3[0:1, 0:1], onescol_b[:, :], dcolb[:, 0:1], start=True, stop=True),
         wait=[(S_V, MK["dcol"])])
    MK["dsum"] = S_T.n
    emit("v", lambda v: v.tensor_scalar(m01s[0:1, 0:1], pd3[0:1, 0:1], TH_C, None, OP.is_ge),
         wait=[(S_T, MK["dsum"])])
    vfill()
    emit("v", lambda v: v.tensor_copy(m01b[0:1, 0:1], m01s[0:1, 0:1]))
    MK["m01b"] = S_V.n
    emit("t", lambda t: t.matmul(pd3[:, 4:5], ones1_b[:, :], m01b[0:1, 0:1], start=True, stop=True),
         wait=[(S_V, MK["m01b"])])
    MK["m01bc"] = S_T.n
    emit("v", lambda v: v.tensor_copy(m01c[:, 0:1], pd3[:, 4:5]),
         wait=[(S_T, MK["m01bc"])])
    vfill()
    for i in range(DC):
        emit("v", lambda v, i=i: v.scalar_tensor_tensor(
            it1[:, i, :], it1[:, i, :], m01c[:, 0:1], qf[:, i, :], OP.mult, OP.add))
        emit("v", lambda v, i=i: v.tensor_copy(qcb[:, i, :], it1[:, i, :]))
    MK[("qin", 2)] = S_V.n
    attn_step(2)

    for i in range(DC):
        for c in range(DC):
            wt = []
            if i == 0 and c == 0:
                wt = [(S_G, MK["g_awoh"]), (S_V, MK[(("st", 2), "aoev", DC - 1)])]
            elif i == 4 and c == 0:
                wt = [(S_G, MK["g_awo"])]
            emit("t", lambda t, i=i, c=c: t.matmul(
                pd2[:, i * 16:i * 16 + 16], awT[:, c, i * 128:i * 128 + 128], aob[:, c, :],
                start=(c == 0), stop=(c == DC - 1)), wait=wt)
    MK["oproj"] = S_T.n
    for i in range(DC):
        emit("v", lambda v, i=i: v.tensor_copy(o16[:, i, :], pd2[:, i * 16:i * 16 + 16]),
             wait=[(S_T, MK["oproj"])] if i == 0 else ())
    MK["o16ev"] = S_V.n

    SA.close()
    sp = lambda name, shape, dt=F32: es.enter_context(nc.sbuf_tensor(name, list(shape), dt))
    xp_bf = sp("xp_bf", (128, DC, S), BF16)
    sq_bf = sp("sq_bf", (128, DC, LNT), BF16)
    rl_s2 = [sp(f"rl_s{i}", (16, LNT)) for i in range(2)]
    rowsL = sp("rowsL", (1, 1536))
    rowsLb = sp("rowsLb", (1, 1024), BF16)
    rlt_s = sp("rlt_s", (128, 16))
    rls_t = sp("rls_t", (128, 16))
    xlt = sp("xlt", (128, LNT))
    eqs = sp("eqs", (128, 16))
    mx1 = sp("mx1", (128, 1))
    mx2 = sp("mx2", (128, 1))
    rc_ = sp("rc_", (128, 1))

    def stream_ln(tile, with_xp, tag):
        t0 = tile * LNT
        tcap = [(S_T, S_T.n), (S_S, MK["obadd"])]
        for c in range(DC):
            emit("v", lambda v, c=c, t0=t0: v.tensor_copy(xp_bf[:, c, t0:t0 + LNT], strm[:, c, t0:t0 + LNT]),
                 wait=tcap if c == 0 else ())
        MK[(tag, "cast")] = S_V.n
        for c in range(DC):
            emit("s", lambda s, c=c, t0=t0: s.activation(sq_bf[:, c, :], strm[:, c, t0:t0 + LNT], AF.Square),
                 wait=(tcap + [(S_V, MK[(tag, "cast")])]) if c == 0 else ())
        MK[(tag, "sq")] = S_S.n
        for c in range(DC):
            emit("t", lambda t, c=c, t0=t0: t.matmul(pd0[0:1, 0:512], onescol_b[:, :], xp_bf[:, c, t0:t0 + LNT],
                                              start=(c == 0), stop=(c == DC - 1)),
                 wait=[(S_V, MK[(tag, "cast")])] if c == 0 else ())
        for c in range(DC):
            emit("t", lambda t, c=c: t.matmul(pd0[32:33, 0:512], onescol_b[:, :], sq_bf[:, c, :],
                                              start=(c == 0), stop=(c == DC - 1)),
                 wait=[(S_S, MK[(tag, "sq")])] if c == 0 else ())
        MK[(tag, "stats")] = S_T.n
        emit("v", lambda v: v.tensor_copy(rowsLb[0:1, 0:512], pd0[0:1, 0:512]),
             wait=[(S_T, MK[(tag, "stats")])])
        emit("v", lambda v: v.tensor_scalar(rowsL[0:1, 0:512], pd0[0:1, 0:512], 1.0 / D, None, OP.mult))
        emit("v", lambda v: v.tensor_tensor(rowsL[0:1, 512:1024], rowsL[0:1, 0:512], rowsL[0:1, 0:512], OP.mult))
        emit("v", lambda v: v.tensor_scalar(rowsL[0:1, 1024:1536], pd0[32:33, 0:512], 1.0 / D, None, OP.mult))
        emit("v", lambda v: v.tensor_tensor(rowsL[0:1, 1024:1536], rowsL[0:1, 1024:1536],
                                            rowsL[0:1, 512:1024], OP.subtract))
        MK[(tag, "var")] = S_V.n
        emit("s", lambda s: s.activation(rowsL[0:1, 512:1024], rowsL[0:1, 1024:1536], AF.Ln, bias=EPS, scale=1.0),
             wait=[(S_V, MK[(tag, "var")])])
        emit("s", lambda s: s.activation(rowsLb[0:1, 512:1024], rowsL[0:1, 512:1024], AF.Exp, bias=0.0, scale=-0.5))
        MK[(tag, "rstd")] = S_S.n
        emit("t", lambda t: t.matmul(pd1[:, 0:512], inv1024_b[:, :], rowsLb[0:1, 0:512], start=True, stop=True),
             wait=[(S_S, MK[(tag, "rstd")])])
        MK[(tag, "bcm")] = S_T.n
        for c in range(DC):
            emit("v", lambda v, c=c, t0=t0: v.tensor_tensor(
                strm[:, c, t0:t0 + LNT], strm[:, c, t0:t0 + LNT], pd1[:, 0:512], OP.subtract),
                 wait=[(S_T, MK[(tag, "bcm")])] if c == 0 else ())
        MK[(tag, "sub")] = S_V.n
        emit("t", lambda t: t.matmul(pd1[:, 0:512], ones1_b[:, :], rowsLb[0:1, 512:1024], start=True, stop=True),
             wait=[(S_V, MK[(tag, "sub")])])
        MK[(tag, "bcr")] = S_T.n
        if with_xp:
            for c in range(DC):
                emit("v", lambda v, c=c, t0=t0: v.tensor_tensor(
                    xp_bf[:, c, t0:t0 + LNT], strm[:, c, t0:t0 + LNT], pd1[:, 0:512], OP.mult),
                     wait=[(S_T, MK[(tag, "bcr")])] if c == 0 else ())
            MK[(tag, "xpw")] = S_V.n
            emit("v", lambda v: v.tensor_scalar(rowsLb[0:1, 0:512], rowsLb[0:1, 512:1024],
                                                2.0, None, OP.mult),
                 wait=[(S_T, MK[(tag, "bcm")])])
            MK[(tag, "r2")] = S_V.n
            emit("t", lambda t: t.matmul(pd1[:, 0:512], ones1_b[:, :], rowsLb[0:1, 0:512],
                                         start=True, stop=True),
                 wait=[(S_V, MK[(tag, "r2")]), (S_V, MK[(tag, "xpw")])]),
            MK.__setitem__((tag, "bcr2"), S_T.n)
            for c in range(DC):
                emit("v", lambda v, c=c, t0=t0: v.tensor_tensor(
                    strm[:, c, t0:t0 + LNT], strm[:, c, t0:t0 + LNT], pd1[:, 0:512], OP.mult),
                     wait=[(S_T, MK[(tag, "bcr2")])] if c == 0 else ())
        else:
            for c in range(DC):
                emit("v", lambda v, c=c, t0=t0: v.tensor_tensor(
                    strm[:, c, t0:t0 + LNT], strm[:, c, t0:t0 + LNT], pd1[:, 0:512], OP.mult),
                     wait=[(S_T, MK[(tag, "bcr")])] if c == 0 else ())
        MK[(tag, "done")] = S_V.n

    def emit_xlo(lt, wt_extra=()):
        t0 = lt * LNT
        wt = [(S_V, MK[(("xln", lt), "done")]), (S_T, S_T.n)] + list(wt_extra)
        for c in range(DC):
            emit("v", lambda v, c=c, t0=t0: v.tensor_scalar(
                xlt[:, :], strm[:, c, t0:t0 + LNT], 0.5, None, OP.mult),
                 wait=wt if c == 0 else ())
            emit("v", lambda v, c=c, t0=t0: v.tensor_tensor(
                sq_bf[:, c, :], xlt[:, :], xp_bf[:, c, t0:t0 + LNT], OP.subtract))
        MK[("xlo", lt)] = S_V.n

    def emit_rlog(lt, plg, wt_extra=()):
        t0 = lt * LNT
        for c in range(DC):
            wt = ([(S_V, MK[("xlo", lt)]), (S_V, S_V.n)] + list(wt_extra)) if c == 0 else []
            emit("t", lambda t, c=c, t0=t0, plg=plg: t.matmul(
                plg[0:16, 0:512], gwhi_s[:, c * 16:c * 16 + 16], xp_bf[:, c, t0:t0 + LNT],
                start=(c == 0), stop=False), wait=wt)
            emit("t", lambda t, c=c, t0=t0, plg=plg: t.matmul(
                plg[0:16, 0:512], gwlo_s[:, c * 16:c * 16 + 16], xp_bf[:, c, t0:t0 + LNT],
                start=False, stop=False))
            emit("t", lambda t, c=c, t0=t0, plg=plg: t.matmul(
                plg[0:16, 0:512], gwhi_s[:, c * 16:c * 16 + 16], sq_bf[:, c, :],
                start=False, stop=(c == DC - 1)))
        MK[("rlog", lt)] = S_T.n
        rls = rl_s2[lt % 2]
        emit("v", lambda v, plg=plg, rls=rls: v.tensor_scalar(
            rls[0:16, 0:512], plg[0:16, 0:512], gb_s[0:16, 0:1], None, OP.add),
             wait=[(S_T, MK[("rlog", lt)]), (S_T, S_T.n)])
        MK[("sig", lt)] = S_V.n

    top2_prev = {}

    def emit_top2(lt, tp_bank, tr_bank, wt_extra=()):
        for kk in range(4):
            k = lt * 4 + kk
            wt = [(S_V, MK[("sig", lt)])] + (list(wt_extra) if kk == 0 else [])
            if kk in top2_prev:
                wt.append((S_V, MK[("wlo", top2_prev[kk])]))
            top2_prev[kk] = k
            emit("t", lambda t, kk=kk, tp_bank=tp_bank, rls=rl_s2[lt % 2]: t.transpose(
                tp_bank[:, kk * 16:kk * 16 + 16], rls[0:16, kk * 128:kk * 128 + 128],
                idf_s[0:16, 0:16]), wait=wt)
            MK[("rlt", k)] = S_T.n
            emit("s", lambda s, kk=kk, tp_bank=tp_bank: s.activation(
                rls_t[:, :], tp_bank[:, kk * 16:kk * 16 + 16], AF.Sigmoid),
                 wait=[(S_T, MK[("rlt", k)])])
            MK[("sigt", k)] = S_S.n
            emit("v", lambda v, kk=kk, tp_bank=tp_bank: v.tensor_copy(
                rlt_s[:, :], tp_bank[:, kk * 16:kk * 16 + 16]),
                 wait=[(S_T, MK[("rlt", k)]), (S_S, MK[("sigt", k)])])
            vfill()
            emit("v", lambda v: v.reduce_max(mx1[:, 0:1], rlt_s[:, :], axis=AX))
            vfill()
            emit("v", lambda v: v.tensor_scalar(eqs[:, :], rlt_s[:, :], mx1[:, 0:1], None, OP.is_ge))
            vfill()
            emit("v", lambda v: v.tensor_scalar(eqs[:, :], eqs[:, :], NEG_BIG, None, OP.mult))
            vfill()
            emit("v", lambda v: v.tensor_tensor(eqs[:, :], eqs[:, :], rlt_s[:, :], OP.add))
            vfill()
            emit("v", lambda v: v.reduce_max(mx2[:, 0:1], eqs[:, :], axis=AX))
            vfill()
            emit("v", lambda v: v.tensor_scalar(eqs[:, :], rlt_s[:, :], mx2[:, 0:1], None, OP.is_ge))
            vfill()
            emit("v", lambda v: v.tensor_tensor(rlt_s[:, :], rls_t[:, :], eqs[:, :], OP.mult))
            vfill()
            emit("v", lambda v: v.reduce_sum(mx1[:, 0:1], rlt_s[:, :], axis=AX))
            vfill()
            emit("v", lambda v: v.reciprocal(rc_[:, 0:1], mx1[:, 0:1]))
            vfill()
            emit("v", lambda v: v.tensor_scalar(rlt_s[:, :], rlt_s[:, :], rc_[:, 0:1], None, OP.mult))
            MK[("wtok", k)] = S_V.n
            emit("t", lambda t, kk=kk, tr_bank=tr_bank: t.transpose(
                tr_bank[0:16, kk * 128:kk * 128 + 128], rlt_s[:, :], idf_s[:, :]),
                 wait=[(S_V, MK[("wtok", k)])])
            MK[("wtr", k)] = S_T.n
            emit("v", lambda v, k=k, kk=kk, tr_bank=tr_bank: v.tensor_copy(
                w_hi[0:16, k * 128:k * 128 + 128], tr_bank[0:16, kk * 128:kk * 128 + 128]),
                 wait=[(S_T, MK[("wtr", k)])])
            emit("v", lambda v, k=k, kk=kk, tr_bank=tr_bank: v.tensor_tensor(
                w_lo[0:16, k * 128:k * 128 + 128], tr_bank[0:16, kk * 128:kk * 128 + 128],
                w_hi[0:16, k * 128:k * 128 + 128], OP.subtract))
            MK[("wlo", k)] = S_V.n

    for c in range(DC):
        emit("s", lambda s, c=c: s.activation(
            strm[:, c, :], strm[:, c, :], AF.Identity, bias=ob8_s[:, c:c + 1], scale=1.0),
             wait=[(S_Y, MK["y_xT"]), (S_Y, MK["y_consts"])] if c == 0 else ())
    MK["obadd"] = S_S.n
    stream_ln(1, True, ("xln", 1))
    emit_xlo(1)
    emit_rlog(1, pd1)
    stream_ln(2, True, ("xln", 2))
    emit_xlo(2)
    emit_rlog(2, pd1)

    ABANK = [pg0, pg1, pu0, pu1]
    BBANK = [pd0, pd1, pd2, pd3]

    def ptile(e, j, k):
        idx = e * NT + j - k
        return idx // NT, idx % NT

    def emit_gu(e, j):
        par = e % 2
        t0 = GTO[j] * GT
        lt = GTO[j]
        for i in range(IC):
            hgb = (ABANK[i] if i < 2 else BBANK[i - 2])
            hub = (ABANK[2 + i] if i < 2 else BBANK[2 + (i - 2)])
            wt = []
            if i == 0:
                if j == 0:
                    wt.append((S_G, MK[("wload", e)]))
                if e == 0:
                    wt.append((S_V, MK[(("xln", lt), "done")]))
                if e * NT + j >= 1:
                    p1e, p1j = ptile(e, j, 1)
                    wt.append((S_S, MK[("silu", p1e, p1j, 1)]))
                    wt.append((S_V, MK[("h1", p1e, p1j, 1)]))
                if e * NT + j >= 2:
                    pe, pj = ptile(e, j, 2)
                    wt.append((S_V, MK[("drain", pe, pj, 0)]))
            if i == 2:
                if e == 0 and j <= 1:
                    wt.append((S_V, S_V.n))
                if e * NT + j >= 1:
                    p1e, p1j = ptile(e, j, 1)
                    wt.append((S_S, MK[("silu", p1e, p1j, 3)]))
                    wt.append((S_V, MK[("h1", p1e, p1j, 3)]))
                if e * NT + j >= 2:
                    pe, pj = ptile(e, j, 2)
                    wt.append((S_V, MK[("drain", pe, pj, 1)]))
                if j == 1 and e >= 1:
                    wt.append((S_V, MK[("wbcev", e)]))
                if extra_wd_wait:
                    wt.extend(extra_wd_wait)
                    del extra_wd_wait[:]
            for c in range(DC):
                emit("t", lambda t, i=i, c=c, par=par, t0=t0, hgb=hgb: t.matmul(
                    hgb[:, 0:GT], wg_sb[par][:, c, i * 128:i * 128 + 128], xp_bf[:, c, t0:t0 + GT],
                    start=(c == 0), stop=(c == DC - 1)),
                     wait=wt if c == 0 else ())
            for c in range(DC):
                emit("t", lambda t, i=i, c=c, par=par, t0=t0, hub=hub: t.matmul(
                    hub[:, 0:GT], wu_sb[par][:, c, i * 128:i * 128 + 128], xp_bf[:, c, t0:t0 + GT],
                    start=(c == 0), stop=(c == DC - 1)))
            MK[("gu", e, j, i)] = S_T.n

    extra_wd_wait_DUMMY = []
    extra_wd_wait = []

    def emit_wd(e, j, sub, cur, extra=()):
        par = e % 2
        hp = j % 2
        t0 = GTO[j] * GT + sub * TT
        bs = ABANK if sub == 0 else BBANK
        wt0 = [(S_V, MK[("hb", e, j, IC - 1)])] + list(extra)
        ce, cj = cur
        wt0.append((S_S, MK[("silu", ce, cj, 1 if sub == 0 else 3)]))
        wt0.append((S_V, MK[("h1", ce, cj, 1 if sub == 0 else 3)]))
        for c in range(DC):
            dstp = bs[c // 2][:, (c % 2) * TT:(c % 2) * TT + TT]
            for i in range(IC):
                emit("t", lambda t, i=i, c=c, par=par, hp=hp, sub=sub, dstp=dstp: t.matmul(
                    dstp, wd_sb[par][:, i, c * 128:c * 128 + 128],
                    h_sb[hp][:, i, sub * TT:sub * TT + TT],
                    start=(i == 0), stop=(i == IC - 1)),
                     wait=wt0 if (i == 0 and c == 0) else ())
            MK[("wdmm", e, j, sub, c)] = S_T.n
        MK[("wdmm", e, j, sub)] = S_T.n

    def emit_evac_sh(e, j):
        for i in range(IC):
            hgb = (ABANK[i] if i < 2 else BBANK[i - 2])
            hub = (ABANK[2 + i] if i < 2 else BBANK[2 + (i - 2)])
            wt = [(S_T, MK[("gu", e, j, i)])]
            if e * NT + j >= 1:
                pe, pj = ptile(e, j, 1)
                wt.append((S_V, MK[("hb", pe, pj, i)]))
            emit("s", lambda s, i=i, hgb=hgb: s.activation(
                sx_sb[:, i, :], hgb[:, 0:GT], AF.Silu), wait=wt)
            MK[("silu", e, j, i)] = S_S.n
            emit("v", lambda v, i=i, hub=hub: v.tensor_tensor(
                sx_sb[:, i, :], sx_sb[:, i, :], hub[:, 0:GT], OP.mult),
                 wait=[(S_S, MK[("silu", e, j, i)])])
            MK[("h1", e, j, i)] = S_V.n

    def emit_evac_hb(e, j):
        hp = j % 2
        t0 = GTO[j] * GT
        for i in range(IC):
            wh = []
            if e * NT + j >= 2:
                pe, pj = ptile(e, j, 2)
                wh.append((S_T, MK[("wdmm", pe, pj, 1)]))
            if e == 0:
                emit("v", lambda v, i=i, hp=hp: v.tensor_copy(h_sb[hp][:, i, :], sx_sb[:, i, :]),
                     wait=wh)
            else:
                emit("v", lambda v, i=i, hp=hp, t0=t0: v.tensor_tensor(
                    h_sb[hp][:, i, :], sx_sb[:, i, :], wbc[:, t0:t0 + GT], OP.mult),
                     wait=wh)
            MK[("hb", e, j, i)] = S_V.n

    def emit_drain(e, j, sub):
        t0 = GTO[j] * GT + sub * TT
        bs = ABANK if sub == 0 else BBANK
        for c in range(DC):
            srcp = bs[c // 2][:, (c % 2) * TT:(c % 2) * TT + TT]
            emit("v", lambda v, c=c, t0=t0, srcp=srcp: v.tensor_tensor(
                strm[:, c, t0:t0 + TT], strm[:, c, t0:t0 + TT], srcp, OP.add),
                 wait=[(S_T, MK[("wdmm", e, j, sub, c | 1)])])
        MK[("drain", e, j, sub)] = S_V.n

    def emit_wbc(e):
        ei = e - 1
        wt = [(S_V, MK[("drain", e - 1, NT - 1, 1)]), (S_V, MK["router_all"])]
        for r in range(4):
            t0 = r * LNT
            emit("t", lambda t, ei=ei, t0=t0, r=r: t.matmul(
                pd[r][:, 0:512], oh16_b[0:16, ei * 128:ei * 128 + 128], w_hi[0:16, t0:t0 + LNT],
                start=True, stop=False), wait=wt if r == 0 else ())
            emit("t", lambda t, ei=ei, t0=t0, r=r: t.matmul(
                pd[r][:, 0:512], oh16_b[0:16, ei * 128:ei * 128 + 128], w_lo[0:16, t0:t0 + LNT],
                start=False, stop=True))
            MK[("wbcmm", e, r)] = S_T.n
        for r in range(4):
            t0 = r * LNT
            emit("v", lambda v, t0=t0, r=r: v.tensor_copy(wbc[:, t0:t0 + LNT], pd[r][:, 0:512]),
                 wait=[(S_T, MK[("wbcmm", e, r)])])
        MK[("wbcev", e)] = S_V.n

    def hook_a():
        emit_top2(1, pd0, pd1, wt_extra=[(S_V, S_V.n)])
        stream_ln(3, True, ("xln", 3))
        emit_xlo(3)
        emit_rlog(3, pd1)
        extra_wd_wait.append((S_V, MK[("wlo", 7)]))
        extra_wd_wait.append((S_V, MK[(("xln", 3), "done")]))

    def hook_xln0_inner():
        for c in range(DC):
            emit("v", lambda v, c=c: v.tensor_tensor(
                strm[:, c, 0:16], strm[:, c, 0:16], o16[:, c, :], OP.add))
        MK["ownadd"] = S_V.n
        stream_ln(0, True, ("xln", 0))
        extra_wd_wait.append((S_V, MK[(("xln", 0), "done")]))

    def hook_b():
        emit_top2(2, pd0, pd1)
        hook_xln0_inner()
        emit_xlo(0)
        emit_rlog(0, pd0, wt_extra=[(S_V, MK[(("xln", 0), "done")])])
        extra_wd_wait.append((S_V, MK[("wlo", 11)]))
        extra_wd_wait.append((S_V, MK[("sig", 0)]))

    def hook_c():
        emit_top2(3, pd0, pd1)
        emit_top2(0, pd0, pd1)
        MK["router_all"] = S_V.n
        extra_wd_wait.append((S_V, MK["router_all"]))

    ndma = [0]

    def emit_fln(lt):
        stream_ln(lt, False, ("fln", lt))
        t0 = lt * LNT
        for c in range(DC):
            wt = [(S_V, MK[(("fln", lt), "done")])] if c < 2 else []
            if ndma[0] % 2 == 0:
                emit("y", lambda y, c=c, t0=t0: y.dma_start(
                    outT_h[c * 128:(c + 1) * 128, t0:t0 + LNT], strm[:, c, t0:t0 + LNT]), wait=wt)
            else:
                emit("s", lambda s, c=c, t0=t0: s.dma_start(
                    outT_h[c * 128:(c + 1) * 128, t0:t0 + LNT], strm[:, c, t0:t0 + LNT]), wait=wt, sem=S_C)
            ndma[0] += 1

    def hook_fln(lt):
        def h():
            emit_fln(lt)
            extra_wd_wait.append((S_V, MK[(("fln", lt), "done")]))
        return h

    hooks = {(0, 0): hook_a, (0, 1): hook_b, (0, 2): hook_c,
             (NEXP - 1, 0): hook_fln(1), (NEXP - 1, 1): hook_fln(2),
             (NEXP - 1, 2): hook_fln(3)}

    prev = None
    for e in range(NEXP):
        for j in range(NT):
            emit_gu(e, j)
            emit_evac_sh(e, j)
            if prev is not None:
                pe, pj = prev
                emit_wd(pe, pj, 0, cur=(e, j))
                emit_drain(pe, pj, 0)
                emit_wd(pe, pj, 1, cur=(e, j))
                emit_drain(pe, pj, 1)
                if pj == NT - 1:
                    MK[("emm", pe)] = S_T.n
                    if pe + 1 < NEXP:
                        emit_wbc(pe + 1)
                    if pe + 2 < NEXP:
                        emit_wload(pe + 2)
                if prev in hooks:
                    hooks[prev]()
            emit_evac_hb(e, j)
            prev = (e, j)
    fln_gate = list(extra_wd_wait)
    del extra_wd_wait[:]
    emit_wd(NEXP - 1, NT - 1, 0, cur=(NEXP - 1, NT - 1), extra=fln_gate)
    emit_drain(NEXP - 1, NT - 1, 0)
    emit_wd(NEXP - 1, NT - 1, 1, cur=(NEXP - 1, NT - 1))
    emit_drain(NEXP - 1, NT - 1, 1)
    MK[("emm", NEXP - 1)] = S_T.n
    emit_fln(0)

    if DEBUG:
        dbg_y_h = nc.dram_tensor("dbg_y", [128, DC, S], F32, kind="ExternalOutput").ap()
        dbg_xp_h = nc.dram_tensor("dbg_xp", [128, DC, S], BF16, kind="ExternalOutput").ap()
        emit("y", lambda y: y.dma_start(dbg_y_h, strm[:, :, :]),
             wait=[(S_V, MK[("drain", NEXP - 1, NT - 1, 1)])])
        emit("y", lambda y: y.dma_start(dbg_xp_h, xp_bf[:, :, :]))
        MK["dbgdma"] = S_Y.n
    emit("v", lambda v: None, wait=[(S_Y, S_Y.n), (S_C, S_C.n)], sem=None)

    with es, nc.Block() as block:
        @block.tensor
        def _(t):
            for f in P["t"]:
                f(t)

        @block.vector
        def _(v):
            for f in P["v"]:
                f(v)

        @block.scalar
        def _(s):
            for f in P["s"]:
                f(s)

        @block.sync
        def _(y):
            for f in P["y"]:
                f(y)

        @block.gpsimd
        def _(g):
            for f in P["g"]:
                f(g)

    return nc


def kernel(**inputs):
    gi = lambda k: np.ascontiguousarray(np.asarray(inputs[k], dtype=np.float32))
    x = gi("L0_attn_x")
    qw, kw, vw, ow = gi("L0_attn_qw"), gi("L0_attn_kw"), gi("L0_attn_vw"), gi("L0_attn_ow")
    qb, vb, ob = gi("L0_attn_qb"), gi("L0_attn_vb"), gi("L0_attn_ob")
    lna, lnb = gi("L0_attn_lna_w"), gi("L0_attn_lnb_w")
    lnc, lnd = gi("L0_attn_lnc_w"), gi("L0_attn_lnd_w")
    lqw, lqb = gi("L0_attn_lqw"), gi("L0_attn_lqb")
    lkw, lkb = gi("L0_attn_lkw"), gi("L0_attn_lkb")
    lvw, lvb = gi("L0_attn_lvw"), gi("L0_attn_lvb")
    gw, gb = gi("L0_ffn_gate_w"), gi("L0_ffn_gate_b")
    swg, swu, swd = gi("L0_ffn_shared_wg"), gi("L0_ffn_shared_wu"), gi("L0_ffn_shared_wd")
    ewg, ewu, ewd = gi("L0_ffn_experts_wg"), gi("L0_ffn_experts_wu"), gi("L0_ffn_experts_wd")

    c8 = lambda v: np.ascontiguousarray(v.reshape(DC, 128).T)
    t2 = lambda v: np.ascontiguousarray(np.tile(v, 2).reshape(128, 1))
    z64 = np.zeros((64, 64), np.float32)
    bd = lambda w: np.ascontiguousarray(np.block([[w.T, z64], [z64, w.T]]))
    mask = np.full((128, 128), NEG_BIG, np.float32)
    for b in range(8):
        mask[b * 16:(b + 1) * 16, b * 16:(b + 1) * 16] = 0.0
    oh16 = np.zeros((16, 16 * 128), np.float32)
    for e in range(16):
        oh16[e, e * 128:(e + 1) * 128] = 1.0
    hm2s = np.zeros((128, 2), np.float32)
    hm2s[0:64, 0] = 1.0 / 64
    hm2s[64:128, 1] = 1.0 / 64
    hm2T = np.zeros((2, 128), np.float32)
    hm2T[0, 0:64] = 1.0
    hm2T[1, 64:128] = 1.0
    gw128 = np.ascontiguousarray(gw.reshape(DC, 128, 16).transpose(1, 0, 2).reshape(128, 128))

    common = {
        "x16T": np.ascontiguousarray(x[:, :16, :].reshape(128, D).T),
        "qwT": np.ascontiguousarray(qw.T), "kwT": np.ascontiguousarray(kw.T),
        "vwT": np.ascontiguousarray(vw.T), "owT": np.ascontiguousarray(ow.T),
        "lqbd": bd(lqw), "lkbd": bd(lkw), "lvbd": bd(lvw),
        "qb8": c8(qb), "vb8": c8(vb), "ob8": c8(ob), "lna8": c8(lna), "lnb8": c8(lnb),
        "lqb1": t2(lqb), "lkb1": t2(lkb), "lvb1": t2(lvb), "lnc1": t2(lnc), "lnd1": t2(lnd),
        "gw128": gw128, "gb16": np.ascontiguousarray(gb.reshape(16, 1)),
        "idf": np.eye(128, dtype=np.float32), "mask": mask, "oh16": oh16,
        "inv1024r": np.full((1, 128), 1.0 / 1024, np.float32),
        "ones1r": np.ones((1, 128), np.float32),
        "onescol": np.ones((128, 1), np.float32),
        "hm2s": hm2s, "hm2T": hm2T,
        "swg": swg, "swu": swu, "swd": swd, "ewg": ewg, "ewu": ewu, "ewd": ewd,
    }
    in_maps = []
    for i in range(8):
        sel = np.zeros((128, 16), np.float32)
        for j in range(16):
            sel[i * 16 + j, j] = 1.0
        m = dict(common)
        m["xT"] = np.ascontiguousarray(x[i].T)
        m["selT"] = sel
        in_maps.append(m)

    nc = build_nc()
    res = run_bass_kernel_spmd(nc, in_maps, core_ids=list(range(8)))
    out = np.empty((B, S, D), np.float32)
    for i in range(8):
        out[i] = np.asarray(res.results[i]["outT"]).T
    return out


if __name__ == "__main__":
    import reference as _r
    ins = _r.setup_inputs(0)
    got = kernel(**{k: np.asarray(v) for k, v in ins.items()})
    want = np.asarray(_r.reference(**ins))
    num = float(((got - want) ** 2).mean())
    den = float((want ** 2).mean())
    print("resid_var_ratio:", num / den)
